# version 03
# baseline (speedup 1.0000x reference)
import numpy as np
import ml_dtypes
import concourse.bass as bass
import concourse.mybir as mybir
from concourse.bass_utils import run_bass_kernel_spmd
from contextlib import ExitStack

F32 = mybir.dt.float32
BF16 = mybir.dt.bfloat16
I32 = mybir.dt.int32
AF = mybir.ActivationFunctionType
ALU = mybir.AluOpType
AX = mybir.AxisListType
BFDT = ml_dtypes.bfloat16

D = 1024
S = 2048
NT = 16
NCORES = 8
NEG = -30000.0


class Plan:
    ENGS = ("pe", "act", "dve", "pool", "sp")
    SELF = {"dve", "act", "pool"}

    def __init__(self):
        self.streams = {e: [] for e in self.ENGS}
        self.cnt = {"pe": 0, "act": 0, "dve": 0, "pool": 0, "d0": 0, "d1": 0}
        self.mark = {}
        self.groups = {}

    def _wait(self, eng, sem, val):
        if val is None or val <= 0:
            return
        k = (eng, sem)
        if self.mark.get(k, 0) >= val:
            return
        self.mark[k] = val
        self.streams[eng].append(("w", sem, val))

    def _waits(self, eng, waits):
        for w in waits:
            if not w:
                continue
            if isinstance(w, dict):
                for s_, v in w.items():
                    self._wait(eng, s_, v)
            else:
                self._wait(eng, w[0], w[1])

    def op(self, eng, fn, waits=(), inc=False):
        self._waits(eng, waits)
        incs = []
        tok = {}
        if eng in self.SELF:
            prev = self.cnt[eng]
            self._wait(eng, eng, prev)
            self.cnt[eng] = prev + 1
            incs.append((eng, 1))
            tok[eng] = prev + 1
        if inc and eng == "pe":
            self.cnt["pe"] += 1
            incs.append(("pe", 1))
            tok["pe"] = self.cnt["pe"]
        self.streams[eng].append(("o", fn, incs))
        return tok

    def dma(self, eng, fn, waits=(), sem=None, group=None):
        sem = sem or ("d0" if eng == "sp" else "d1")
        gkey = (eng, sem)
        if group is None or self.groups.get(gkey) != (sem, group):
            self._wait(eng, sem, self.cnt[sem])
            self.groups[gkey] = (sem, group) if group is not None else None
        self._waits(eng, waits)
        self.cnt[sem] += 16
        self.streams[eng].append(("o", fn, [(sem, 16)]))
        return {sem: self.cnt[sem]}

    def tok(self):
        return dict(self.cnt)

    def fence(self):
        t = self.tok()
        for e in self.ENGS:
            self._waits(e, [t])
        return t

    def emit(self, block, sems):
        def runner(name):
            def run(engine):
                for it in self.streams[name]:
                    if it[0] == "w":
                        engine.wait_ge(sems[it[1]], it[2])
                    else:
                        inst = it[1](engine)
                        for s_, n in it[2]:
                            inst.then_inc(sems[s_], n)
            return run
        block.tensor(runner("pe"))
        block.scalar(runner("act"))
        block.vector(runner("dve"))
        block.gpsimd(runner("pool"))
        block.sync(runner("sp"))


def ap_dyn(apx, reg):
    return bass.AP(apx.tensor, reg, [list(d) for d in apx.ap])


def ap_make(apx, rel, dims):
    return bass.AP(apx.tensor, apx.offset + rel,
                   [list(d) if not isinstance(d, int) else d for d in dims])


def build_program(cfg):
    nexp = cfg.get("nexp", 64)
    ebs = cfg.get("ebs", 8)
    neb = nexp // ebs
    nh = cfg.get("nh", 16)
    nhp = nh // 2
    nb = cfg.get("nb", 2)
    do_ffn1 = cfg.get("ffn1", True)
    do_attn = cfg.get("attn", True)
    do_ffn3 = cfg.get("ffn3", True)
    taps = cfg.get("taps", ())

    nc = bass.Bass("TRN2", target_bir_lowering=False)

    x_h = nc.dram_tensor("x", [nb, S, D], F32, kind="ExternalInput").ap()
    k8_h = nc.dram_tensor("k8", [D, nexp * 128], F32, kind="ExternalInput").ap()
    val_h = nc.dram_tensor("val", [nexp, 128, D], F32, kind="ExternalInput").ap()
    selT_h = nc.dram_tensor("selT", [D, 64], F32, kind="ExternalInput").ap()
    wT_h = {m: nc.dram_tensor("w%sT" % m, [D, D], F32, kind="ExternalInput").ap()
            for m in "qkvo"}
    bias_h = nc.dram_tensor("biases", [1, 4 * D], F32, kind="ExternalInput").ap()
    srctgt_h = nc.dram_tensor("srctgt", [1, 6], I32, kind="ExternalInput").ap()
    idn_h = nc.dram_tensor("idn", [128, 128], BF16, kind="ExternalInput").ap()
    ones_h = nc.dram_tensor("onesrow", [1, 512], BF16, kind="ExternalInput").ap()
    qbvec_h = nc.dram_tensor("qbvec", [1, S], F32, kind="ExternalInput").ap()
    kxind_h = nc.dram_tensor("kxind", [3, 192], BF16, kind="ExternalInput").ap()
    oh64_h = nc.dram_tensor("oh64", [64, 64], BF16, kind="ExternalInput").ap()
    iotak_h = nc.dram_tensor("iotak", [64, 2], I32, kind="ExternalInput").ap()
    out_h = nc.dram_tensor("out", [nb, S, D], F32, kind="ExternalOutput").ap()
    mu_d = nc.dram_tensor("mu_d", [3, S], BF16, kind="Internal").ap()
    akd_h = nc.dram_tensor("akd", [64, S], BF16, kind="Internal").ap()
    avd_h = nc.dram_tensor("avd", [128, NT * 65], BF16, kind="Internal").ap()
    wbf_d = nc.dram_tensor("wbf", [4, D, D], BF16, kind="Internal").ap()
    kd_h = nc.dram_tensor("kd", [D, nexp * 128], BF16, kind="Internal").ap()
    vd_h = nc.dram_tensor("vd", [nexp, 128, D], BF16, kind="Internal").ap()
    tap_h = {}
    for tname in taps:
        shape, dt = {"yT1": ([128, NT * 1024], BF16), "gT": ([64, S], BF16),
                     "scT0": ([128, ebs * S], BF16),
                     "attnc0": ([128, S], BF16)}[tname]
        tap_h[tname] = nc.dram_tensor("tap_" + tname, shape, dt,
                                      kind="ExternalOutput").ap()

    P = Plan()
    st = {}

    with ExitStack() as es:
        E = es.enter_context
        xb = E(nc.sbuf_tensor("xb", [128, NT * 1024], F32)).ap()
        yT = E(nc.sbuf_tensor("yT", [128, NT * 1024], BF16)).ap()
        a1 = E(nc.sbuf_tensor("a1", [128, 18688], BF16)).ap()
        a2 = E(nc.sbuf_tensor("a2", [128, 16384], BF16)).ap()
        gT = E(nc.sbuf_tensor("gT", [64, S], BF16)).ap()
        selbuf = E(nc.sbuf_tensor("selbuf", [128, 8 * 64], BF16)).ap()
        selb = [E(nc.sbuf_tensor("sel%d" % i, [128, 64], F32)).ap()
                for i in range(2)]
        curb = [E(nc.sbuf_tensor("cur%d" % i, [128, 64], F32)).ap()
                for i in range(2)]
        geb = [E(nc.sbuf_tensor("ge%d" % i, [128, 64], F32)).ap()
               for i in range(2)]
        gateb = [E(nc.sbuf_tensor("gate%d" % i, [128, 64], BF16)).ap()
                 for i in range(2)]
        xmb = [E(nc.sbuf_tensor("xm%d" % i, [128, 1024], F32)).ap()
               for i in range(2)]
        ybf = [E(nc.sbuf_tensor("ybf%d" % i, [128, 1024], BF16)).ap()
               for i in range(2)]
        gbc = [E(nc.sbuf_tensor("gbc%d" % i, [128, 512], BF16)).ap()
               for i in range(2)]
        col1 = E(nc.sbuf_tensor("col1", [128, 8], F32)).ap()
        idn = E(nc.sbuf_tensor("idn_s", [128, 128], BF16)).ap()
        onesr = E(nc.sbuf_tensor("onesr", [1, 512], BF16)).ap()
        oh64 = E(nc.sbuf_tensor("oh64_s", [64, 64], BF16)).ap()
        biasb = E(nc.sbuf_tensor("biasb", [1, 4 * D], BF16)).ap()
        srctgt = E(nc.sbuf_tensor("srctgt_s", [1, 6], I32)).ap()
        scr = E(nc.sbuf_tensor("scr", [1, 64], F32)).ap()
        col0i = E(nc.sbuf_tensor("col0i", [1, 3], I32)).ap()
        coli2 = E(nc.sbuf_tensor("coli2", [1, 3], I32)).ap()
        scr3i = E(nc.sbuf_tensor("scr3i", [1, 3], I32)).ap()
        tgtbf = E(nc.sbuf_tensor("tgtbf", [1, 9], BF16)).ap()
        tgtf = E(nc.sbuf_tensor("tgtf", [1, 9], F32)).ap()
        iotak = E(nc.sbuf_tensor("iotak_s", [64, 2], I32)).ap()
        kxidx = E(nc.sbuf_tensor("kxidx", [64, 3], I32)).ap()
        avidx = E(nc.sbuf_tensor("avidx", [64, 3], I32)).ap()
        idxf = E(nc.sbuf_tensor("idxf", [64, 4], F32)).ap()
        kxis = E(nc.sbuf_tensor("kxis", [3, 192], BF16)).ap()
        rz = E(nc.sbuf_tensor("rz", [1, 512], BF16)).ap()
        lnz = E(nc.sbuf_tensor("lnz", [1, 512], F32)).ap()

        scT = a1[:, 0:ebs * S]
        qh = [a1[0:67, 0:2048], a1[0:67, 2048:4096]]
        kh = [a1[0:64, 4096:6144], a1[0:64, 6144:8192]]
        avT = a1[0:64, 8192:10240]
        vaug = [a1[:, 10240:11280], a1[:, 11280:12320]]
        vsh = [a1[0:64, 12320:13360], a1[0:64, 13360:14400]]
        attnc = a1[:, 14400:16448]
        eful = a1[:, 16448:16960]
        etail = a1[0:64, 16960:17984]
        rzb = a1[0:64, 17984:18496]
        keysb = a2[:, 0:8 * 1024]
        valsb = a2[:, 8 * 1024:8 * 1024 + ebs * 1024]
        wbuf = a2[:, 0:8 * 1024]
        qstage = a2[:, 8192:10240]
        kstage = a2[:, 10240:12288]
        vstage = a2[:, 12288:14336]
        wo_s = a2[:, 14336:15360]
        kxb = [a2[0:67, 15360:15552], a2[0:67, 15552:15744]]
        vxg = [a2[:, 15744:15874], a2[:, 15874:16004]]
        wtmp = [a2[:, 0:1024], a2[:, 1024:2048]]
        qbv = xb[0:1, 0:S]
        mrow = yT[0:1, 0:3 * S]

        psA = E(nc.psum_tensor("psA", [128, 512], F32)).ap()
        psB = E(nc.psum_tensor("psB", [128, 512], F32)).ap()
        psC = E(nc.psum_tensor("psC", [128, 512], F32)).ap()
        psD = E(nc.psum_tensor("psD", [128, 512], F32)).ap()
        psE = E(nc.psum_tensor("psE", [128, 512], F32)).ap()
        psV = E(nc.psum_tensor("psV", [128, 512], F32)).ap()
        psT = E(nc.psum_tensor("psT", [128, 1024], BF16)).ap()
        psG = E(nc.psum_tensor("psG", [128, 512], BF16)).ap()

        sems = {}
        for s_ in ("d0", "d1", "pe", "act", "dve", "pool"):
            sems[s_] = E(nc.semaphore(s_))

        EPSC = col1[:, 3:4]
        S1C = [col1[:, 0:1], col1[:, 4:5]]
        S2C = [col1[:, 1:2], col1[:, 5:6]]
        RSC = [col1[:, 2:3], col1[:, 6:7]]
        BROW = {m: biasb[0:1, i * D:(i + 1) * D] for i, m in enumerate("qkvo")}

        def xtile(t, lo=0, n=1024):
            return xb[:, t * 1024 + lo: t * 1024 + lo + n]

        def yT_chunk(t, dc):
            return yT[:, t * 1024 + dc * 128: t * 1024 + dc * 128 + 128]

        def yT_rhs(dc, tok0, ntok):
            return ap_make(yT, (tok0 // 128) * 1024 + dc * 128,
                           [yT.ap[0], [1024, ntok // 128], [1, 128]])

        def ohsel(e):
            return ap_make(oh64, e, [oh64.ap[0], [0, 128]])

        tok_idn = P.dma("sp", lambda e: e.dma_start(idn, idn_h))
        tok_ones = P.dma("sp", lambda e: e.dma_start(onesr, ones_h))
        tok_oh = P.dma("sp", lambda e: e.dma_start(oh64, oh64_h))
        tok_st = P.dma("sp", lambda e: e.dma_start(srctgt, srctgt_h))
        tok_qbv = P.dma("sp", lambda e: e.dma_start(qbv, qbvec_h))
        tok_kxi = P.dma("sp", lambda e: e.dma_start(kxis, kxind_h))
        P.dma("pool", lambda e: e.dma_start(biasb, bias_h))
        for dc in range(8):
            P.dma("pool", lambda e, dc=dc: e.dma_start(
                selbuf[:, dc * 64:(dc + 1) * 64],
                selT_h[dc * 128:(dc + 1) * 128, :]), group="setup")
        P.op("dve", lambda e: e.memset(EPSC, 1e-5))

        sf = scr[0:1, 0:3]
        tf = scr[0:1, 3:6]
        P.op("dve", lambda e: e.tensor_copy(sf, srctgt[0:1, 0:3]), waits=[tok_st])
        P.op("dve", lambda e: e.tensor_copy(tf, srctgt[0:1, 3:6]))
        P.op("dve", lambda e: e.tensor_sub(scr[0:1, 6:9], sf, tf))
        P.op("dve", lambda e: e.tensor_mul(
            scr[0:1, 6:9], scr[0:1, 6:9], scr[0:1, 6:9]))
        P.op("dve", lambda e: e.tensor_single_scalar(
            scr[0:1, 6:9], scr[0:1, 6:9], 2.25, ALU.is_gt))
        P.op("dve", lambda e: e.tensor_single_scalar(
            scr[0:1, 9:12], tf, 0.5, ALU.is_gt))
        P.op("dve", lambda e: e.tensor_single_scalar(
            scr[0:1, 12:15], tf, 30.5, ALU.is_lt))
        P.op("dve", lambda e: e.tensor_scalar(
            scr[0:1, 15:18], sf, 32.0, None, ALU.mult))
        P.op("dve", lambda e: e.tensor_add(scr[0:1, 15:18], scr[0:1, 15:18], tf))
        P.op("dve", lambda e: e.memset(scr[0:1, 18:21], 0.0))
        P.op("dve", lambda e: e.tensor_single_scalar(
            scr[0:1, 19:20], scr[0:1, 16:17], scr[0:1, 15:16], ALU.is_equal))
        P.op("dve", lambda e: e.tensor_single_scalar(
            scr[0:1, 20:21], scr[0:1, 17:18], scr[0:1, 15:16], ALU.is_equal))
        P.op("dve", lambda e: e.tensor_single_scalar(
            scr[0:1, 21:22], scr[0:1, 17:18], scr[0:1, 16:17], ALU.is_equal))
        P.op("dve", lambda e: e.tensor_max(
            scr[0:1, 20:21], scr[0:1, 20:21], scr[0:1, 21:22]))
        P.op("dve", lambda e: e.tensor_mul(
            scr[0:1, 22:25], scr[0:1, 6:9], scr[0:1, 9:12]))
        P.op("dve", lambda e: e.tensor_mul(
            scr[0:1, 22:25], scr[0:1, 22:25], scr[0:1, 12:15]))
        P.op("dve", lambda e: e.tensor_scalar(
            scr[0:1, 25:28], scr[0:1, 18:21], -1.0, 1.0, ALU.mult, ALU.add))
        P.op("dve", lambda e: e.tensor_mul(
            scr[0:1, 22:25], scr[0:1, 22:25], scr[0:1, 25:28]))
        P.op("dve", lambda e: e.tensor_scalar(
            scr[0:1, 28:31], scr[0:1, 22:25], -NEG, None, ALU.mult))
        tok_mu = None
        for t in range(3):
            P.op("dve", lambda e, t=t: e.tensor_single_scalar(
                mrow[0:1, t * S:(t + 1) * S], qbv, sf[0:1, t:t + 1],
                ALU.is_equal), waits=[tok_qbv])
            tok_mu = P.op("dve", lambda e, t=t: e.tensor_scalar(
                mrow[0:1, t * S:(t + 1) * S], mrow[0:1, t * S:(t + 1) * S],
                scr[0:1, 28 + t:29 + t], NEG, ALU.mult, ALU.add))
        tok_mud = P.dma("sp", lambda e: e.dma_start(
            ap_make(mu_d, 0, [[S, 3], [1, S]]),
            ap_make(mrow, 0, [mrow.ap[0], [1, 3 * S]])), waits=[tok_mu])
        tok_iok = P.dma("sp", lambda e: e.dma_start(iotak, iotak_h))
        P.op("dve", lambda e: e.tensor_single_scalar(
            col0i, srctgt[0:1, 3:6], 6, ALU.arith_shift_left), waits=[tok_st])
        P.op("dve", lambda e: e.tensor_single_scalar(
            coli2, srctgt[0:1, 3:6], 1, ALU.bitwise_and))
        P.op("dve", lambda e: e.tensor_single_scalar(
            scr3i, srctgt[0:1, 3:6], 1, ALU.arith_shift_right))
        P.op("dve", lambda e: e.tensor_copy(tgtf[0:1, 0:3], col0i))
        P.op("dve", lambda e: e.tensor_copy(tgtf[0:1, 3:6], coli2))
        P.op("dve", lambda e: e.tensor_copy(tgtf[0:1, 6:9], scr3i))
        tok_tbf = P.op("dve", lambda e: e.tensor_copy(tgtbf, tgtf))
        for t in range(3):
            P.op("pe", lambda e, t=t: e.matmul(
                psE[0:64, 0:3], onesr[0:1, 0:64],
                ap_make(tgtbf, t, [tgtbf.ap[0], [3, 3]]),
                start=True, stop=True),
                waits=[tok_tbf, tok_ones, st.get("idx_ps")], inc=True)
            tk = {"pe": P.cnt["pe"]}
            P.op("dve", lambda e: e.tensor_copy(idxf[:, 0:3], psE[0:64, 0:3]),
                 waits=[tk])
            P.op("dve", lambda e, t=t: e.tensor_copy(
                kxidx[:, t:t + 1], idxf[:, 0:1]))
            st["idx_ps"] = P.op("dve", lambda e, t=t: e.tensor_add(
                kxidx[:, t:t + 1], kxidx[:, t:t + 1], iotak[:, 0:1]),
                waits=[tok_iok])
            P.op("dve", lambda e: e.tensor_scalar(
                idxf[:, 1:2], idxf[:, 1:2], 66560.0, None, ALU.mult))
            P.op("dve", lambda e: e.tensor_scalar(
                idxf[:, 2:3], idxf[:, 2:3], 65.0, None, ALU.mult))
            P.op("dve", lambda e: e.tensor_add(
                idxf[:, 1:2], idxf[:, 1:2], idxf[:, 2:3]))
            P.op("dve", lambda e, t=t: e.tensor_copy(
                avidx[:, t:t + 1], idxf[:, 1:2]))
            st["idx_ps"] = P.op("dve", lambda e, t=t: e.tensor_add(
                avidx[:, t:t + 1], avidx[:, t:t + 1], iotak[:, 1:2]))
        jobs = []
        for mi, m in enumerate("qkvo"):
            for dc in range(8):
                jobs.append((wT_h[m][dc * 128:(dc + 1) * 128, :],
                             wbf_d[mi, dc * 128:(dc + 1) * 128, :], 1024))
        for dc in range(8):
            for c4 in range(nexp // 16):
                jobs.append((k8_h[dc * 128:(dc + 1) * 128,
                                  c4 * 2048:(c4 + 1) * 2048],
                             kd_h[dc * 128:(dc + 1) * 128,
                                  c4 * 2048:(c4 + 1) * 2048], 2048))
        for ei in range(nexp // 2):
            jobs.append((ap_make(val_h, 2 * ei * 128 * D,
                                 [[D, 256], [1, D]]),
                         ap_make(vd_h, 2 * ei * 128 * D,
                                 [[D, 256], [1, D]]), 2048))
        for ji, (srcap, dstap, ncol) in enumerate(jobs):
            par = ji % 2
            stg = xb[:, par * 2048: par * 2048 + ncol]
            bfb = a2[:, par * 2048: par * 2048 + ncol]
            if srcap.ap[0][1] == 256:
                srcap = ap_make(srcap, 0, [[2 * D, 128], [1, 2 * D]])
                dstap = ap_make(dstap, 0, [[2 * D, 128], [1, 2 * D]])
            tok_l = P.dma("sp", lambda e, srcap=srcap, stg=stg: e.dma_start(
                stg, srcap), waits=[st.get(("cc", par))])
            eng = "dve" if par == 0 else "act"
            if eng == "dve":
                tk_c = P.op(eng, lambda e, stg=stg, bfb=bfb: e.tensor_copy(
                    bfb, stg), waits=[tok_l])
            else:
                tk_c = P.op(eng, lambda e, stg=stg, bfb=bfb: e.copy(
                    bfb, stg), waits=[tok_l])
            st[("cc", par)] = P.dma("pool",
                lambda e, dstap=dstap, bfb=bfb: e.dma_start(dstap, bfb),
                waits=[tk_c])
        P.fence()

        def emit_ln_pass(gates_too):
            pst_free = {}
            pslg_free = {}
            psg_free = {}
            yev = {}
            prev = {}
            for t in range(NT):
                p = t % 2
                P.op("dve", lambda e, t=t, p=p: e.reduce_sum(S1C[p], xtile(t), AX.X))
                P.op("dve", lambda e, p=p: e.tensor_scalar(
                    S1C[p], S1C[p], 1.0 / D, None, ALU.mult))
                tk_xm = P.op("dve", lambda e, t=t, p=p: e.tensor_scalar(
                    xmb[p], xtile(t), S1C[p], None, ALU.subtract),
                    waits=[prev.get(("sq", p))])
                tk_sq = P.op("act", lambda e, p=p: e.activation(
                    ybf[p], xmb[p], AF.Square, accum_out=S2C[p]),
                    waits=[tk_xm, prev.get(("ytr", p))])
                prev[("sq", p)] = tk_sq
                tk_sd = P.op("act", lambda e, p=p: e.activation(
                    RSC[p], S2C[p], AF.Sqrt, bias=EPSC, scale=1.0 / D),
                    waits=[tk_sq, prev.get(("y", p))])
                tk_rc = P.op("dve", lambda e, p=p: e.reciprocal(RSC[p], RSC[p]),
                             waits=[tk_sd])
                tk_y = P.op("dve", lambda e, p=p: e.tensor_scalar(
                    ybf[p], xmb[p], RSC[p], None, ALU.mult),
                    waits=[tk_rc, prev.get(("ytr", p))])
                prev[("y", p)] = tk_y
                tk_pe = None
                for dc in range(8):
                    w = [tk_y, pst_free.get(t - 1), tok_idn] if dc == 0 else ()
                    tk_pe = P.op("pe", lambda e, dc=dc, p=p: e.transpose(
                        psT[:, dc * 128:(dc + 1) * 128],
                        ybf[p][:, dc * 128:(dc + 1) * 128], idn),
                        waits=w, inc=(dc == 7))
                prev[("ytr", p)] = tk_pe
                yev[t] = P.op("dve", lambda e, t=t: e.tensor_copy(
                    yT[:, t * 1024:(t + 1) * 1024], psT), waits=[tk_pe])
                pst_free[t] = yev[t]
                if not gates_too:
                    continue
                lgs = psE[:, (t % 4) * 64:(t % 4) * 64 + 64]
                for dc in range(8):
                    w = [yev[t], pslg_free.get(t - 4)] if dc == 0 else ()
                    P.op("pe", lambda e, t=t, dc=dc, lgs=lgs: e.matmul(
                        lgs, yT_chunk(t, dc), selbuf[:, dc * 64:(dc + 1) * 64],
                        start=(dc == 0), stop=(dc == 7)), waits=w, inc=(dc == 7))
                tk_lg = {"pe": P.cnt["pe"]}
                tk_sig = P.op("act", lambda e, p=p, lgs=lgs: e.activation(
                    selb[p], lgs, AF.Sigmoid),
                    waits=[tk_lg, prev.get(("gmul", p))])
                pslg_free[t] = tk_sig
                P.op("dve", lambda e, p=p: e.tensor_copy(curb[p], selb[p]),
                     waits=[tk_sig])
                for _ in range(3):
                    P.op("dve", lambda e, p=p: e.reduce_max(S1C[p], curb[p], AX.X))
                    P.op("dve", lambda e, p=p: e.tensor_scalar(
                        geb[p], curb[p], S1C[p], None, ALU.is_ge))
                    P.op("dve", lambda e, p=p: e.scalar_tensor_tensor(
                        curb[p], geb[p], NEG, curb[p], ALU.mult, ALU.add))
                P.op("dve", lambda e, p=p: e.reduce_max(S1C[p], curb[p], AX.X))
                P.op("dve", lambda e, p=p: e.tensor_scalar(
                    geb[p], selb[p], S1C[p], None, ALU.is_ge))
                tk_g = P.op("dve", lambda e, p=p: e.tensor_mul(
                    gateb[p], selb[p], geb[p]), waits=[prev.get(("gtr", p))])
                prev[("gmul", p)] = tk_g
                gsl = psG[:, (t % 4) * 128:(t % 4) * 128 + 128]
                tk_gp = P.op("pe", lambda e, p=p, gsl=gsl: e.transpose(
                    gsl[0:64, :], gateb[p], idn),
                    waits=[tk_g, psg_free.get(t - 4)], inc=True)
                prev[("gtr", p)] = tk_gp
                psg_free[t] = P.op("dve", lambda e, t=t, gsl=gsl: e.tensor_copy(
                    gT[0:64, t * 128:(t + 1) * 128], gsl[0:64, :]),
                    waits=[tk_gp])

        def emit_ffn(b, tag):
            emit_ln_pass(gates_too=True)
            tok_pro = P.fence()
            if "yT1" in taps and tag == "ffn1" and b == 0:
                P.dma("sp", lambda e: e.dma_start(tap_h["yT1"], yT),
                      waits=[tok_pro])
            if "gT" in taps and tag == "ffn1" and b == 0:
                P.dma("sp", lambda e: e.dma_start(tap_h["gT"], gT),
                      waits=[tok_pro])
            h_ps = [psA, psB]
            cd_ps = [psC, psD]
            for eb in range(neb):
                tokw = None
                for dc in range(8):
                    tokw = P.dma("sp", lambda e, dc=dc, eb=eb: e.dma_start(
                        keysb[:, dc * 1024:(dc + 1) * 1024],
                        kd_h[dc * 128:(dc + 1) * 128,
                             eb * 1024:(eb + 1) * 1024]),
                        waits=[st.get("keys_readers")] if dc == 0 else (),
                        group=("w", tag, eb))
                for el in range(ebs):
                    tokw = P.dma("sp", lambda e, el=el, eb=eb: e.dma_start(
                        valsb[:, el * 1024:(el + 1) * 1024],
                        vd_h[eb * ebs + el, :, :]),
                        waits=[st.get("vals_readers")] if el == 0 else (),
                        group=("w", tag, eb))
                tok_w = dict(tokw)
                for tg in range(4):
                    for el in range(ebs):
                        i = tg * ebs + el
                        hp_ = h_ps[i % 2]
                        bp_ = cd_ps[i % 2]
                        gp_ = gbc[i % 2]
                        for dc in range(8):
                            w = [tok_w, st.get(("httok", i % 2))] if dc == 0 else ()
                            P.op("pe", lambda e, dc=dc, el=el, tg=tg, hp_=hp_:
                                 e.matmul(
                                hp_,
                                keysb[:, dc * 1024 + el * 128:dc * 1024 + el * 128 + 128],
                                yT_rhs(dc, tg * 512, 512),
                                start=(dc == 0), stop=(dc == 7)),
                                waits=w, inc=(dc == 7))
                        tk_h = {"pe": P.cnt["pe"]}
                        tk_bc = P.op("pe", lambda e, el=el, eb=eb, tg=tg, bp_=bp_:
                                     e.matmul(
                            bp_, ohsel(eb * ebs + el),
                            gT[0:64, tg * 512:(tg + 1) * 512],
                            start=True, stop=True),
                            waits=[tok_oh, st.get(("cdtok", i % 2))], inc=True)
                        tk_be = P.op("act", lambda e, gp_=gp_, bp_=bp_:
                                     e.copy(gp_, bp_),
                                     waits=[tk_bc, st.get(("httok", i % 2))])
                        st[("cdtok", i % 2)] = tk_be
                        st[("httok", i % 2)] = P.op(
                            "dve", lambda e, el=el, tg=tg, hp_=hp_, gp_=gp_:
                            e.scalar_tensor_tensor(
                                scT[:, el * S + tg * 512:el * S + tg * 512 + 512],
                                hp_, 0.0, gp_, ALU.max, ALU.mult),
                            waits=[tk_h, tk_be, st.get("scT_readers")])
                st["keys_readers"] = {"pe": P.cnt["pe"]}
                tok_p1 = P.tok()
                if "scT0" in taps and tag == "ffn1" and b == 0 and eb == 0:
                    P.dma("sp", lambda e: e.dma_start(tap_h["scT0"], scT),
                          waits=[tok_p1])
                for t in range(NT):
                    for oh in range(2):
                        j = t * 2 + oh
                        op_ = cd_ps[j % 2]
                        for el in range(ebs):
                            w = [tok_p1, st.get(("cdtok", j % 2))] if el == 0 else ()
                            P.op("pe", lambda e, el=el, t=t, oh=oh, op_=op_:
                                 e.matmul(
                                op_,
                                scT[:, el * S + t * 128:el * S + t * 128 + 128],
                                valsb[:, el * 1024 + oh * 512:el * 1024 + oh * 512 + 512],
                                start=(el == 0), stop=(el == ebs - 1)),
                                waits=w, inc=(el == ebs - 1))
                        tk_o = {"pe": P.cnt["pe"]}
                        st[("cdtok", j % 2)] = P.op(
                            "dve", lambda e, t=t, oh=oh, op_=op_:
                            e.scalar_tensor_tensor(
                                xtile(t, oh * 512, 512), op_, 0.5,
                                xtile(t, oh * 512, 512), ALU.mult, ALU.add),
                            waits=[tk_o])
                st["vals_readers"] = {"pe": P.cnt["pe"]}
                st["scT_readers"] = {"pe": P.cnt["pe"]}
            P.fence()

        def emit_attn(b):
            emit_ln_pass(gates_too=False)
            tok_yT = P.fence()
            tok_muq = []
            for i in range(2):
                tok_muq.append(P.dma("sp", lambda e, i=i: e.dma_start(
                    qh[i][64:67, :], mu_d), waits=[tok_yT, tok_mud]))
            for i in range(2):
                P.op("dve", lambda e, i=i: e.tensor_copy(
                    kxb[i][64:67, :], kxis), waits=[tok_kxi])
            gpair = [0]
            exp_done = {}
            psv_free = {}
            core_pe = {}
            for hp in range(nhp):
                P.fence()
                for mi, m, stg, scale in ((0, "q", qstage, 0.125),
                                          (1, "k", kstage, 1.0),
                                          (2, "v", vstage, 1.0)):
                    tok_w = None
                    for dc in range(8):
                        tok_w = P.dma("sp", lambda e, mi=mi, dc=dc: e.dma_start(
                            wbuf[:, dc * 1024:(dc + 1) * 1024],
                            wbf_d[mi, dc * 128:(dc + 1) * 128, :]),
                            waits=[st.get("wbuf_free")] if dc == 0 else (),
                            group=("whp", b, hp, m))
                    for oc in range(8):
                        half = oc % 2
                        psl = [psE, psD][half][:, 0:256]
                        P.op("pe", lambda e, m=m, oc=oc, psl=psl: e.matmul(
                            psl, BROW[m][0:1, oc * 128:(oc + 1) * 128],
                            onesr[0:1, 0:256], start=True, stop=False),
                            waits=[tok_w, tok_ones, st.get(("psEh", half))])
                        for dc in range(8):
                            P.op("pe", lambda e, dc=dc, oc=oc, hp=hp, psl=psl:
                                 e.matmul(
                                psl, wbuf[:, dc * 1024 + oc * 128:
                                          dc * 1024 + oc * 128 + 128],
                                yT_rhs(dc, hp * 256, 256),
                                start=False, stop=(dc == 7)), inc=(dc == 7))
                        tk = {"pe": P.cnt["pe"]}
                        st[("psEh", half)] = P.op(
                            "act", lambda e, stg=stg, oc=oc, psl=psl,
                            scale=scale: e.activation(
                                stg[:, oc * 256:(oc + 1) * 256], psl, AF.Copy,
                                scale=scale),
                            waits=[tk, core_pe.get(hp - 1)])
                    st["wbuf_free"] = {"pe": P.cnt["pe"]}
                tok_stages = P.tok()
                tok_wo = P.dma("sp", lambda e, hp=hp: e.dma_start(
                    wo_s, wbf_d[3, hp * 128:(hp + 1) * 128, :]),
                    waits=[st.get("oproj_pe")])
                for ph in range(2):
                    h = hp * 2 + ph
                    bw = [tok_stages, core_pe.get(hp - 1)]
                    tks = {}
                    for nm, srcstg, dst in (("q", qstage, qh[ph]),
                                            ("k", kstage, kh[ph]),
                                            ("v", vstage, avT)):
                        for j in range(16):
                            oc, half = j // 2, j % 2
                            tks[nm] = P.op("dve", lambda e, srcstg=srcstg,
                                           dst=dst, j=j, oc=oc, half=half,
                                           ph=ph: e.tensor_copy(
                                ap_make(dst, j, [[dst.ap[0][0], 64],
                                                 [16, 128]]),
                                srcstg[half * 64:(half + 1) * 64,
                                       oc * 256 + ph * 128:
                                       oc * 256 + ph * 128 + 128]),
                                waits=bw if j == 0 else ())
                    for w in range(16):
                        P.op("pe", lambda e, w=w: e.transpose(
                            psT[0:128, w * 64:(w + 1) * 64],
                            avT[0:64, w * 128:(w + 1) * 128], idn[0:64, 0:64]),
                            waits=[tks["v"], st.get("psT_free"), tok_idn]
                            if w == 0 else (), inc=(w == 15))
                    tk_tr = {"pe": P.cnt["pe"]}
                    tk_ms = P.op("dve", lambda e, ph=ph: e.memset(
                        ap_make(vaug[ph], 64, [vaug[ph].ap[0], [65, NT], [1, 1]]),
                        1.0), waits=[core_pe.get(hp - 1)])
                    tk_av = P.op("dve", lambda e, ph=ph: e.tensor_copy(
                        ap_make(vaug[ph], 0, [vaug[ph].ap[0], [65, NT], [1, 64]]),
                        ap_make(psT, 0, [psT.ap[0], [64, NT], [1, 64]])),
                        waits=[tk_tr, tk_ms])
                    st["psT_free"] = tk_av
                    tk_sh = P.op("dve", lambda e, ph=ph: e.tensor_copy(
                        vsh[ph], vaug[ph][64:128, :]), waits=[tk_av])
                    tok_ak = P.dma("sp", lambda e, ph=ph: e.dma_start(
                        akd_h, kh[ph][0:64, :]),
                        waits=[tks["k"], st.get("akd_free")])
                    tok_avd = P.dma("sp", lambda e, ph=ph: e.dma_start(
                        avd_h, vaug[ph]), waits=[tk_av, st.get("avd_free")])
                    gx = []
                    for t in range(3):
                        gx.append(P.dma("pool", lambda e, t=t, ph=ph:
                                        e.indirect_dma_start(
                            kxb[ph][0:64, t * 64:(t + 1) * 64], None, akd_h,
                            bass.IndirectOffsetOnAxis(kxidx[:, t:t + 1], 1)),
                            waits=[tok_ak] if t == 0 else (),
                            group=("gk", b, h)))
                    for t in range(3):
                        dst = (vxg[ph][0:64, 0:65] if t == 0 else
                               vxg[ph][64:128, 0:65] if t == 1 else
                               vxg[ph][0:64, 65:130])
                        gx.append(P.dma("pool", lambda e, t=t, dst=dst:
                                        e.indirect_dma_start(
                            dst, None, avd_h,
                            bass.IndirectOffsetOnAxis(avidx[:, t:t + 1], 1)),
                            waits=[tok_avd] if t == 0 else (),
                            group=("gk", b, h)))
                    st["akd_free"] = dict(gx[-1])
                    st["avd_free"] = dict(gx[-1])
                    tok_bufs = P.tok()
                    qb_ = qh[ph]
                    kb_ = kh[ph]
                    for pr in range(8):
                        gp = gpair[0]
                        gpair[0] += 1
                        pp = gp % 2
                        fullp = [psA, psB][pp]
                        msets = []
                        wfirst = [tok_bufs, exp_done.get(gp - 2), tok_muq[ph]]
                        wtail = [exp_done.get(gp - 1)]
                        for ql in range(2):
                            qt = pr * 2 + ql
                            q_rhs = qb_[0:64, qt * 128:(qt + 1) * 128]
                            qx_rhs = qb_[0:67, qt * 128:(qt + 1) * 128]
                            tailp = [psC, psD][ql]
                            P.op("pe", lambda e, kb_=kb_, qt=qt, fullp=fullp,
                                 ql=ql, q_rhs=q_rhs: e.matmul(
                                fullp[:, ql * 256:ql * 256 + 128],
                                kb_[0:64, qt * 128:(qt + 1) * 128], q_rhs,
                                start=True, stop=True),
                                waits=wfirst if ql == 0 else ())
                            P.op("pe", lambda e, fullp=fullp, ql=ql, ph=ph,
                                 qx_rhs=qx_rhs: e.matmul(
                                fullp[:, ql * 256 + 128:ql * 256 + 256],
                                kxb[ph][0:67, 0:128], qx_rhs,
                                start=True, stop=True))
                            if qt != 0:
                                P.op("pe", lambda e, kb_=kb_, tailp=tailp,
                                     q_rhs=q_rhs: e.matmul(
                                    tailp[0:64, 0:128], kb_[0:64, 0:64], q_rhs,
                                    start=True, stop=True), waits=wtail)
                            else:
                                msets.append(P.op("dve", lambda e, tailp=tailp:
                                    e.memset(tailp[0:64, 0:128], NEG),
                                    waits=wtail))
                            if qt != NT - 1:
                                P.op("pe", lambda e, kb_=kb_, tailp=tailp,
                                     q_rhs=q_rhs: e.matmul(
                                    tailp[0:64, 128:256],
                                    kb_[0:64, 1984:2048], q_rhs,
                                    start=True, stop=True), waits=wtail)
                            else:
                                msets.append(P.op("dve", lambda e, tailp=tailp:
                                    e.memset(tailp[0:64, 128:256], NEG),
                                    waits=wtail))
                            if qt != 0:
                                P.op("pe", lambda e, kb_=kb_, qt=qt,
                                     tailp=tailp, qb_=qb_: e.matmul(
                                    tailp[0:64, 256:320],
                                    kb_[0:64, (2 * qt - 1) * 64:(2 * qt) * 64],
                                    qb_[0:64, qt * 128:qt * 128 + 64],
                                    start=True, stop=True), waits=wtail)
                            else:
                                msets.append(P.op("dve", lambda e, tailp=tailp:
                                    e.memset(tailp[0:64, 256:320], NEG),
                                    waits=wtail))
                            if qt != NT - 1:
                                P.op("pe", lambda e, kb_=kb_, qt=qt,
                                     tailp=tailp, qb_=qb_: e.matmul(
                                    tailp[0:64, 320:384],
                                    kb_[0:64, (2 * qt + 2) * 64:
                                        (2 * qt + 3) * 64],
                                    qb_[0:64, qt * 128 + 64:qt * 128 + 128],
                                    start=True, stop=True), waits=wtail)
                            else:
                                msets.append(P.op("dve", lambda e, tailp=tailp:
                                    e.memset(tailp[0:64, 320:384], NEG),
                                    waits=wtail))
                            P.op("pe", lambda e, tailp=tailp, ph=ph,
                                 qx_rhs=qx_rhs: e.matmul(
                                tailp[0:64, 384:512],
                                kxb[ph][0:67, 128:192], qx_rhs,
                                start=True, stop=True), waits=wtail,
                                inc=(ql == 1))
                        tk_st_pe = {"pe": P.cnt["pe"]}
                        wexp = [tk_st_pe] + msets
                        P.op("act", lambda e, fullp=fullp: e.activation(
                            eful, fullp, AF.Exp), waits=wexp)
                        P.op("act", lambda e: e.activation(
                            etail[0:64, 0:512], psC[0:64, :], AF.Exp))
                        exp_done[gp] = P.op("act", lambda e: e.activation(
                            etail[0:64, 512:1024], psD[0:64, :], AF.Exp))
                        for ql in range(2):
                            qt = pr * 2 + ql
                            qv = (qt % 4) * 128
                            w0 = [exp_done[gp],
                                  psv_free.get(h * 4 + qt // 4 - 1),
                                  st.get("psV_oproj")]
                            P.op("pe", lambda e, qt=qt, qv=qv, ql=ql, ph=ph:
                                 e.matmul(
                                psV[0:65, qv:qv + 128],
                                vaug[ph][:, qt * 65:qt * 65 + 65],
                                eful[:, ql * 256:ql * 256 + 128],
                                start=True, stop=False), waits=w0)
                            P.op("pe", lambda e, qv=qv, ql=ql, ph=ph: e.matmul(
                                psV[0:65, qv:qv + 128],
                                vxg[ph][:, 0:65],
                                eful[:, ql * 256 + 128:ql * 256 + 256],
                                start=False, stop=False))
                            P.op("pe", lambda e, qv=qv, ql=ql, ph=ph: e.matmul(
                                psV[0:65, qv:qv + 128],
                                vaug[ph][0:64, 0:65],
                                etail[0:64, ql * 512:ql * 512 + 128],
                                start=False, stop=False))
                            P.op("pe", lambda e, qv=qv, ql=ql, ph=ph: e.matmul(
                                psV[0:65, qv:qv + 128],
                                vsh[ph][0:64, 15 * 65:15 * 65 + 65],
                                etail[0:64, ql * 512 + 128:ql * 512 + 256],
                                start=False, stop=False))
                            if qt != 0:
                                P.op("pe", lambda e, qt=qt, qv=qv, ql=ql,
                                     ph=ph: e.matmul(
                                    psV[0:65, qv:qv + 64],
                                    vsh[ph][0:64, (qt - 1) * 65:
                                            (qt - 1) * 65 + 65],
                                    etail[0:64, ql * 512 + 256:
                                          ql * 512 + 320],
                                    start=False, stop=False))
                            if qt != NT - 1:
                                P.op("pe", lambda e, qt=qt, qv=qv, ql=ql,
                                     ph=ph: e.matmul(
                                    psV[0:65, qv + 64:qv + 128],
                                    vaug[ph][0:64, (qt + 1) * 65:
                                             (qt + 1) * 65 + 65],
                                    etail[0:64, ql * 512 + 320:
                                          ql * 512 + 384],
                                    start=False, stop=False))
                            P.op("pe", lambda e, qv=qv, ql=ql, ph=ph: e.matmul(
                                psV[0:65, qv:qv + 128],
                                vxg[ph][0:64, 65:130],
                                etail[0:64, ql * 512 + 384:ql * 512 + 512],
                                start=False, stop=True), inc=True)
                        if pr % 2 == 1:
                            g4 = pr // 2
                            tk_pv = {"pe": P.cnt["pe"]}
                            P.op("act", lambda e: e.activation(
                                lnz, psV[64:65, 0:512], AF.Ln),
                                waits=[tk_pv, st.get("rz_reader")])
                            def rzfn(e):
                                with nc.allow_low_precision(
                                        reason="1/Z as bf16 matmul operand"):
                                    return e.activation(rz, lnz, AF.Exp,
                                                        scale=-1.0)
                            tk_rz = P.op("act", rzfn)
                            tk_zb = P.op("pe", lambda e: e.matmul(
                                psE[0:64, 0:512], onesr[0:1, 0:64], rz,
                                start=True, stop=True),
                                waits=[tk_rz, st.get("psE_free")], inc=True)
                            st["rz_reader"] = tk_zb
                            tk_ze = P.op("act", lambda e: e.copy(
                                rzb, psE[0:64, 0:512]),
                                waits=[tk_zb, st.get("rzb_free")])
                            st["psE_free"] = tk_ze
                            tk_at = P.op("dve", lambda e, g4=g4, ph=ph:
                                         e.scalar_tensor_tensor(
                                attnc[ph * 64:(ph + 1) * 64,
                                      g4 * 512:(g4 + 1) * 512],
                                psV[0:64, 0:512], 1.0, rzb,
                                ALU.mult, ALU.mult), waits=[tk_ze])
                            st["rzb_free"] = tk_at
                            psv_free[h * 4 + g4] = tk_at
                core_pe[hp] = {"pe": P.cnt["pe"]}
                tok_attnc = P.tok()
                if "attnc0" in taps and b == 0 and hp == 0:
                    P.dma("sp", lambda e: e.dma_start(tap_h["attnc0"], attnc),
                          waits=[tok_attnc])
                for t in range(NT):
                    for oh in range(2):
                        j = t * 2 + oh
                        op_ = [psE, psV][j % 2]
                        w0 = [tok_attnc, tok_wo,
                              st.get("psE_free") if j % 2 == 0 else
                              st.get("psV_oproj")]
                        if hp == 0:
                            P.op("pe", lambda e, oh=oh, op_=op_: e.matmul(
                                op_, onesr[0:1, 0:128],
                                BROW["o"][0:1, oh * 512:(oh + 1) * 512],
                                start=True, stop=False), waits=w0)
                            P.op("pe", lambda e, t=t, oh=oh, op_=op_: e.matmul(
                                op_, attnc[:, t * 128:(t + 1) * 128],
                                wo_s[:, oh * 512:(oh + 1) * 512],
                                start=False, stop=True), inc=True)
                        else:
                            P.op("pe", lambda e, t=t, oh=oh, op_=op_: e.matmul(
                                op_, attnc[:, t * 128:(t + 1) * 128],
                                wo_s[:, oh * 512:(oh + 1) * 512],
                                start=True, stop=True), waits=w0, inc=True)
                        tk_o = {"pe": P.cnt["pe"]}
                        tk_x = P.op("dve", lambda e, t=t, oh=oh, op_=op_:
                                    e.scalar_tensor_tensor(
                            xtile(t, oh * 512, 512), op_, 1.0,
                            xtile(t, oh * 512, 512), ALU.mult, ALU.add),
                            waits=[tk_o])
                        if j % 2 == 0:
                            st["psE_free"] = tk_x
                        else:
                            st["psV_oproj"] = tk_x
                st["oproj_pe"] = {"pe": P.cnt["pe"]}
            st.pop("psV_oproj", None)
            st.pop("oproj_pe", None)
            P.fence()

        for b in range(nb):
            xw = st.get("x_stored")
            for t in range(NT):
                P.dma("sp", lambda e, b=b, t=t: e.dma_start(
                    xtile(t), x_h[b, t * 128:(t + 1) * 128, :]),
                    waits=[xw] if t == 0 else (), group=("xl", b))
            P.fence()
            if do_ffn1:
                emit_ffn(b, "ffn1")
            if do_attn:
                emit_attn(b)
            if do_ffn3:
                emit_ffn(b, "ffn3")
            wtok = P.fence()
            for t in range(NT):
                P.dma("sp", lambda e, b=b, t=t: e.dma_start(
                    out_h[b, t * 128:(t + 1) * 128, :], xtile(t)),
                    waits=[wtok] if t == 0 else (), group=("xs", b))
            st["x_stored"] = P.tok()

        P._waits("sp", [P.tok()])
        P.emit(E(nc.Block()), sems)
    return nc


def make_consts():
    idn = np.eye(128, dtype=np.float32).astype(BFDT)
    ones = np.ones((1, 512), dtype=np.float32).astype(BFDT)
    qbvec = (np.arange(S, dtype=np.float32) // 64).reshape(1, S).astype(np.float32)
    kxind = np.zeros((3, 192), dtype=np.float32)
    for t in range(3):
        kxind[t, t * 64:(t + 1) * 64] = 1.0
    oh64 = np.eye(64, dtype=np.float32).astype(BFDT)
    iotak = np.stack([np.arange(64, dtype=np.int64) * S,
                      np.arange(64, dtype=np.int64) * (NT * 65)],
                     axis=1).astype(np.int32)
    return idn, ones, qbvec, kxind.astype(BFDT), oh64, iotak


def make_inputs(core_slice_x, Wq, bq, Wk, bk, Wv, bv, Wo, bo, src, tgt,
                sel, keys, values):
    k8 = np.ascontiguousarray(
        np.asarray(keys, np.float32).transpose(1, 0, 2).reshape(D, -1))
    val = np.ascontiguousarray(np.asarray(values, np.float32))
    selT = np.ascontiguousarray(np.asarray(sel, np.float32).T)
    idn, ones, qbvec, kxind, oh64, iotak = make_consts()
    return {
        "x": np.ascontiguousarray(np.asarray(core_slice_x, np.float32)),
        "k8": k8, "val": val, "selT": selT,
        "wqT": np.ascontiguousarray(np.asarray(Wq, np.float32).T),
        "wkT": np.ascontiguousarray(np.asarray(Wk, np.float32).T),
        "wvT": np.ascontiguousarray(np.asarray(Wv, np.float32).T),
        "woT": np.ascontiguousarray(np.asarray(Wo, np.float32).T),
        "biases": np.concatenate([np.asarray(v, np.float32).ravel()
                                  for v in (bq, bk, bv, bo)]).reshape(1, 4 * D),
        "srctgt": np.concatenate([np.asarray(src, np.int32).ravel(),
                                  np.asarray(tgt, np.int32).ravel()]).reshape(1, 6),
        "idn": idn, "onesrow": ones, "qbvec": qbvec, "kxind": kxind,
        "oh64": oh64, "iotak": iotak,
    }


def kernel(L0_attn_x, L0_attn_Wq, L0_attn_bq, L0_attn_Wk, L0_attn_bk,
           L0_attn_Wv, L0_attn_bv, L0_attn_Wo, L0_attn_bo,
           L0_attn_src_blocks, L0_attn_tgt_blocks,
           L0_ffn_expert_sel, L0_ffn_keys, L0_ffn_values):
    x = np.asarray(L0_attn_x, dtype=np.float32)
    nc = build_program({})
    in_maps = []
    for c in range(NCORES):
        in_maps.append(make_inputs(
            x[2 * c:2 * c + 2], L0_attn_Wq, L0_attn_bq, L0_attn_Wk, L0_attn_bk,
            L0_attn_Wv, L0_attn_bv, L0_attn_Wo, L0_attn_bo,
            L0_attn_src_blocks, L0_attn_tgt_blocks,
            L0_ffn_expert_sel, L0_ffn_keys, L0_ffn_values))
    res = run_bass_kernel_spmd(nc, in_maps, core_ids=list(range(NCORES)))
    out = np.concatenate([r["out"] for r in res.results], axis=0)
    return out.astype(np.float32)


# version 04
# speedup vs baseline: 1.0683x; 1.0683x over previous
import numpy as np
import ml_dtypes
import concourse.bass as bass
import concourse.mybir as mybir
from concourse.bass_utils import run_bass_kernel_spmd
from contextlib import ExitStack

F32 = mybir.dt.float32
BF16 = mybir.dt.bfloat16
I32 = mybir.dt.int32
AF = mybir.ActivationFunctionType
ALU = mybir.AluOpType
AX = mybir.AxisListType
BFDT = ml_dtypes.bfloat16

D = 1024
S = 2048
NT = 16
NCORES = 8
NEG = -30000.0


class Plan:
    ENGS = ("pe", "act", "dve", "pool", "sp")
    SELF = {"dve", "act", "pool"}

    def __init__(self):
        self.streams = {e: [] for e in self.ENGS}
        self.cnt = {"pe": 0, "act": 0, "dve": 0, "pool": 0, "d0": 0, "d1": 0}
        self.mark = {}
        self.groups = {}

    def _wait(self, eng, sem, val):
        if val is None or val <= 0:
            return
        k = (eng, sem)
        if self.mark.get(k, 0) >= val:
            return
        self.mark[k] = val
        self.streams[eng].append(("w", sem, val))

    def _waits(self, eng, waits):
        for w in waits:
            if not w:
                continue
            if isinstance(w, dict):
                for s_, v in w.items():
                    self._wait(eng, s_, v)
            else:
                self._wait(eng, w[0], w[1])

    def op(self, eng, fn, waits=(), inc=False):
        self._waits(eng, waits)
        incs = []
        tok = {}
        if eng in self.SELF:
            prev = self.cnt[eng]
            self._wait(eng, eng, prev)
            self.cnt[eng] = prev + 1
            incs.append((eng, 1))
            tok[eng] = prev + 1
        if inc and eng == "pe":
            self.cnt["pe"] += 1
            incs.append(("pe", 1))
            tok["pe"] = self.cnt["pe"]
        self.streams[eng].append(("o", fn, incs))
        return tok

    def dma(self, eng, fn, waits=(), sem=None, group=None):
        sem = sem or ("d0" if eng == "sp" else "d1")
        gkey = (eng, sem)
        if group is None or self.groups.get(gkey) != (sem, group):
            self._wait(eng, sem, self.cnt[sem])
            self.groups[gkey] = (sem, group) if group is not None else None
        self._waits(eng, waits)
        self.cnt[sem] += 16
        self.streams[eng].append(("o", fn, [(sem, 16)]))
        return {sem: self.cnt[sem]}

    def tok(self):
        return dict(self.cnt)

    def fence(self):
        t = self.tok()
        for e in self.ENGS:
            self._waits(e, [t])
        return t

    def emit(self, block, sems):
        def runner(name):
            def run(engine):
                for it in self.streams[name]:
                    if it[0] == "w":
                        engine.wait_ge(sems[it[1]], it[2])
                    else:
                        inst = it[1](engine)
                        for s_, n in it[2]:
                            inst.then_inc(sems[s_], n)
            return run
        block.tensor(runner("pe"))
        block.scalar(runner("act"))
        block.vector(runner("dve"))
        block.gpsimd(runner("pool"))
        block.sync(runner("sp"))


def ap_dyn(apx, reg):
    return bass.AP(apx.tensor, reg, [list(d) for d in apx.ap])


def ap_make(apx, rel, dims):
    return bass.AP(apx.tensor, apx.offset + rel,
                   [list(d) if not isinstance(d, int) else d for d in dims])


def build_program(cfg):
    nexp = cfg.get("nexp", 64)
    ebs = cfg.get("ebs", 8)
    neb = nexp // ebs
    nh = cfg.get("nh", 16)
    nhp = nh // 2
    nb = cfg.get("nb", 2)
    do_ffn1 = cfg.get("ffn1", True)
    do_attn = cfg.get("attn", True)
    do_ffn3 = cfg.get("ffn3", True)
    taps = cfg.get("taps", ())

    nc = bass.Bass("TRN2", target_bir_lowering=False)

    x_h = nc.dram_tensor("x", [nb, S, D], F32, kind="ExternalInput").ap()
    k8_h = nc.dram_tensor("k8", [D, nexp * 128], F32, kind="ExternalInput").ap()
    val_h = nc.dram_tensor("val", [nexp, 128, D], F32, kind="ExternalInput").ap()
    selT_h = nc.dram_tensor("selT", [D, 64], F32, kind="ExternalInput").ap()
    wT_h = {m: nc.dram_tensor("w%sT" % m, [D, D], F32, kind="ExternalInput").ap()
            for m in "qkvo"}
    bias_h = nc.dram_tensor("biases", [1, 4 * D], F32, kind="ExternalInput").ap()
    srctgt_h = nc.dram_tensor("srctgt", [1, 6], I32, kind="ExternalInput").ap()
    idn_h = nc.dram_tensor("idn", [128, 128], BF16, kind="ExternalInput").ap()
    ones_h = nc.dram_tensor("onesrow", [1, 512], BF16, kind="ExternalInput").ap()
    qbvec_h = nc.dram_tensor("qbvec", [1, S], F32, kind="ExternalInput").ap()
    kxind_h = nc.dram_tensor("kxind", [3, 192], BF16, kind="ExternalInput").ap()
    oh64_h = nc.dram_tensor("oh64", [64, 64], BF16, kind="ExternalInput").ap()
    iotak_h = nc.dram_tensor("iotak", [64, 2], I32, kind="ExternalInput").ap()
    out_h = nc.dram_tensor("out", [nb, S, D], F32, kind="ExternalOutput").ap()
    mu_d = nc.dram_tensor("mu_d", [3, S], BF16, kind="Internal").ap()
    akd_h = nc.dram_tensor("akd", [64, S], BF16, kind="Internal").ap()
    avd_h = nc.dram_tensor("avd", [128, NT * 65], BF16, kind="Internal").ap()
    wbf_d = nc.dram_tensor("wbf", [4, D, D], BF16, kind="Internal").ap()
    kd_h = nc.dram_tensor("kd", [D, nexp * 128], BF16, kind="Internal").ap()
    vd_h = nc.dram_tensor("vd", [nexp, 128, D], BF16, kind="Internal").ap()
    tap_h = {}
    for tname in taps:
        shape, dt = {"yT1": ([128, NT * 1024], BF16), "gT": ([64, S], BF16),
                     "scT0": ([128, ebs * S], BF16),
                     "attnc0": ([128, S], BF16)}[tname]
        tap_h[tname] = nc.dram_tensor("tap_" + tname, shape, dt,
                                      kind="ExternalOutput").ap()

    P = Plan()
    st = {}

    with ExitStack() as es:
        E = es.enter_context
        xb = E(nc.sbuf_tensor("xb", [128, NT * 1024], F32)).ap()
        yT = E(nc.sbuf_tensor("yT", [128, NT * 1024], BF16)).ap()
        a1 = E(nc.sbuf_tensor("a1", [128, 18688], BF16)).ap()
        a2 = E(nc.sbuf_tensor("a2", [128, 16384], BF16)).ap()
        gT = E(nc.sbuf_tensor("gT", [64, S], BF16)).ap()
        selbuf = E(nc.sbuf_tensor("selbuf", [128, 8 * 64], BF16)).ap()
        selb = [E(nc.sbuf_tensor("sel%d" % i, [128, 64], F32)).ap()
                for i in range(2)]
        curb = [E(nc.sbuf_tensor("cur%d" % i, [128, 64], F32)).ap()
                for i in range(2)]
        geb = [E(nc.sbuf_tensor("ge%d" % i, [128, 64], F32)).ap()
               for i in range(2)]
        gateb = [E(nc.sbuf_tensor("gate%d" % i, [128, 64], BF16)).ap()
                 for i in range(2)]
        xmb = [E(nc.sbuf_tensor("xm%d" % i, [128, 1024], F32)).ap()
               for i in range(2)]
        ybf = [E(nc.sbuf_tensor("ybf%d" % i, [128, 1024], BF16)).ap()
               for i in range(2)]
        gbc = [E(nc.sbuf_tensor("gbc%d" % i, [128, 512], BF16)).ap()
               for i in range(2)]
        col1 = E(nc.sbuf_tensor("col1", [128, 8], F32)).ap()
        idn = E(nc.sbuf_tensor("idn_s", [128, 128], BF16)).ap()
        onesr = E(nc.sbuf_tensor("onesr", [1, 512], BF16)).ap()
        oh64 = E(nc.sbuf_tensor("oh64_s", [64, 64], BF16)).ap()
        biasb = E(nc.sbuf_tensor("biasb", [1, 4 * D], BF16)).ap()
        srctgt = E(nc.sbuf_tensor("srctgt_s", [1, 6], I32)).ap()
        scr = E(nc.sbuf_tensor("scr", [1, 64], F32)).ap()
        col0i = E(nc.sbuf_tensor("col0i", [1, 3], I32)).ap()
        coli2 = E(nc.sbuf_tensor("coli2", [1, 3], I32)).ap()
        scr3i = E(nc.sbuf_tensor("scr3i", [1, 3], I32)).ap()
        tgtbf = E(nc.sbuf_tensor("tgtbf", [1, 9], BF16)).ap()
        tgtf = E(nc.sbuf_tensor("tgtf", [1, 9], F32)).ap()
        iotak = E(nc.sbuf_tensor("iotak_s", [64, 2], I32)).ap()
        kxidx = E(nc.sbuf_tensor("kxidx", [64, 3], I32)).ap()
        avidx = E(nc.sbuf_tensor("avidx", [64, 3], I32)).ap()
        idxf = E(nc.sbuf_tensor("idxf", [64, 4], F32)).ap()
        kxis = E(nc.sbuf_tensor("kxis", [3, 192], BF16)).ap()
        rz = E(nc.sbuf_tensor("rz", [1, 512], BF16)).ap()
        lnz = E(nc.sbuf_tensor("lnz", [1, 512], F32)).ap()

        scT = a1[:, 0:ebs * S]
        qh = [a1[0:67, 0:2048], a1[0:67, 2048:4096]]
        kh = [a1[0:64, 4096:6144], a1[0:64, 6144:8192]]
        avT = a1[0:64, 8192:10240]
        vaug = [a1[:, 10240:11280], a1[:, 11280:12320]]
        vsh = [a1[0:64, 12320:13360], a1[0:64, 13360:14400]]
        attnc = a1[:, 14400:16448]
        eful = a1[:, 16448:16960]
        etail = a1[0:64, 16960:17984]
        rzb = a1[0:64, 17984:18496]
        keysb = a2[:, 0:8 * 1024]
        valsb = a2[:, 8 * 1024:8 * 1024 + ebs * 1024]
        wbuf = a2[:, 0:8 * 1024]
        qstage = a2[:, 8192:10240]
        kstage = a2[:, 10240:12288]
        vstage = a2[:, 12288:14336]
        wo_s = a2[:, 14336:15360]
        kxb = [a2[0:67, 15360:15552], a2[0:67, 15552:15744]]
        vxg = [a2[:, 15744:15874], a2[:, 15874:16004]]
        wtmp = [a2[:, 0:1024], a2[:, 1024:2048]]
        qbv = xb[0:1, 0:S]
        mrow = yT[0:1, 0:3 * S]

        psA = E(nc.psum_tensor("psA", [128, 512], F32)).ap()
        psB = E(nc.psum_tensor("psB", [128, 512], F32)).ap()
        psC = E(nc.psum_tensor("psC", [128, 512], F32)).ap()
        psD = E(nc.psum_tensor("psD", [128, 512], F32)).ap()
        psE = E(nc.psum_tensor("psE", [128, 512], F32)).ap()
        psV = E(nc.psum_tensor("psV", [128, 512], F32)).ap()
        psT = E(nc.psum_tensor("psT", [128, 1024], BF16)).ap()
        psVB = E(nc.psum_tensor("psVB", [128, 512], F32)).ap()

        sems = {}
        for s_ in ("d0", "d1", "pe", "act", "dve", "pool"):
            sems[s_] = E(nc.semaphore(s_))

        EPSC = col1[:, 3:4]
        S1C = [col1[:, 0:1], col1[:, 4:5]]
        S2C = [col1[:, 1:2], col1[:, 5:6]]
        RSC = [col1[:, 2:3], col1[:, 6:7]]
        BROW = {m: biasb[0:1, i * D:(i + 1) * D] for i, m in enumerate("qkvo")}

        def xtile(t, lo=0, n=1024):
            return xb[:, t * 1024 + lo: t * 1024 + lo + n]

        def yT_chunk(t, dc):
            return yT[:, t * 1024 + dc * 128: t * 1024 + dc * 128 + 128]

        def yT_rhs(dc, tok0, ntok):
            return ap_make(yT, (tok0 // 128) * 1024 + dc * 128,
                           [yT.ap[0], [1024, ntok // 128], [1, 128]])

        def ohsel(e):
            return ap_make(oh64, e, [oh64.ap[0], [0, 128]])

        tok_idn = P.dma("sp", lambda e: e.dma_start(idn, idn_h))
        tok_ones = P.dma("sp", lambda e: e.dma_start(onesr, ones_h))
        tok_oh = P.dma("sp", lambda e: e.dma_start(oh64, oh64_h))
        tok_st = P.dma("sp", lambda e: e.dma_start(srctgt, srctgt_h))
        tok_qbv = P.dma("sp", lambda e: e.dma_start(qbv, qbvec_h))
        tok_kxi = P.dma("sp", lambda e: e.dma_start(kxis, kxind_h))
        P.dma("pool", lambda e: e.dma_start(biasb, bias_h))
        for dc in range(8):
            P.dma("pool", lambda e, dc=dc: e.dma_start(
                selbuf[:, dc * 64:(dc + 1) * 64],
                selT_h[dc * 128:(dc + 1) * 128, :]), group="setup")
        P.op("dve", lambda e: e.memset(EPSC, 1e-5))

        sf = scr[0:1, 0:3]
        tf = scr[0:1, 3:6]
        P.op("dve", lambda e: e.tensor_copy(sf, srctgt[0:1, 0:3]), waits=[tok_st])
        P.op("dve", lambda e: e.tensor_copy(tf, srctgt[0:1, 3:6]))
        P.op("dve", lambda e: e.tensor_sub(scr[0:1, 6:9], sf, tf))
        P.op("dve", lambda e: e.tensor_mul(
            scr[0:1, 6:9], scr[0:1, 6:9], scr[0:1, 6:9]))
        P.op("dve", lambda e: e.tensor_single_scalar(
            scr[0:1, 6:9], scr[0:1, 6:9], 2.25, ALU.is_gt))
        P.op("dve", lambda e: e.tensor_single_scalar(
            scr[0:1, 9:12], tf, 0.5, ALU.is_gt))
        P.op("dve", lambda e: e.tensor_single_scalar(
            scr[0:1, 12:15], tf, 30.5, ALU.is_lt))
        P.op("dve", lambda e: e.tensor_scalar(
            scr[0:1, 15:18], sf, 32.0, None, ALU.mult))
        P.op("dve", lambda e: e.tensor_add(scr[0:1, 15:18], scr[0:1, 15:18], tf))
        P.op("dve", lambda e: e.memset(scr[0:1, 18:21], 0.0))
        P.op("dve", lambda e: e.tensor_single_scalar(
            scr[0:1, 19:20], scr[0:1, 16:17], scr[0:1, 15:16], ALU.is_equal))
        P.op("dve", lambda e: e.tensor_single_scalar(
            scr[0:1, 20:21], scr[0:1, 17:18], scr[0:1, 15:16], ALU.is_equal))
        P.op("dve", lambda e: e.tensor_single_scalar(
            scr[0:1, 21:22], scr[0:1, 17:18], scr[0:1, 16:17], ALU.is_equal))
        P.op("dve", lambda e: e.tensor_max(
            scr[0:1, 20:21], scr[0:1, 20:21], scr[0:1, 21:22]))
        P.op("dve", lambda e: e.tensor_mul(
            scr[0:1, 22:25], scr[0:1, 6:9], scr[0:1, 9:12]))
        P.op("dve", lambda e: e.tensor_mul(
            scr[0:1, 22:25], scr[0:1, 22:25], scr[0:1, 12:15]))
        P.op("dve", lambda e: e.tensor_scalar(
            scr[0:1, 25:28], scr[0:1, 18:21], -1.0, 1.0, ALU.mult, ALU.add))
        P.op("dve", lambda e: e.tensor_mul(
            scr[0:1, 22:25], scr[0:1, 22:25], scr[0:1, 25:28]))
        P.op("dve", lambda e: e.tensor_scalar(
            scr[0:1, 28:31], scr[0:1, 22:25], -NEG, None, ALU.mult))
        tok_mu = None
        for t in range(3):
            P.op("dve", lambda e, t=t: e.tensor_single_scalar(
                mrow[0:1, t * S:(t + 1) * S], qbv, sf[0:1, t:t + 1],
                ALU.is_equal), waits=[tok_qbv])
            tok_mu = P.op("dve", lambda e, t=t: e.tensor_scalar(
                mrow[0:1, t * S:(t + 1) * S], mrow[0:1, t * S:(t + 1) * S],
                scr[0:1, 28 + t:29 + t], NEG, ALU.mult, ALU.add))
        tok_mud = P.dma("sp", lambda e: e.dma_start(
            ap_make(mu_d, 0, [[S, 3], [1, S]]),
            ap_make(mrow, 0, [mrow.ap[0], [1, 3 * S]])), waits=[tok_mu])
        tok_iok = P.dma("sp", lambda e: e.dma_start(iotak, iotak_h))
        P.op("dve", lambda e: e.tensor_single_scalar(
            col0i, srctgt[0:1, 3:6], 6, ALU.arith_shift_left), waits=[tok_st])
        P.op("dve", lambda e: e.tensor_single_scalar(
            coli2, srctgt[0:1, 3:6], 1, ALU.bitwise_and))
        P.op("dve", lambda e: e.tensor_single_scalar(
            scr3i, srctgt[0:1, 3:6], 1, ALU.arith_shift_right))
        P.op("dve", lambda e: e.tensor_copy(tgtf[0:1, 0:3], col0i))
        P.op("dve", lambda e: e.tensor_copy(tgtf[0:1, 3:6], coli2))
        P.op("dve", lambda e: e.tensor_copy(tgtf[0:1, 6:9], scr3i))
        tok_tbf = P.op("dve", lambda e: e.tensor_copy(tgtbf, tgtf))
        for t in range(3):
            P.op("pe", lambda e, t=t: e.matmul(
                psE[0:64, 0:3], onesr[0:1, 0:64],
                ap_make(tgtbf, t, [tgtbf.ap[0], [3, 3]]),
                start=True, stop=True),
                waits=[tok_tbf, tok_ones, st.get("idx_ps")], inc=True)
            tk = {"pe": P.cnt["pe"]}
            P.op("dve", lambda e: e.tensor_copy(idxf[:, 0:3], psE[0:64, 0:3]),
                 waits=[tk])
            P.op("dve", lambda e, t=t: e.tensor_copy(
                kxidx[:, t:t + 1], idxf[:, 0:1]))
            st["idx_ps"] = P.op("dve", lambda e, t=t: e.tensor_add(
                kxidx[:, t:t + 1], kxidx[:, t:t + 1], iotak[:, 0:1]),
                waits=[tok_iok])
            P.op("dve", lambda e: e.tensor_scalar(
                idxf[:, 1:2], idxf[:, 1:2], 66560.0, None, ALU.mult))
            P.op("dve", lambda e: e.tensor_scalar(
                idxf[:, 2:3], idxf[:, 2:3], 65.0, None, ALU.mult))
            P.op("dve", lambda e: e.tensor_add(
                idxf[:, 1:2], idxf[:, 1:2], idxf[:, 2:3]))
            P.op("dve", lambda e, t=t: e.tensor_copy(
                avidx[:, t:t + 1], idxf[:, 1:2]))
            st["idx_ps"] = P.op("dve", lambda e, t=t: e.tensor_add(
                avidx[:, t:t + 1], avidx[:, t:t + 1], iotak[:, 1:2]))
        jobs = []
        for mi, m in enumerate("qkvo"):
            for dc in range(8):
                jobs.append((wT_h[m][dc * 128:(dc + 1) * 128, :],
                             wbf_d[mi, dc * 128:(dc + 1) * 128, :], 1024))
        for dc in range(8):
            for c4 in range(nexp // 16):
                jobs.append((k8_h[dc * 128:(dc + 1) * 128,
                                  c4 * 2048:(c4 + 1) * 2048],
                             kd_h[dc * 128:(dc + 1) * 128,
                                  c4 * 2048:(c4 + 1) * 2048], 2048))
        for ei in range(nexp // 2):
            jobs.append((ap_make(val_h, 2 * ei * 128 * D,
                                 [[D, 256], [1, D]]),
                         ap_make(vd_h, 2 * ei * 128 * D,
                                 [[D, 256], [1, D]]), 2048))
        for ji, (srcap, dstap, ncol) in enumerate(jobs):
            par = ji % 2
            stg = xb[:, par * 2048: par * 2048 + ncol]
            bfb = a2[:, par * 2048: par * 2048 + ncol]
            if srcap.ap[0][1] == 256:
                srcap = ap_make(srcap, 0, [[2 * D, 128], [1, 2 * D]])
                dstap = ap_make(dstap, 0, [[2 * D, 128], [1, 2 * D]])
            tok_l = P.dma("sp", lambda e, srcap=srcap, stg=stg: e.dma_start(
                stg, srcap), waits=[st.get(("cc", par))])
            eng = "dve" if par == 0 else "act"
            if eng == "dve":
                tk_c = P.op(eng, lambda e, stg=stg, bfb=bfb: e.tensor_copy(
                    bfb, stg), waits=[tok_l])
            else:
                tk_c = P.op(eng, lambda e, stg=stg, bfb=bfb: e.copy(
                    bfb, stg), waits=[tok_l])
            st[("cc", par)] = P.dma("pool",
                lambda e, dstap=dstap, bfb=bfb: e.dma_start(dstap, bfb),
                waits=[tk_c])
        P.fence()

        def emit_ln_pass(gates_too):
            pst_free = {}
            pslg_free = {}
            psg_free = {}
            yev = {}
            prev = {}
            for t in range(NT):
                p = t % 2
                P.op("dve", lambda e, t=t, p=p: e.reduce_sum(S1C[p], xtile(t), AX.X))
                P.op("dve", lambda e, p=p: e.tensor_scalar(
                    S1C[p], S1C[p], 1.0 / D, None, ALU.mult))
                tk_xm = P.op("dve", lambda e, t=t, p=p: e.tensor_scalar(
                    xmb[p], xtile(t), S1C[p], None, ALU.subtract),
                    waits=[prev.get(("sq", p))])
                tk_sq = P.op("act", lambda e, p=p: e.activation(
                    ybf[p], xmb[p], AF.Square, accum_out=S2C[p]),
                    waits=[tk_xm, prev.get(("ytr", p))])
                prev[("sq", p)] = tk_sq
                tk_sd = P.op("act", lambda e, p=p: e.activation(
                    RSC[p], S2C[p], AF.Sqrt, bias=EPSC, scale=1.0 / D),
                    waits=[tk_sq, prev.get(("y", p))])
                tk_rc = P.op("dve", lambda e, p=p: e.reciprocal(RSC[p], RSC[p]),
                             waits=[tk_sd])
                tk_y = P.op("dve", lambda e, p=p: e.tensor_scalar(
                    ybf[p], xmb[p], RSC[p], None, ALU.mult),
                    waits=[tk_rc, prev.get(("ytr", p))])
                prev[("y", p)] = tk_y
                tk_pe = None
                for dc in range(8):
                    w = [tk_y, pst_free.get(t - 1), psg_free.get(t - 1),
                         tok_idn] if dc == 0 else ()
                    tk_pe = P.op("pe", lambda e, dc=dc, p=p: e.transpose(
                        psT[:, dc * 128:(dc + 1) * 128],
                        ybf[p][:, dc * 128:(dc + 1) * 128], idn),
                        waits=w, inc=(dc == 7))
                prev[("ytr", p)] = tk_pe
                yev[t] = P.op("dve", lambda e, t=t: e.tensor_copy(
                    yT[:, t * 1024:(t + 1) * 1024], psT), waits=[tk_pe])
                pst_free[t] = yev[t]
                if not gates_too:
                    continue
                lgs = psE[:, (t % 4) * 64:(t % 4) * 64 + 64]
                for dc in range(8):
                    w = [yev[t], pslg_free.get(t - 4)] if dc == 0 else ()
                    P.op("pe", lambda e, t=t, dc=dc, lgs=lgs: e.matmul(
                        lgs, yT_chunk(t, dc), selbuf[:, dc * 64:(dc + 1) * 64],
                        start=(dc == 0), stop=(dc == 7)), waits=w, inc=(dc == 7))
                tk_lg = {"pe": P.cnt["pe"]}
                tk_sig = P.op("act", lambda e, p=p, lgs=lgs: e.activation(
                    selb[p], lgs, AF.Sigmoid),
                    waits=[tk_lg, prev.get(("gmul", p))])
                pslg_free[t] = tk_sig
                P.op("dve", lambda e, p=p: e.tensor_copy(curb[p], selb[p]),
                     waits=[tk_sig])
                for _ in range(3):
                    P.op("dve", lambda e, p=p: e.reduce_max(S1C[p], curb[p], AX.X))
                    P.op("dve", lambda e, p=p: e.tensor_scalar(
                        geb[p], curb[p], S1C[p], None, ALU.is_ge))
                    P.op("dve", lambda e, p=p: e.scalar_tensor_tensor(
                        curb[p], geb[p], NEG, curb[p], ALU.mult, ALU.add))
                P.op("dve", lambda e, p=p: e.reduce_max(S1C[p], curb[p], AX.X))
                P.op("dve", lambda e, p=p: e.tensor_scalar(
                    geb[p], selb[p], S1C[p], None, ALU.is_ge))
                tk_g = P.op("dve", lambda e, p=p: e.tensor_mul(
                    gateb[p], selb[p], geb[p]), waits=[prev.get(("gtr", p))])
                prev[("gmul", p)] = tk_g
                gsl = psT[:, 0:128]
                tk_gp = P.op("pe", lambda e, p=p, gsl=gsl: e.transpose(
                    gsl[0:64, :], gateb[p], idn),
                    waits=[tk_g, psg_free.get(t - 1), yev[t]], inc=True)
                prev[("gtr", p)] = tk_gp
                psg_free[t] = P.op("dve", lambda e, t=t, gsl=gsl: e.tensor_copy(
                    gT[0:64, t * 128:(t + 1) * 128], gsl[0:64, :]),
                    waits=[tk_gp])

        def emit_ffn(b, tag):
            emit_ln_pass(gates_too=True)
            tok_pro = P.fence()
            if "yT1" in taps and tag == "ffn1" and b == 0:
                P.dma("sp", lambda e: e.dma_start(tap_h["yT1"], yT),
                      waits=[tok_pro])
            if "gT" in taps and tag == "ffn1" and b == 0:
                P.dma("sp", lambda e: e.dma_start(tap_h["gT"], gT),
                      waits=[tok_pro])
            h_ps = [psA, psB]
            cd_ps = [psC, psD]
            for eb in range(neb):
                tokw = None
                for dc in range(8):
                    tokw = P.dma("sp", lambda e, dc=dc, eb=eb: e.dma_start(
                        keysb[:, dc * 1024:(dc + 1) * 1024],
                        kd_h[dc * 128:(dc + 1) * 128,
                             eb * 1024:(eb + 1) * 1024]),
                        waits=[st.get("keys_readers")] if dc == 0 else (),
                        group=("w", tag, eb))
                for el in range(ebs):
                    tokw = P.dma("sp", lambda e, el=el, eb=eb: e.dma_start(
                        valsb[:, el * 1024:(el + 1) * 1024],
                        vd_h[eb * ebs + el, :, :]),
                        waits=[st.get("vals_readers")] if el == 0 else (),
                        group=("w", tag, eb))
                tok_w = dict(tokw)
                for tg in range(4):
                    for el in range(ebs):
                        i = tg * ebs + el
                        hp_ = h_ps[i % 2]
                        bp_ = cd_ps[i % 2]
                        gp_ = gbc[i % 2]
                        for dc in range(8):
                            w = [tok_w, st.get(("httok", i % 2))] if dc == 0 else ()
                            P.op("pe", lambda e, dc=dc, el=el, tg=tg, hp_=hp_:
                                 e.matmul(
                                hp_,
                                keysb[:, dc * 1024 + el * 128:dc * 1024 + el * 128 + 128],
                                yT_rhs(dc, tg * 512, 512),
                                start=(dc == 0), stop=(dc == 7)),
                                waits=w, inc=(dc == 7))
                        tk_h = {"pe": P.cnt["pe"]}
                        tk_bc = P.op("pe", lambda e, el=el, eb=eb, tg=tg, bp_=bp_:
                                     e.matmul(
                            bp_, ohsel(eb * ebs + el),
                            gT[0:64, tg * 512:(tg + 1) * 512],
                            start=True, stop=True),
                            waits=[tok_oh, st.get(("cdtok", i % 2))], inc=True)
                        tk_be = P.op("act", lambda e, gp_=gp_, bp_=bp_:
                                     e.copy(gp_, bp_),
                                     waits=[tk_bc, st.get(("httok", i % 2))])
                        st[("cdtok", i % 2)] = tk_be
                        st[("httok", i % 2)] = P.op(
                            "dve", lambda e, el=el, tg=tg, hp_=hp_, gp_=gp_:
                            e.scalar_tensor_tensor(
                                scT[:, el * S + tg * 512:el * S + tg * 512 + 512],
                                hp_, 0.0, gp_, ALU.max, ALU.mult),
                            waits=[tk_h, tk_be, st.get("scT_readers")])
                st["keys_readers"] = {"pe": P.cnt["pe"]}
                tok_p1 = P.tok()
                if "scT0" in taps and tag == "ffn1" and b == 0 and eb == 0:
                    P.dma("sp", lambda e: e.dma_start(tap_h["scT0"], scT),
                          waits=[tok_p1])
                for t in range(NT):
                    for oh in range(2):
                        j = t * 2 + oh
                        op_ = cd_ps[j % 2]
                        for el in range(ebs):
                            w = [tok_p1, st.get(("cdtok", j % 2))] if el == 0 else ()
                            P.op("pe", lambda e, el=el, t=t, oh=oh, op_=op_:
                                 e.matmul(
                                op_,
                                scT[:, el * S + t * 128:el * S + t * 128 + 128],
                                valsb[:, el * 1024 + oh * 512:el * 1024 + oh * 512 + 512],
                                start=(el == 0), stop=(el == ebs - 1)),
                                waits=w, inc=(el == ebs - 1))
                        tk_o = {"pe": P.cnt["pe"]}
                        st[("cdtok", j % 2)] = P.op(
                            "dve", lambda e, t=t, oh=oh, op_=op_:
                            e.scalar_tensor_tensor(
                                xtile(t, oh * 512, 512), op_, 0.5,
                                xtile(t, oh * 512, 512), ALU.mult, ALU.add),
                            waits=[tk_o])
                st["vals_readers"] = {"pe": P.cnt["pe"]}
                st["scT_readers"] = {"pe": P.cnt["pe"]}
            P.fence()

        def emit_attn(b):
            emit_ln_pass(gates_too=False)
            tok_yT = P.fence()
            tok_muq = []
            for i in range(2):
                tok_muq.append(P.dma("sp", lambda e, i=i: e.dma_start(
                    qh[i][64:67, :], mu_d), waits=[tok_yT, tok_mud]))
            for i in range(2):
                P.op("dve", lambda e, i=i: e.tensor_copy(
                    kxb[i][64:67, :], kxis), waits=[tok_kxi])
            gpair = [0]
            exp_done = {}
            psv_free = {}
            core_pe = {}
            for hp in range(nhp):
                P.fence()
                for mi, m, stg, scale in ((0, "q", qstage, 0.125),
                                          (1, "k", kstage, 1.0),
                                          (2, "v", vstage, 1.0)):
                    tok_w = None
                    for dc in range(8):
                        tok_w = P.dma("sp", lambda e, mi=mi, dc=dc: e.dma_start(
                            wbuf[:, dc * 1024:(dc + 1) * 1024],
                            wbf_d[mi, dc * 128:(dc + 1) * 128, :]),
                            waits=[st.get("wbuf_free")] if dc == 0 else (),
                            group=("whp", b, hp, m))
                    for oc in range(8):
                        half = oc % 2
                        psl = [psE, psD][half][:, 0:256]
                        P.op("pe", lambda e, m=m, oc=oc, psl=psl: e.matmul(
                            psl, BROW[m][0:1, oc * 128:(oc + 1) * 128],
                            onesr[0:1, 0:256], start=True, stop=False),
                            waits=[tok_w, tok_ones, st.get(("psEh", half))])
                        for dc in range(8):
                            P.op("pe", lambda e, dc=dc, oc=oc, hp=hp, psl=psl:
                                 e.matmul(
                                psl, wbuf[:, dc * 1024 + oc * 128:
                                          dc * 1024 + oc * 128 + 128],
                                yT_rhs(dc, hp * 256, 256),
                                start=False, stop=(dc == 7)), inc=(dc == 7))
                        tk = {"pe": P.cnt["pe"]}
                        st[("psEh", half)] = P.op(
                            "act", lambda e, stg=stg, oc=oc, psl=psl,
                            scale=scale: e.activation(
                                stg[:, oc * 256:(oc + 1) * 256], psl, AF.Copy,
                                scale=scale),
                            waits=[tk, core_pe.get(hp - 1)])
                    st["wbuf_free"] = {"pe": P.cnt["pe"]}
                tok_stages = P.tok()
                tok_wo = P.dma("sp", lambda e, hp=hp: e.dma_start(
                    wo_s, wbf_d[3, hp * 128:(hp + 1) * 128, :]),
                    waits=[st.get("oproj_pe")])
                tok_bufs_l = [None, None]
                for ph in range(2):
                    h = hp * 2 + ph
                    bw = [tok_stages, core_pe.get(hp - 1)]
                    tks = {}
                    for j in range(16):
                        oc, half = j // 2, j % 2
                        tks["q"] = P.op("act", lambda e, j=j, oc=oc, half=half,
                                        ph=ph: e.copy(
                            ap_make(qh[ph], j, [[qh[ph].ap[0][0], 64],
                                                [16, 128]]),
                            qstage[half * 64:(half + 1) * 64,
                                   oc * 256 + ph * 128:
                                   oc * 256 + ph * 128 + 128]),
                            waits=bw if j == 0 else ())
                    for nm, srcstg, dst in (("k", kstage, kh[ph]),
                                            ("v", vstage, avT)):
                        for j in range(16):
                            oc, half = j // 2, j % 2
                            tks[nm] = P.op("dve", lambda e, srcstg=srcstg,
                                           dst=dst, j=j, oc=oc, half=half,
                                           ph=ph: e.tensor_copy(
                                ap_make(dst, j, [[dst.ap[0][0], 64],
                                                 [16, 128]]),
                                srcstg[half * 64:(half + 1) * 64,
                                       oc * 256 + ph * 128:
                                       oc * 256 + ph * 128 + 128]),
                                waits=bw if j == 0 else ())
                    for w in range(16):
                        P.op("pe", lambda e, w=w: e.transpose(
                            psT[0:128, w * 64:(w + 1) * 64],
                            avT[0:64, w * 128:(w + 1) * 128], idn[0:64, 0:64]),
                            waits=[tks["v"], st.get("psT_free"), tok_idn]
                            if w == 0 else (), inc=(w == 15))
                    tk_tr = {"pe": P.cnt["pe"]}
                    tk_ms = P.op("dve", lambda e, ph=ph: e.memset(
                        ap_make(vaug[ph], 64, [vaug[ph].ap[0], [65, NT], [1, 1]]),
                        1.0), waits=[core_pe.get(hp - 1)])
                    tk_av = P.op("dve", lambda e, ph=ph: e.tensor_copy(
                        ap_make(vaug[ph], 0, [vaug[ph].ap[0], [65, NT], [1, 64]]),
                        ap_make(psT, 0, [psT.ap[0], [64, NT], [1, 64]])),
                        waits=[tk_tr, tk_ms])
                    st["psT_free"] = tk_av
                    tk_sh = P.op("dve", lambda e, ph=ph: e.tensor_copy(
                        vsh[ph], vaug[ph][64:128, :]), waits=[tk_av])
                    tok_ak = P.dma("sp", lambda e, ph=ph: e.dma_start(
                        akd_h, kh[ph][0:64, :]),
                        waits=[tks["k"], st.get("akd_free")])
                    tok_avd = P.dma("sp", lambda e, ph=ph: e.dma_start(
                        avd_h, vaug[ph]), waits=[tk_av, st.get("avd_free")])
                    gx = []
                    for t in range(3):
                        gx.append(P.dma("pool", lambda e, t=t, ph=ph:
                                        e.indirect_dma_start(
                            kxb[ph][0:64, t * 64:(t + 1) * 64], None, akd_h,
                            bass.IndirectOffsetOnAxis(kxidx[:, t:t + 1], 1)),
                            waits=[tok_ak] if t == 0 else (),
                            group=("gk", b, h)))
                    for t in range(3):
                        dst = (vxg[ph][0:64, 0:65] if t == 0 else
                               vxg[ph][64:128, 0:65] if t == 1 else
                               vxg[ph][0:64, 65:130])
                        gx.append(P.dma("pool", lambda e, t=t, dst=dst:
                                        e.indirect_dma_start(
                            dst, None, avd_h,
                            bass.IndirectOffsetOnAxis(avidx[:, t:t + 1], 1)),
                            waits=[tok_avd] if t == 0 else (),
                            group=("gk", b, h)))
                    st["akd_free"] = dict(gx[-1])
                    st["avd_free"] = dict(gx[-1])
                    tok_bufs_l[ph] = P.tok()
                for ph in range(2):
                    h = hp * 2 + ph
                    tok_bufs = tok_bufs_l[ph]
                    qb_ = qh[ph]
                    kb_ = kh[ph]
                    for pr in range(8):
                        gp = gpair[0]
                        gpair[0] += 1
                        pp = gp % 2
                        fullp = [psA, psB][pp]
                        msets = []
                        wfirst = [tok_bufs, exp_done.get(gp - 2), tok_muq[ph]]
                        wtail = [exp_done.get(gp - 1)]
                        for ql in range(2):
                            qt = pr * 2 + ql
                            q_rhs = qb_[0:64, qt * 128:(qt + 1) * 128]
                            qx_rhs = qb_[0:67, qt * 128:(qt + 1) * 128]
                            tailp = [psC, psD][ql]
                            P.op("pe", lambda e, kb_=kb_, qt=qt, fullp=fullp,
                                 ql=ql, q_rhs=q_rhs: e.matmul(
                                fullp[:, ql * 256:ql * 256 + 128],
                                kb_[0:64, qt * 128:(qt + 1) * 128], q_rhs,
                                start=True, stop=True),
                                waits=wfirst if ql == 0 else ())
                            P.op("pe", lambda e, fullp=fullp, ql=ql, ph=ph,
                                 qx_rhs=qx_rhs: e.matmul(
                                fullp[:, ql * 256 + 128:ql * 256 + 256],
                                kxb[ph][0:67, 0:128], qx_rhs,
                                start=True, stop=True))
                            if qt != 0:
                                P.op("pe", lambda e, kb_=kb_, tailp=tailp,
                                     q_rhs=q_rhs: e.matmul(
                                    tailp[0:64, 0:128], kb_[0:64, 0:64], q_rhs,
                                    start=True, stop=True), waits=wtail)
                            else:
                                msets.append(P.op("dve", lambda e, tailp=tailp:
                                    e.memset(tailp[0:64, 0:128], NEG),
                                    waits=wtail))
                            if qt != NT - 1:
                                P.op("pe", lambda e, kb_=kb_, tailp=tailp,
                                     q_rhs=q_rhs: e.matmul(
                                    tailp[0:64, 128:256],
                                    kb_[0:64, 1984:2048], q_rhs,
                                    start=True, stop=True), waits=wtail)
                            else:
                                msets.append(P.op("dve", lambda e, tailp=tailp:
                                    e.memset(tailp[0:64, 128:256], NEG),
                                    waits=wtail))
                            if qt != 0:
                                P.op("pe", lambda e, kb_=kb_, qt=qt,
                                     tailp=tailp, qb_=qb_: e.matmul(
                                    tailp[0:64, 256:320],
                                    kb_[0:64, (2 * qt - 1) * 64:(2 * qt) * 64],
                                    qb_[0:64, qt * 128:qt * 128 + 64],
                                    start=True, stop=True), waits=wtail)
                            else:
                                msets.append(P.op("dve", lambda e, tailp=tailp:
                                    e.memset(tailp[0:64, 256:320], NEG),
                                    waits=wtail))
                            if qt != NT - 1:
                                P.op("pe", lambda e, kb_=kb_, qt=qt,
                                     tailp=tailp, qb_=qb_: e.matmul(
                                    tailp[0:64, 320:384],
                                    kb_[0:64, (2 * qt + 2) * 64:
                                        (2 * qt + 3) * 64],
                                    qb_[0:64, qt * 128 + 64:qt * 128 + 128],
                                    start=True, stop=True), waits=wtail)
                            else:
                                msets.append(P.op("dve", lambda e, tailp=tailp:
                                    e.memset(tailp[0:64, 320:384], NEG),
                                    waits=wtail))
                            P.op("pe", lambda e, tailp=tailp, ph=ph,
                                 qx_rhs=qx_rhs: e.matmul(
                                tailp[0:64, 384:512],
                                kxb[ph][0:67, 128:192], qx_rhs,
                                start=True, stop=True), waits=wtail,
                                inc=(ql == 1))
                        tk_st_pe = {"pe": P.cnt["pe"]}
                        wexp = [tk_st_pe] + msets
                        P.op("act", lambda e, fullp=fullp: e.activation(
                            eful, fullp, AF.Exp), waits=wexp)
                        P.op("act", lambda e: e.activation(
                            etail[0:64, 0:512], psC[0:64, :], AF.Exp))
                        exp_done[gp] = P.op("act", lambda e: e.activation(
                            etail[0:64, 512:1024], psD[0:64, :], AF.Exp))
                        g4g = h * 4 + pr // 2
                        pvb = [psV, psVB][g4g % 2]
                        for ql in range(2):
                            qt = pr * 2 + ql
                            qv = (qt % 4) * 128
                            w0 = [exp_done[gp], psv_free.get(g4g - 2),
                                  st.get("psV_oproj") if g4g % 2 == 0
                                  else None]
                            P.op("pe", lambda e, qt=qt, qv=qv, ql=ql, ph=ph,
                                 pvb=pvb: e.matmul(
                                pvb[0:65, qv:qv + 128],
                                vaug[ph][:, qt * 65:qt * 65 + 65],
                                eful[:, ql * 256:ql * 256 + 128],
                                start=True, stop=False), waits=w0)
                            P.op("pe", lambda e, qv=qv, ql=ql, ph=ph, pvb=pvb: e.matmul(
                                pvb[0:65, qv:qv + 128],
                                vxg[ph][:, 0:65],
                                eful[:, ql * 256 + 128:ql * 256 + 256],
                                start=False, stop=False))
                            P.op("pe", lambda e, qv=qv, ql=ql, ph=ph, pvb=pvb: e.matmul(
                                pvb[0:65, qv:qv + 128],
                                vaug[ph][0:64, 0:65],
                                etail[0:64, ql * 512:ql * 512 + 128],
                                start=False, stop=False))
                            P.op("pe", lambda e, qv=qv, ql=ql, ph=ph, pvb=pvb: e.matmul(
                                pvb[0:65, qv:qv + 128],
                                vsh[ph][0:64, 15 * 65:15 * 65 + 65],
                                etail[0:64, ql * 512 + 128:ql * 512 + 256],
                                start=False, stop=False))
                            if qt != 0:
                                P.op("pe", lambda e, qt=qt, qv=qv, ql=ql,
                                     ph=ph, pvb=pvb: e.matmul(
                                    pvb[0:65, qv:qv + 64],
                                    vsh[ph][0:64, (qt - 1) * 65:
                                            (qt - 1) * 65 + 65],
                                    etail[0:64, ql * 512 + 256:
                                          ql * 512 + 320],
                                    start=False, stop=False))
                            if qt != NT - 1:
                                P.op("pe", lambda e, qt=qt, qv=qv, ql=ql,
                                     ph=ph, pvb=pvb: e.matmul(
                                    pvb[0:65, qv + 64:qv + 128],
                                    vaug[ph][0:64, (qt + 1) * 65:
                                             (qt + 1) * 65 + 65],
                                    etail[0:64, ql * 512 + 320:
                                          ql * 512 + 384],
                                    start=False, stop=False))
                            P.op("pe", lambda e, qv=qv, ql=ql, ph=ph,
                                 pvb=pvb: e.matmul(
                                pvb[0:65, qv:qv + 128],
                                vxg[ph][0:64, 65:130],
                                etail[0:64, ql * 512 + 384:ql * 512 + 512],
                                start=False, stop=True), inc=True)
                        if pr % 2 == 1:
                            g4 = pr // 2
                            tk_pv = {"pe": P.cnt["pe"]}
                            P.op("act", lambda e, pvb=pvb: e.activation(
                                lnz, pvb[64:65, 0:512], AF.Ln),
                                waits=[tk_pv, st.get("rz_reader")])
                            def rzfn(e):
                                with nc.allow_low_precision(
                                        reason="1/Z as bf16 matmul operand"):
                                    return e.activation(rz, lnz, AF.Exp,
                                                        scale=-1.0)
                            tk_rz = P.op("act", rzfn)
                            tk_zb = P.op("pe", lambda e: e.matmul(
                                psE[0:64, 0:512], onesr[0:1, 0:64], rz,
                                start=True, stop=True),
                                waits=[tk_rz, st.get("psE_free")], inc=True)
                            st["rz_reader"] = tk_zb
                            tk_ze = P.op("act", lambda e: e.copy(
                                rzb, psE[0:64, 0:512]),
                                waits=[tk_zb, st.get("rzb_free")])
                            st["psE_free"] = tk_ze
                            tk_at = P.op("dve", lambda e, g4=g4, ph=ph,
                                         pvb=pvb: e.scalar_tensor_tensor(
                                attnc[ph * 64:(ph + 1) * 64,
                                      g4 * 512:(g4 + 1) * 512],
                                pvb[0:64, 0:512], 1.0, rzb,
                                ALU.mult, ALU.mult), waits=[tk_ze])
                            st["rzb_free"] = tk_at
                            psv_free[g4g] = tk_at
                core_pe[hp] = {"pe": P.cnt["pe"]}
                tok_attnc = P.tok()
                if "attnc0" in taps and b == 0 and hp == 0:
                    P.dma("sp", lambda e: e.dma_start(tap_h["attnc0"], attnc),
                          waits=[tok_attnc])
                for t in range(NT):
                    for oh in range(2):
                        j = t * 2 + oh
                        op_ = [psE, psV][j % 2]
                        w0 = [tok_attnc, tok_wo,
                              st.get("psE_free") if j % 2 == 0 else
                              st.get("psV_oproj")]
                        if hp == 0:
                            P.op("pe", lambda e, oh=oh, op_=op_: e.matmul(
                                op_, onesr[0:1, 0:128],
                                BROW["o"][0:1, oh * 512:(oh + 1) * 512],
                                start=True, stop=False), waits=w0)
                            P.op("pe", lambda e, t=t, oh=oh, op_=op_: e.matmul(
                                op_, attnc[:, t * 128:(t + 1) * 128],
                                wo_s[:, oh * 512:(oh + 1) * 512],
                                start=False, stop=True), inc=True)
                        else:
                            P.op("pe", lambda e, t=t, oh=oh, op_=op_: e.matmul(
                                op_, attnc[:, t * 128:(t + 1) * 128],
                                wo_s[:, oh * 512:(oh + 1) * 512],
                                start=True, stop=True), waits=w0, inc=True)
                        tk_o = {"pe": P.cnt["pe"]}
                        tk_x = P.op("dve", lambda e, t=t, oh=oh, op_=op_:
                                    e.scalar_tensor_tensor(
                            xtile(t, oh * 512, 512), op_, 1.0,
                            xtile(t, oh * 512, 512), ALU.mult, ALU.add),
                            waits=[tk_o])
                        if j % 2 == 0:
                            st["psE_free"] = tk_x
                        else:
                            st["psV_oproj"] = tk_x
                st["oproj_pe"] = {"pe": P.cnt["pe"]}
            st.pop("psV_oproj", None)
            st.pop("oproj_pe", None)
            P.fence()

        for b in range(nb):
            xw = st.get("x_stored")
            for t in range(NT):
                P.dma("sp", lambda e, b=b, t=t: e.dma_start(
                    xtile(t), x_h[b, t * 128:(t + 1) * 128, :]),
                    waits=[xw] if t == 0 else (), group=("xl", b))
            P.fence()
            if do_ffn1:
                emit_ffn(b, "ffn1")
            if do_attn:
                emit_attn(b)
            if do_ffn3:
                emit_ffn(b, "ffn3")
            wtok = P.fence()
            for t in range(NT):
                P.dma("sp", lambda e, b=b, t=t: e.dma_start(
                    out_h[b, t * 128:(t + 1) * 128, :], xtile(t)),
                    waits=[wtok] if t == 0 else (), group=("xs", b))
            st["x_stored"] = P.tok()

        P._waits("sp", [P.tok()])
        P.emit(E(nc.Block()), sems)
    return nc


def make_consts():
    idn = np.eye(128, dtype=np.float32).astype(BFDT)
    ones = np.ones((1, 512), dtype=np.float32).astype(BFDT)
    qbvec = (np.arange(S, dtype=np.float32) // 64).reshape(1, S).astype(np.float32)
    kxind = np.zeros((3, 192), dtype=np.float32)
    for t in range(3):
        kxind[t, t * 64:(t + 1) * 64] = 1.0
    oh64 = np.eye(64, dtype=np.float32).astype(BFDT)
    iotak = np.stack([np.arange(64, dtype=np.int64) * S,
                      np.arange(64, dtype=np.int64) * (NT * 65)],
                     axis=1).astype(np.int32)
    return idn, ones, qbvec, kxind.astype(BFDT), oh64, iotak


def make_inputs(core_slice_x, Wq, bq, Wk, bk, Wv, bv, Wo, bo, src, tgt,
                sel, keys, values):
    k8 = np.ascontiguousarray(
        np.asarray(keys, np.float32).transpose(1, 0, 2).reshape(D, -1))
    val = np.ascontiguousarray(np.asarray(values, np.float32))
    selT = np.ascontiguousarray(np.asarray(sel, np.float32).T)
    idn, ones, qbvec, kxind, oh64, iotak = make_consts()
    return {
        "x": np.ascontiguousarray(np.asarray(core_slice_x, np.float32)),
        "k8": k8, "val": val, "selT": selT,
        "wqT": np.ascontiguousarray(np.asarray(Wq, np.float32).T),
        "wkT": np.ascontiguousarray(np.asarray(Wk, np.float32).T),
        "wvT": np.ascontiguousarray(np.asarray(Wv, np.float32).T),
        "woT": np.ascontiguousarray(np.asarray(Wo, np.float32).T),
        "biases": np.concatenate([np.asarray(v, np.float32).ravel()
                                  for v in (bq, bk, bv, bo)]).reshape(1, 4 * D),
        "srctgt": np.concatenate([np.asarray(src, np.int32).ravel(),
                                  np.asarray(tgt, np.int32).ravel()]).reshape(1, 6),
        "idn": idn, "onesrow": ones, "qbvec": qbvec, "kxind": kxind,
        "oh64": oh64, "iotak": iotak,
    }


def kernel(L0_attn_x, L0_attn_Wq, L0_attn_bq, L0_attn_Wk, L0_attn_bk,
           L0_attn_Wv, L0_attn_bv, L0_attn_Wo, L0_attn_bo,
           L0_attn_src_blocks, L0_attn_tgt_blocks,
           L0_ffn_expert_sel, L0_ffn_keys, L0_ffn_values):
    x = np.asarray(L0_attn_x, dtype=np.float32)
    nc = build_program({})
    in_maps = []
    for c in range(NCORES):
        in_maps.append(make_inputs(
            x[2 * c:2 * c + 2], L0_attn_Wq, L0_attn_bq, L0_attn_Wk, L0_attn_bk,
            L0_attn_Wv, L0_attn_bv, L0_attn_Wo, L0_attn_bo,
            L0_attn_src_blocks, L0_attn_tgt_blocks,
            L0_ffn_expert_sel, L0_ffn_keys, L0_ffn_values))
    res = run_bass_kernel_spmd(nc, in_maps, core_ids=list(range(NCORES)))
    out = np.concatenate([r["out"] for r in res.results], axis=0)
    return out.astype(np.float32)


# version 05
# speedup vs baseline: 1.1020x; 1.0315x over previous
import numpy as np
import ml_dtypes
import concourse.bass as bass
import concourse.mybir as mybir
from concourse.bass_utils import run_bass_kernel_spmd
from contextlib import ExitStack

F32 = mybir.dt.float32
BF16 = mybir.dt.bfloat16
I32 = mybir.dt.int32
AF = mybir.ActivationFunctionType
ALU = mybir.AluOpType
AX = mybir.AxisListType
BFDT = ml_dtypes.bfloat16

D = 1024
S = 2048
NT = 16
NCORES = 8
NEG = -30000.0


class Plan:
    ENGS = ("pe", "act", "dve", "pool", "sp")
    SELF = {"dve", "act", "pool"}

    def __init__(self):
        self.streams = {e: [] for e in self.ENGS}
        self.cnt = {"pe": 0, "act": 0, "dve": 0, "pool": 0, "d0": 0, "d1": 0}
        self.mark = {}
        self.groups = {}

    def _wait(self, eng, sem, val):
        if val is None or val <= 0:
            return
        k = (eng, sem)
        if self.mark.get(k, 0) >= val:
            return
        self.mark[k] = val
        self.streams[eng].append(("w", sem, val))

    def _waits(self, eng, waits):
        for w in waits:
            if not w:
                continue
            if isinstance(w, dict):
                for s_, v in w.items():
                    self._wait(eng, s_, v)
            else:
                self._wait(eng, w[0], w[1])

    def op(self, eng, fn, waits=(), inc=False):
        self._waits(eng, waits)
        incs = []
        tok = {}
        if eng in self.SELF:
            prev = self.cnt[eng]
            self._wait(eng, eng, prev)
            self.cnt[eng] = prev + 1
            incs.append((eng, 1))
            tok[eng] = prev + 1
        if inc and eng == "pe":
            self.cnt["pe"] += 1
            incs.append(("pe", 1))
            tok["pe"] = self.cnt["pe"]
        self.streams[eng].append(("o", fn, incs))
        return tok

    def dma(self, eng, fn, waits=(), sem=None, group=None):
        sem = sem or ("d0" if eng == "sp" else "d1")
        gkey = (eng, sem)
        if group is None or self.groups.get(gkey) != (sem, group):
            self._wait(eng, sem, self.cnt[sem])
            self.groups[gkey] = (sem, group) if group is not None else None
        self._waits(eng, waits)
        self.cnt[sem] += 16
        self.streams[eng].append(("o", fn, [(sem, 16)]))
        return {sem: self.cnt[sem]}

    def tok(self):
        return dict(self.cnt)

    def fence(self):
        t = self.tok()
        for e in self.ENGS:
            self._waits(e, [t])
        return t

    def emit(self, block, sems):
        def runner(name):
            def run(engine):
                for it in self.streams[name]:
                    if it[0] == "w":
                        engine.wait_ge(sems[it[1]], it[2])
                    else:
                        inst = it[1](engine)
                        for s_, n in it[2]:
                            inst.then_inc(sems[s_], n)
            return run
        block.tensor(runner("pe"))
        block.scalar(runner("act"))
        block.vector(runner("dve"))
        block.gpsimd(runner("pool"))
        block.sync(runner("sp"))


def ap_dyn(apx, reg):
    return bass.AP(apx.tensor, reg, [list(d) for d in apx.ap])


def ap_make(apx, rel, dims):
    return bass.AP(apx.tensor, apx.offset + rel,
                   [list(d) if not isinstance(d, int) else d for d in dims])


def build_program(cfg):
    nexp = cfg.get("nexp", 64)
    ebs = cfg.get("ebs", 8)
    neb = nexp // ebs
    nh = cfg.get("nh", 16)
    nhp = nh // 2
    nb = cfg.get("nb", 2)
    do_ffn1 = cfg.get("ffn1", True)
    do_attn = cfg.get("attn", True)
    do_ffn3 = cfg.get("ffn3", True)
    taps = cfg.get("taps", ())

    nc = bass.Bass("TRN2", target_bir_lowering=False)

    x_h = nc.dram_tensor("x", [nb, S, D], F32, kind="ExternalInput").ap()
    k8_h = nc.dram_tensor("k8", [D, nexp * 128], F32, kind="ExternalInput").ap()
    val_h = nc.dram_tensor("val", [nexp, 128, D], F32, kind="ExternalInput").ap()
    selT_h = nc.dram_tensor("selT", [D, 64], F32, kind="ExternalInput").ap()
    wT_h = {m: nc.dram_tensor("w%sT" % m, [D, D], F32, kind="ExternalInput").ap()
            for m in "qkvo"}
    bias_h = nc.dram_tensor("biases", [1, 4 * D], F32, kind="ExternalInput").ap()
    srctgt_h = nc.dram_tensor("srctgt", [1, 6], I32, kind="ExternalInput").ap()
    idn_h = nc.dram_tensor("idn", [128, 128], BF16, kind="ExternalInput").ap()
    ones_h = nc.dram_tensor("onesrow", [1, 512], BF16, kind="ExternalInput").ap()
    qbvec_h = nc.dram_tensor("qbvec", [1, S], F32, kind="ExternalInput").ap()
    kxind_h = nc.dram_tensor("kxind", [3, 192], BF16, kind="ExternalInput").ap()
    oh64_h = nc.dram_tensor("oh64", [64, 64], BF16, kind="ExternalInput").ap()
    iotak_h = nc.dram_tensor("iotak", [64, 2], I32, kind="ExternalInput").ap()
    out_h = nc.dram_tensor("out", [nb, S, D], F32, kind="ExternalOutput").ap()
    mu_d = nc.dram_tensor("mu_d", [3, S], BF16, kind="Internal").ap()
    akd_h = nc.dram_tensor("akd", [64, S], BF16, kind="Internal").ap()
    avd_h = nc.dram_tensor("avd", [128, NT * 65], BF16, kind="Internal").ap()
    wbf_d = nc.dram_tensor("wbf", [4, D, D], BF16, kind="Internal").ap()
    kd_h = nc.dram_tensor("kd", [D, nexp * 128], BF16, kind="Internal").ap()
    vd_h = nc.dram_tensor("vd", [nexp, 128, D], BF16, kind="Internal").ap()
    tap_h = {}
    for tname in taps:
        shape, dt = {"yT1": ([128, NT * 1024], BF16), "gT": ([64, S], BF16),
                     "scT0": ([128, ebs * S], BF16),
                     "attnc0": ([128, S], BF16)}[tname]
        tap_h[tname] = nc.dram_tensor("tap_" + tname, shape, dt,
                                      kind="ExternalOutput").ap()

    P = Plan()
    st = {}

    with ExitStack() as es:
        E = es.enter_context
        xb = E(nc.sbuf_tensor("xb", [128, NT * 1024], F32)).ap()
        yT = E(nc.sbuf_tensor("yT", [128, NT * 1024], BF16)).ap()
        a1 = E(nc.sbuf_tensor("a1", [128, 18688], BF16)).ap()
        a2 = E(nc.sbuf_tensor("a2", [128, 16384], BF16)).ap()
        gT = E(nc.sbuf_tensor("gT", [64, S], BF16)).ap()
        selbuf = E(nc.sbuf_tensor("selbuf", [128, 8 * 64], BF16)).ap()
        selb = [E(nc.sbuf_tensor("sel%d" % i, [128, 64], F32)).ap()
                for i in range(2)]
        curb = [E(nc.sbuf_tensor("cur%d" % i, [128, 64], F32)).ap()
                for i in range(2)]
        geb = [E(nc.sbuf_tensor("ge%d" % i, [128, 64], F32)).ap()
               for i in range(2)]
        gateb = [E(nc.sbuf_tensor("gate%d" % i, [128, 64], BF16)).ap()
                 for i in range(2)]
        xmb = [E(nc.sbuf_tensor("xm%d" % i, [128, 1024], F32)).ap()
               for i in range(2)]
        ybf = [E(nc.sbuf_tensor("ybf%d" % i, [128, 1024], BF16)).ap()
               for i in range(2)]
        gbc = [E(nc.sbuf_tensor("gbc%d" % i, [128, 512], BF16)).ap()
               for i in range(2)]
        col1 = E(nc.sbuf_tensor("col1", [128, 8], F32)).ap()
        idn = E(nc.sbuf_tensor("idn_s", [128, 128], BF16)).ap()
        onesr = E(nc.sbuf_tensor("onesr", [1, 512], BF16)).ap()
        oh64 = E(nc.sbuf_tensor("oh64_s", [64, 64], BF16)).ap()
        biasb = E(nc.sbuf_tensor("biasb", [1, 4 * D], BF16)).ap()
        srctgt = E(nc.sbuf_tensor("srctgt_s", [1, 6], I32)).ap()
        scr = E(nc.sbuf_tensor("scr", [1, 64], F32)).ap()
        col0i = E(nc.sbuf_tensor("col0i", [1, 3], I32)).ap()
        coli2 = E(nc.sbuf_tensor("coli2", [1, 3], I32)).ap()
        scr3i = E(nc.sbuf_tensor("scr3i", [1, 3], I32)).ap()
        tgtbf = E(nc.sbuf_tensor("tgtbf", [1, 9], BF16)).ap()
        tgtf = E(nc.sbuf_tensor("tgtf", [1, 9], F32)).ap()
        iotak = E(nc.sbuf_tensor("iotak_s", [64, 2], I32)).ap()
        kxidx = E(nc.sbuf_tensor("kxidx", [64, 3], I32)).ap()
        avidx = E(nc.sbuf_tensor("avidx", [64, 3], I32)).ap()
        idxf = E(nc.sbuf_tensor("idxf", [64, 4], F32)).ap()
        kxis = E(nc.sbuf_tensor("kxis", [3, 192], BF16)).ap()
        rz = E(nc.sbuf_tensor("rz", [1, 512], BF16)).ap()
        lnz = E(nc.sbuf_tensor("lnz", [1, 512], F32)).ap()

        scT = a1[:, 0:ebs * S]
        qh = [a1[0:67, 0:2048], a1[0:67, 2048:4096]]
        kh = [a1[0:64, 4096:6144], a1[0:64, 6144:8192]]
        avT = a1[0:64, 8192:10240]
        vaug = [a1[:, 10240:11280], a1[:, 11280:12320]]
        vsh = [a1[0:64, 12320:13360], a1[0:64, 13360:14400]]
        eful = a1[:, 16448:16960]
        etail = a1[0:64, 16960:17984]
        rzb = a1[0:64, 17984:18496]
        keysb = a2[:, 0:8 * 1024]
        valsb = a2[:, 8 * 1024:8 * 1024 + ebs * 1024]
        vstage = a2[:, 12288:13312]
        wo_sb = [a2[:, 13312:14336], a1[:, 14400:15424]]
        kxb = [a2[0:67, 14336:14528], a2[0:67, 14528:14720]]
        vxg = [a2[:, 14720:14850], a2[:, 14850:14980]]
        wtmp = [a2[:, 0:1024], a2[:, 1024:2048]]
        qbv = xb[0:1, 0:S]
        mrow = yT[0:1, 0:3 * S]

        psA = E(nc.psum_tensor("psA", [128, 512], F32)).ap()
        psB = E(nc.psum_tensor("psB", [128, 512], F32)).ap()
        psC = E(nc.psum_tensor("psC", [128, 512], F32)).ap()
        psD = E(nc.psum_tensor("psD", [128, 512], F32)).ap()
        psE = E(nc.psum_tensor("psE", [128, 512], F32)).ap()
        psV = E(nc.psum_tensor("psV", [128, 512], F32)).ap()
        psT = E(nc.psum_tensor("psT", [128, 1024], BF16)).ap()
        psVB = E(nc.psum_tensor("psVB", [128, 512], F32)).ap()

        sems = {}
        for s_ in ("d0", "d1", "pe", "act", "dve", "pool"):
            sems[s_] = E(nc.semaphore(s_))

        EPSC = col1[:, 3:4]
        S1C = [col1[:, 0:1], col1[:, 4:5]]
        S2C = [col1[:, 1:2], col1[:, 5:6]]
        RSC = [col1[:, 2:3], col1[:, 6:7]]
        BROW = {m: biasb[0:1, i * D:(i + 1) * D] for i, m in enumerate("qkvo")}

        def xtile(t, lo=0, n=1024):
            return xb[:, t * 1024 + lo: t * 1024 + lo + n]

        def yT_chunk(t, dc):
            return yT[:, t * 1024 + dc * 128: t * 1024 + dc * 128 + 128]

        def yT_rhs(dc, tok0, ntok):
            return ap_make(yT, (tok0 // 128) * 1024 + dc * 128,
                           [yT.ap[0], [1024, ntok // 128], [1, 128]])

        def ohsel(e):
            return ap_make(oh64, e, [oh64.ap[0], [0, 128]])

        tok_idn = P.dma("sp", lambda e: e.dma_start(idn, idn_h))
        tok_ones = P.dma("sp", lambda e: e.dma_start(onesr, ones_h))
        tok_oh = P.dma("sp", lambda e: e.dma_start(oh64, oh64_h))
        tok_st = P.dma("sp", lambda e: e.dma_start(srctgt, srctgt_h))
        tok_qbv = P.dma("sp", lambda e: e.dma_start(qbv, qbvec_h))
        tok_kxi = P.dma("sp", lambda e: e.dma_start(kxis, kxind_h))
        P.dma("pool", lambda e: e.dma_start(biasb, bias_h))
        for dc in range(8):
            P.dma("pool", lambda e, dc=dc: e.dma_start(
                selbuf[:, dc * 64:(dc + 1) * 64],
                selT_h[dc * 128:(dc + 1) * 128, :]), group="setup")
        P.op("dve", lambda e: e.memset(EPSC, 1e-5))

        sf = scr[0:1, 0:3]
        tf = scr[0:1, 3:6]
        P.op("dve", lambda e: e.tensor_copy(sf, srctgt[0:1, 0:3]), waits=[tok_st])
        P.op("dve", lambda e: e.tensor_copy(tf, srctgt[0:1, 3:6]))
        P.op("dve", lambda e: e.tensor_sub(scr[0:1, 6:9], sf, tf))
        P.op("dve", lambda e: e.tensor_mul(
            scr[0:1, 6:9], scr[0:1, 6:9], scr[0:1, 6:9]))
        P.op("dve", lambda e: e.tensor_single_scalar(
            scr[0:1, 6:9], scr[0:1, 6:9], 2.25, ALU.is_gt))
        P.op("dve", lambda e: e.tensor_single_scalar(
            scr[0:1, 9:12], tf, 0.5, ALU.is_gt))
        P.op("dve", lambda e: e.tensor_single_scalar(
            scr[0:1, 12:15], tf, 30.5, ALU.is_lt))
        P.op("dve", lambda e: e.tensor_scalar(
            scr[0:1, 15:18], sf, 32.0, None, ALU.mult))
        P.op("dve", lambda e: e.tensor_add(scr[0:1, 15:18], scr[0:1, 15:18], tf))
        P.op("dve", lambda e: e.memset(scr[0:1, 18:21], 0.0))
        P.op("dve", lambda e: e.tensor_single_scalar(
            scr[0:1, 19:20], scr[0:1, 16:17], scr[0:1, 15:16], ALU.is_equal))
        P.op("dve", lambda e: e.tensor_single_scalar(
            scr[0:1, 20:21], scr[0:1, 17:18], scr[0:1, 15:16], ALU.is_equal))
        P.op("dve", lambda e: e.tensor_single_scalar(
            scr[0:1, 21:22], scr[0:1, 17:18], scr[0:1, 16:17], ALU.is_equal))
        P.op("dve", lambda e: e.tensor_max(
            scr[0:1, 20:21], scr[0:1, 20:21], scr[0:1, 21:22]))
        P.op("dve", lambda e: e.tensor_mul(
            scr[0:1, 22:25], scr[0:1, 6:9], scr[0:1, 9:12]))
        P.op("dve", lambda e: e.tensor_mul(
            scr[0:1, 22:25], scr[0:1, 22:25], scr[0:1, 12:15]))
        P.op("dve", lambda e: e.tensor_scalar(
            scr[0:1, 25:28], scr[0:1, 18:21], -1.0, 1.0, ALU.mult, ALU.add))
        P.op("dve", lambda e: e.tensor_mul(
            scr[0:1, 22:25], scr[0:1, 22:25], scr[0:1, 25:28]))
        P.op("dve", lambda e: e.tensor_scalar(
            scr[0:1, 28:31], scr[0:1, 22:25], -NEG, None, ALU.mult))
        tok_mu = None
        for t in range(3):
            P.op("dve", lambda e, t=t: e.tensor_single_scalar(
                mrow[0:1, t * S:(t + 1) * S], qbv, sf[0:1, t:t + 1],
                ALU.is_equal), waits=[tok_qbv])
            tok_mu = P.op("dve", lambda e, t=t: e.tensor_scalar(
                mrow[0:1, t * S:(t + 1) * S], mrow[0:1, t * S:(t + 1) * S],
                scr[0:1, 28 + t:29 + t], NEG, ALU.mult, ALU.add))
        tok_mud = P.dma("sp", lambda e: e.dma_start(
            ap_make(mu_d, 0, [[S, 3], [1, S]]),
            ap_make(mrow, 0, [mrow.ap[0], [1, 3 * S]])), waits=[tok_mu])
        tok_iok = P.dma("sp", lambda e: e.dma_start(iotak, iotak_h))
        P.op("dve", lambda e: e.tensor_single_scalar(
            col0i, srctgt[0:1, 3:6], 6, ALU.arith_shift_left), waits=[tok_st])
        P.op("dve", lambda e: e.tensor_single_scalar(
            coli2, srctgt[0:1, 3:6], 1, ALU.bitwise_and))
        P.op("dve", lambda e: e.tensor_single_scalar(
            scr3i, srctgt[0:1, 3:6], 1, ALU.arith_shift_right))
        P.op("dve", lambda e: e.tensor_copy(tgtf[0:1, 0:3], col0i))
        P.op("dve", lambda e: e.tensor_copy(tgtf[0:1, 3:6], coli2))
        P.op("dve", lambda e: e.tensor_copy(tgtf[0:1, 6:9], scr3i))
        tok_tbf = P.op("dve", lambda e: e.tensor_copy(tgtbf, tgtf))
        for t in range(3):
            P.op("pe", lambda e, t=t: e.matmul(
                psE[0:64, 0:3], onesr[0:1, 0:64],
                ap_make(tgtbf, t, [tgtbf.ap[0], [3, 3]]),
                start=True, stop=True),
                waits=[tok_tbf, tok_ones, st.get("idx_ps")], inc=True)
            tk = {"pe": P.cnt["pe"]}
            P.op("dve", lambda e: e.tensor_copy(idxf[:, 0:3], psE[0:64, 0:3]),
                 waits=[tk])
            P.op("dve", lambda e, t=t: e.tensor_copy(
                kxidx[:, t:t + 1], idxf[:, 0:1]))
            st["idx_ps"] = P.op("dve", lambda e, t=t: e.tensor_add(
                kxidx[:, t:t + 1], kxidx[:, t:t + 1], iotak[:, 0:1]),
                waits=[tok_iok])
            P.op("dve", lambda e: e.tensor_scalar(
                idxf[:, 1:2], idxf[:, 1:2], 66560.0, None, ALU.mult))
            P.op("dve", lambda e: e.tensor_scalar(
                idxf[:, 2:3], idxf[:, 2:3], 65.0, None, ALU.mult))
            P.op("dve", lambda e: e.tensor_add(
                idxf[:, 1:2], idxf[:, 1:2], idxf[:, 2:3]))
            P.op("dve", lambda e, t=t: e.tensor_copy(
                avidx[:, t:t + 1], idxf[:, 1:2]))
            st["idx_ps"] = P.op("dve", lambda e, t=t: e.tensor_add(
                avidx[:, t:t + 1], avidx[:, t:t + 1], iotak[:, 1:2]))
        jobs = []
        for mi, m in enumerate("qkvo"):
            for dc in range(8):
                jobs.append((wT_h[m][dc * 128:(dc + 1) * 128, :],
                             wbf_d[mi, dc * 128:(dc + 1) * 128, :], 1024))
        for dc in range(8):
            for c4 in range(nexp // 16):
                jobs.append((k8_h[dc * 128:(dc + 1) * 128,
                                  c4 * 2048:(c4 + 1) * 2048],
                             kd_h[dc * 128:(dc + 1) * 128,
                                  c4 * 2048:(c4 + 1) * 2048], 2048))
        for ei in range(nexp // 2):
            jobs.append((ap_make(val_h, 2 * ei * 128 * D,
                                 [[D, 256], [1, D]]),
                         ap_make(vd_h, 2 * ei * 128 * D,
                                 [[D, 256], [1, D]]), 2048))
        for ji, (srcap, dstap, ncol) in enumerate(jobs):
            par = ji % 2
            stg = xb[:, par * 2048: par * 2048 + ncol]
            bfb = a2[:, par * 2048: par * 2048 + ncol]
            if srcap.ap[0][1] == 256:
                srcap = ap_make(srcap, 0, [[2 * D, 128], [1, 2 * D]])
                dstap = ap_make(dstap, 0, [[2 * D, 128], [1, 2 * D]])
            tok_l = P.dma("sp", lambda e, srcap=srcap, stg=stg: e.dma_start(
                stg, srcap), waits=[st.get(("cc", par))])
            eng = "dve" if par == 0 else "act"
            if eng == "dve":
                tk_c = P.op(eng, lambda e, stg=stg, bfb=bfb: e.tensor_copy(
                    bfb, stg), waits=[tok_l])
            else:
                tk_c = P.op(eng, lambda e, stg=stg, bfb=bfb: e.copy(
                    bfb, stg), waits=[tok_l])
            st[("cc", par)] = P.dma("pool",
                lambda e, dstap=dstap, bfb=bfb: e.dma_start(dstap, bfb),
                waits=[tk_c])
        P.fence()

        def emit_ln_pass(gates_too):
            pst_free = {}
            pslg_free = {}
            psg_free = {}
            yev = {}
            prev = {}
            for t in range(NT):
                p = t % 2
                P.op("dve", lambda e, t=t, p=p: e.reduce_sum(S1C[p], xtile(t), AX.X))
                P.op("dve", lambda e, p=p: e.tensor_scalar(
                    S1C[p], S1C[p], 1.0 / D, None, ALU.mult))
                tk_xm = P.op("dve", lambda e, t=t, p=p: e.tensor_scalar(
                    xmb[p], xtile(t), S1C[p], None, ALU.subtract),
                    waits=[prev.get(("sq", p))])
                tk_sq = P.op("act", lambda e, p=p: e.activation(
                    ybf[p], xmb[p], AF.Square, accum_out=S2C[p]),
                    waits=[tk_xm, prev.get(("ytr", p))])
                prev[("sq", p)] = tk_sq
                tk_sd = P.op("act", lambda e, p=p: e.activation(
                    RSC[p], S2C[p], AF.Sqrt, bias=EPSC, scale=1.0 / D),
                    waits=[tk_sq, prev.get(("y", p))])
                tk_rc = P.op("dve", lambda e, p=p: e.reciprocal(RSC[p], RSC[p]),
                             waits=[tk_sd])
                tk_y = P.op("dve", lambda e, p=p: e.tensor_scalar(
                    ybf[p], xmb[p], RSC[p], None, ALU.mult),
                    waits=[tk_rc, prev.get(("ytr", p))])
                prev[("y", p)] = tk_y
                tk_pe = None
                for dc in range(8):
                    w = [tk_y, pst_free.get(t - 1), psg_free.get(t - 1),
                         tok_idn] if dc == 0 else ()
                    tk_pe = P.op("pe", lambda e, dc=dc, p=p: e.transpose(
                        psT[:, dc * 128:(dc + 1) * 128],
                        ybf[p][:, dc * 128:(dc + 1) * 128], idn),
                        waits=w, inc=(dc == 7))
                prev[("ytr", p)] = tk_pe
                yev[t] = P.op("dve", lambda e, t=t: e.tensor_copy(
                    yT[:, t * 1024:(t + 1) * 1024], psT), waits=[tk_pe])
                pst_free[t] = yev[t]
                if not gates_too:
                    continue
                lgs = psE[:, (t % 4) * 64:(t % 4) * 64 + 64]
                for dc in range(8):
                    w = [yev[t], pslg_free.get(t - 4)] if dc == 0 else ()
                    P.op("pe", lambda e, t=t, dc=dc, lgs=lgs: e.matmul(
                        lgs, yT_chunk(t, dc), selbuf[:, dc * 64:(dc + 1) * 64],
                        start=(dc == 0), stop=(dc == 7)), waits=w, inc=(dc == 7))
                tk_lg = {"pe": P.cnt["pe"]}
                tk_sig = P.op("act", lambda e, p=p, lgs=lgs: e.activation(
                    selb[p], lgs, AF.Sigmoid),
                    waits=[tk_lg, prev.get(("gmul", p))])
                pslg_free[t] = tk_sig
                P.op("dve", lambda e, p=p: e.tensor_copy(curb[p], selb[p]),
                     waits=[tk_sig])
                for _ in range(3):
                    P.op("dve", lambda e, p=p: e.reduce_max(S1C[p], curb[p], AX.X))
                    P.op("dve", lambda e, p=p: e.tensor_scalar(
                        geb[p], curb[p], S1C[p], None, ALU.is_ge))
                    P.op("dve", lambda e, p=p: e.scalar_tensor_tensor(
                        curb[p], geb[p], NEG, curb[p], ALU.mult, ALU.add))
                P.op("dve", lambda e, p=p: e.reduce_max(S1C[p], curb[p], AX.X))
                P.op("dve", lambda e, p=p: e.tensor_scalar(
                    geb[p], selb[p], S1C[p], None, ALU.is_ge))
                tk_g = P.op("dve", lambda e, p=p: e.tensor_mul(
                    gateb[p], selb[p], geb[p]), waits=[prev.get(("gtr", p))])
                prev[("gmul", p)] = tk_g
                gsl = psT[:, 0:128]
                tk_gp = P.op("pe", lambda e, p=p, gsl=gsl: e.transpose(
                    gsl[0:64, :], gateb[p], idn),
                    waits=[tk_g, psg_free.get(t - 1), yev[t]], inc=True)
                prev[("gtr", p)] = tk_gp
                psg_free[t] = P.op("dve", lambda e, t=t, gsl=gsl: e.tensor_copy(
                    gT[0:64, t * 128:(t + 1) * 128], gsl[0:64, :]),
                    waits=[tk_gp])

        def emit_ffn(b, tag):
            emit_ln_pass(gates_too=True)
            tok_pro = P.fence()
            if "yT1" in taps and tag == "ffn1" and b == 0:
                P.dma("sp", lambda e: e.dma_start(tap_h["yT1"], yT),
                      waits=[tok_pro])
            if "gT" in taps and tag == "ffn1" and b == 0:
                P.dma("sp", lambda e: e.dma_start(tap_h["gT"], gT),
                      waits=[tok_pro])
            h_ps = [psA, psB]
            cd_ps = [psC, psD]
            for eb in range(neb):
                tokw = None
                for dc in range(8):
                    tokw = P.dma("sp", lambda e, dc=dc, eb=eb: e.dma_start(
                        keysb[:, dc * 1024:(dc + 1) * 1024],
                        kd_h[dc * 128:(dc + 1) * 128,
                             eb * 1024:(eb + 1) * 1024]),
                        waits=[st.get("keys_readers")] if dc == 0 else (),
                        group=("w", tag, eb))
                for el in range(ebs):
                    tokw = P.dma("sp", lambda e, el=el, eb=eb: e.dma_start(
                        valsb[:, el * 1024:(el + 1) * 1024],
                        vd_h[eb * ebs + el, :, :]),
                        waits=[st.get("vals_readers")] if el == 0 else (),
                        group=("w", tag, eb))
                tok_w = dict(tokw)
                for tg in range(4):
                    for el in range(ebs):
                        i = tg * ebs + el
                        hp_ = h_ps[i % 2]
                        bp_ = cd_ps[i % 2]
                        gp_ = gbc[i % 2]
                        for dc in range(8):
                            w = [tok_w, st.get(("httok", i % 2))] if dc == 0 else ()
                            P.op("pe", lambda e, dc=dc, el=el, tg=tg, hp_=hp_:
                                 e.matmul(
                                hp_,
                                keysb[:, dc * 1024 + el * 128:dc * 1024 + el * 128 + 128],
                                yT_rhs(dc, tg * 512, 512),
                                start=(dc == 0), stop=(dc == 7)),
                                waits=w, inc=(dc == 7))
                        tk_h = {"pe": P.cnt["pe"]}
                        tk_bc = P.op("pe", lambda e, el=el, eb=eb, tg=tg, bp_=bp_:
                                     e.matmul(
                            bp_, ohsel(eb * ebs + el),
                            gT[0:64, tg * 512:(tg + 1) * 512],
                            start=True, stop=True),
                            waits=[tok_oh, st.get(("cdtok", i % 2))], inc=True)
                        tk_be = P.op("act", lambda e, gp_=gp_, bp_=bp_:
                                     e.copy(gp_, bp_),
                                     waits=[tk_bc, st.get(("httok", i % 2))])
                        st[("cdtok", i % 2)] = tk_be
                        st[("httok", i % 2)] = P.op(
                            "dve", lambda e, el=el, tg=tg, hp_=hp_, gp_=gp_:
                            e.scalar_tensor_tensor(
                                scT[:, el * S + tg * 512:el * S + tg * 512 + 512],
                                hp_, 0.0, gp_, ALU.max, ALU.mult),
                            waits=[tk_h, tk_be, st.get("scT_readers")])
                st["keys_readers"] = {"pe": P.cnt["pe"]}
                tok_p1 = P.tok()
                if "scT0" in taps and tag == "ffn1" and b == 0 and eb == 0:
                    P.dma("sp", lambda e: e.dma_start(tap_h["scT0"], scT),
                          waits=[tok_p1])
                for t in range(NT):
                    for oh in range(2):
                        j = t * 2 + oh
                        op_ = cd_ps[j % 2]
                        for el in range(ebs):
                            w = [tok_p1, st.get(("cdtok", j % 2))] if el == 0 else ()
                            P.op("pe", lambda e, el=el, t=t, oh=oh, op_=op_:
                                 e.matmul(
                                op_,
                                scT[:, el * S + t * 128:el * S + t * 128 + 128],
                                valsb[:, el * 1024 + oh * 512:el * 1024 + oh * 512 + 512],
                                start=(el == 0), stop=(el == ebs - 1)),
                                waits=w, inc=(el == ebs - 1))
                        tk_o = {"pe": P.cnt["pe"]}
                        st[("cdtok", j % 2)] = P.op(
                            "dve", lambda e, t=t, oh=oh, op_=op_:
                            e.scalar_tensor_tensor(
                                xtile(t, oh * 512, 512), op_, 0.5,
                                xtile(t, oh * 512, 512), ALU.mult, ALU.add),
                            waits=[tk_o])
                st["vals_readers"] = {"pe": P.cnt["pe"]}
                st["scT_readers"] = {"pe": P.cnt["pe"]}
            P.fence()

        def emit_attn(b):
            wl = {}
            wotok = {}

            def emit_wloads(mg):
                if mg >= 3 * nhp:
                    return
                mi = mg % 3
                for dc in range(8):
                    s_ = (8 * mg + dc) % 12
                    eng = ["sp", "pool"][(8 * mg + dc) % 2]
                    wl[(mg, dc)] = P.dma(
                        eng, lambda e, mi=mi, dc=dc, s_=s_: e.dma_start(
                            a2[:, s_ * 1024:(s_ + 1) * 1024],
                            wbf_d[mi, dc * 128:(dc + 1) * 128, :]),
                        waits=[st.get(("slot_free", s_))],
                        group=("w", b, mg, eng, dc))

            emit_wloads(0)
            emit_ln_pass(gates_too=False)
            tok_yT = P.fence()
            tok_muq = []
            for i in range(2):
                tok_muq.append(P.dma("sp", lambda e, i=i: e.dma_start(
                    qh[i][64:67, :], mu_d), waits=[tok_yT, tok_mud]))
            for i in range(2):
                P.op("dve", lambda e, i=i: e.tensor_copy(
                    kxb[i][64:67, :], kxis), waits=[tok_kxi])
            gpair = [0]
            exp_done = {}
            psv_free = {}
            core_pe_ph = {}
            pctr = [0]

            def emit_proj(hp):
                for mi, m in enumerate("qkv"):
                    mg = hp * 3 + mi
                    tke = None
                    for oc in range(8):
                        bank = oc % 2
                        psl = [psE, psD][bank][:, 0:256]
                        w0 = [st.get(("proj_free", bank)), tok_ones]
                        if mi == 0 and oc == 0:
                            w0.append(st.get(("pass_free", 0)))
                        if mi == 0 and oc == 1:
                            w0.append(st.get("tails_read"))
                        P.op("pe", lambda e, m=m, oc=oc, psl=psl: e.matmul(
                            psl, BROW[m][0:1, oc * 128:(oc + 1) * 128],
                            onesr[0:1, 0:256], start=True, stop=False),
                            waits=w0)
                        for dc in range(8):
                            s_ = (8 * mg + dc) % 12
                            tkdc = P.op("pe", lambda e, dc=dc, oc=oc, hp=hp,
                                        psl=psl, s_=s_: e.matmul(
                                psl, a2[:, s_ * 1024 + oc * 128:
                                        s_ * 1024 + oc * 128 + 128],
                                yT_rhs(dc, hp * 256, 256),
                                start=False, stop=(dc == 7)),
                                waits=[wl[(mg, dc)]] if oc == 0 else (),
                                inc=(oc == 7 or dc == 7))
                            if oc == 7:
                                st[("slot_free", s_)] = tkdc
                        tk = {"pe": P.cnt["pe"]}
                        if m == "q":
                            for rh in range(2):
                                for pp in range(2):
                                    tke = P.op("act", lambda e, psl=psl, rh=rh,
                                               pp=pp, fc=2 * oc + rh:
                                               e.activation(
                                        ap_make(qh[pp], fc,
                                                [[qh[pp].ap[0][0], 64],
                                                 [16, 128]]),
                                        psl[rh * 64:(rh + 1) * 64,
                                            pp * 128:(pp + 1) * 128],
                                        AF.Copy, scale=0.125),
                                        waits=[tk] if rh + pp == 0 else ())
                        elif m == "k":
                            for rh in range(2):
                                for pp in range(2):
                                    tke = P.op("dve", lambda e, psl=psl, rh=rh,
                                               pp=pp, fc=2 * oc + rh:
                                               e.tensor_copy(
                                        ap_make(kh[pp], fc,
                                                [[kh[pp].ap[0][0], 64],
                                                 [16, 128]]),
                                        psl[rh * 64:(rh + 1) * 64,
                                            pp * 128:(pp + 1) * 128]),
                                        waits=[tk] if rh + pp == 0 else ())
                        else:
                            for rh in range(2):
                                tke = P.op("dve", lambda e, psl=psl, rh=rh,
                                           fc=2 * oc + rh: e.tensor_copy(
                                    ap_make(avT, fc, [[avT.ap[0][0], 64],
                                                      [16, 128]]),
                                    psl[rh * 64:(rh + 1) * 64, 0:128]),
                                    waits=[tk] if rh == 0 else ())
                            for rh in range(2):
                                tke = P.op("dve", lambda e, psl=psl, rh=rh,
                                           oc=oc: e.tensor_copy(
                                    vstage[rh * 64:(rh + 1) * 64,
                                           oc * 128:(oc + 1) * 128],
                                    psl[rh * 64:(rh + 1) * 64, 128:256]))
                        st[("proj_free", bank)] = tke
                    st[("m_done", m)] = tke
                    emit_wloads(mg + 1)

            def emit_preps(hp):
                bufs = [None, None]
                for ph in range(2):
                    h = hp * 2 + ph
                    if ph == 1:
                        w0 = [st["avT_free"], st[("m_done", "v")]]
                        for j in range(16):
                            tk_vsrc = P.op("dve", lambda e, j=j: e.tensor_copy(
                                ap_make(avT, j, [[avT.ap[0][0], 64],
                                                 [16, 128]]),
                                vstage[(j % 2) * 64:(j % 2) * 64 + 64,
                                       (j // 2) * 128:(j // 2) * 128 + 128]),
                                waits=w0 if j == 0 else ())
                    else:
                        tk_vsrc = st[("m_done", "v")]
                    for w in range(16):
                        P.op("pe", lambda e, w=w: e.transpose(
                            psT[0:128, w * 64:(w + 1) * 64],
                            avT[0:64, w * 128:(w + 1) * 128], idn[0:64, 0:64]),
                            waits=[tk_vsrc, st.get("psT_free"), tok_idn]
                            if w == 0 else (), inc=(w == 15))
                    tk_tr = {"pe": P.cnt["pe"]}
                    if ph == 0:
                        st["avT_free"] = tk_tr
                    tk_ms = P.op("dve", lambda e, ph=ph: e.memset(
                        ap_make(vaug[ph], 64, [vaug[ph].ap[0], [65, NT],
                                               [1, 1]]), 1.0),
                        waits=[core_pe_ph.get((hp - 1, ph))])
                    tk_av = P.op("dve", lambda e, ph=ph: e.tensor_copy(
                        ap_make(vaug[ph], 0, [vaug[ph].ap[0], [65, NT],
                                              [1, 64]]),
                        ap_make(psT, 0, [psT.ap[0], [64, NT], [1, 64]])),
                        waits=[tk_tr, tk_ms])
                    st["psT_free"] = tk_av
                    P.op("dve", lambda e, ph=ph: e.tensor_copy(
                        vsh[ph], vaug[ph][64:128, :]), waits=[tk_av])
                    tok_ak = P.dma("sp", lambda e, ph=ph: e.dma_start(
                        akd_h, kh[ph][0:64, :]),
                        waits=[st[("m_done", "k")], st.get("akd_free")])
                    tok_avd = P.dma("sp", lambda e, ph=ph: e.dma_start(
                        avd_h, vaug[ph]), waits=[tk_av, st.get("avd_free")])
                    gx = []
                    for t in range(3):
                        gx.append(P.dma("pool", lambda e, t=t, ph=ph:
                                        e.indirect_dma_start(
                            kxb[ph][0:64, t * 64:(t + 1) * 64], None, akd_h,
                            bass.IndirectOffsetOnAxis(kxidx[:, t:t + 1], 1)),
                            waits=[tok_ak] if t == 0 else (),
                            group=("gk", b, h)))
                    for t in range(3):
                        dst = (vxg[ph][0:64, 0:65] if t == 0 else
                               vxg[ph][64:128, 0:65] if t == 1 else
                               vxg[ph][0:64, 65:130])
                        gx.append(P.dma("pool", lambda e, t=t, dst=dst:
                                        e.indirect_dma_start(
                            dst, None, avd_h,
                            bass.IndirectOffsetOnAxis(avidx[:, t:t + 1], 1)),
                            waits=[tok_avd] if t == 0 else (),
                            group=("gk", b, h)))
                    st["akd_free"] = dict(gx[-1])
                    st["avd_free"] = dict(gx[-1])
                    bufs[ph] = P.tok()
                return bufs

            def emit_pass(hps, pi):
                wo_toks = [wotok[h_] for h_ in hps]
                for t in range(NT):
                    for oh in range(2):
                        j = t * 2 + oh
                        op_ = [psE, psV][j % 2]
                        w0 = [st.get(("pass_free", j % 2))]
                        if j == 0:
                            w0 += ([st["attnc_ready"],
                                    st.get(("proj_free", 0))] + wo_toks)
                        if j == 1:
                            w0 += [st.get(("pvfree", 0))]
                        if pi == 0:
                            P.op("pe", lambda e, oh=oh, op_=op_: e.matmul(
                                op_, onesr[0:1, 0:128],
                                BROW["o"][0:1, oh * 512:(oh + 1) * 512],
                                start=True, stop=False), waits=w0)
                        for ii, h_ in enumerate(hps):
                            first = (ii == 0 and pi != 0)
                            last = (ii == len(hps) - 1)
                            P.op("pe", lambda e, t=t, oh=oh, op_=op_, h_=h_,
                                 first=first, last=last: e.matmul(
                                op_, yT[:, h_ * 2048 + t * 128:
                                        h_ * 2048 + (t + 1) * 128],
                                wo_sb[h_ % 2][:, oh * 512:(oh + 1) * 512],
                                start=first, stop=last),
                                waits=w0 if first else (), inc=last)
                        tk_o = {"pe": P.cnt["pe"]}
                        tk_x = P.op("dve", lambda e, t=t, oh=oh, op_=op_:
                                    e.scalar_tensor_tensor(
                            xtile(t, oh * 512, 512), op_, 1.0,
                            xtile(t, oh * 512, 512), ALU.mult, ALU.add),
                            waits=[tk_o])
                        st[("pass_free", j % 2)] = tk_x
                        if j % 2 == 1:
                            st[("pvfree", 0)] = tk_x
                cp = {"pe": P.cnt["pe"]}
                for h_ in hps:
                    st[("wo_free", h_ % 2)] = cp

            def emit_cores(hp, bufs):
                for ph in range(2):
                    h = hp * 2 + ph
                    tok_bufs = bufs[ph]
                    qb_ = qh[ph]
                    kb_ = kh[ph]
                    for pr in range(8):
                        gp = gpair[0]
                        gpair[0] += 1
                        pp = gp % 2
                        fullp = [psA, psB][pp]
                        msets = []
                        wfirst = [tok_bufs, exp_done.get(gp - 2), tok_muq[ph]]
                        wtail = [exp_done.get(gp - 1)]
                        if ph == 0 and pr == 0:
                            wtail.append(st.get(("proj_free", 1)))
                        for ql in range(2):
                            qt = pr * 2 + ql
                            q_rhs = qb_[0:64, qt * 128:(qt + 1) * 128]
                            qx_rhs = qb_[0:67, qt * 128:(qt + 1) * 128]
                            tailp = [psC, psD][ql]
                            P.op("pe", lambda e, kb_=kb_, qt=qt, fullp=fullp,
                                 ql=ql, q_rhs=q_rhs: e.matmul(
                                fullp[:, ql * 256:ql * 256 + 128],
                                kb_[0:64, qt * 128:(qt + 1) * 128], q_rhs,
                                start=True, stop=True),
                                waits=wfirst if ql == 0 else ())
                            P.op("pe", lambda e, fullp=fullp, ql=ql, ph=ph,
                                 qx_rhs=qx_rhs: e.matmul(
                                fullp[:, ql * 256 + 128:ql * 256 + 256],
                                kxb[ph][0:67, 0:128], qx_rhs,
                                start=True, stop=True))
                            if qt != 0:
                                P.op("pe", lambda e, kb_=kb_, tailp=tailp,
                                     q_rhs=q_rhs: e.matmul(
                                    tailp[0:64, 0:128], kb_[0:64, 0:64],
                                    q_rhs, start=True, stop=True), waits=wtail)
                            else:
                                msets.append(P.op("dve", lambda e, tailp=tailp:
                                    e.memset(tailp[0:64, 0:128], NEG),
                                    waits=wtail))
                            if qt != NT - 1:
                                P.op("pe", lambda e, kb_=kb_, tailp=tailp,
                                     q_rhs=q_rhs: e.matmul(
                                    tailp[0:64, 128:256],
                                    kb_[0:64, 1984:2048], q_rhs,
                                    start=True, stop=True), waits=wtail)
                            else:
                                msets.append(P.op("dve", lambda e, tailp=tailp:
                                    e.memset(tailp[0:64, 128:256], NEG),
                                    waits=wtail))
                            if qt != 0:
                                P.op("pe", lambda e, kb_=kb_, qt=qt,
                                     tailp=tailp, qb_=qb_: e.matmul(
                                    tailp[0:64, 256:320],
                                    kb_[0:64, (2 * qt - 1) * 64:(2 * qt) * 64],
                                    qb_[0:64, qt * 128:qt * 128 + 64],
                                    start=True, stop=True), waits=wtail)
                            else:
                                msets.append(P.op("dve", lambda e, tailp=tailp:
                                    e.memset(tailp[0:64, 256:320], NEG),
                                    waits=wtail))
                            if qt != NT - 1:
                                P.op("pe", lambda e, kb_=kb_, qt=qt,
                                     tailp=tailp, qb_=qb_: e.matmul(
                                    tailp[0:64, 320:384],
                                    kb_[0:64, (2 * qt + 2) * 64:
                                        (2 * qt + 3) * 64],
                                    qb_[0:64, qt * 128 + 64:qt * 128 + 128],
                                    start=True, stop=True), waits=wtail)
                            else:
                                msets.append(P.op("dve", lambda e, tailp=tailp:
                                    e.memset(tailp[0:64, 320:384], NEG),
                                    waits=wtail))
                            P.op("pe", lambda e, tailp=tailp, ph=ph,
                                 qx_rhs=qx_rhs: e.matmul(
                                tailp[0:64, 384:512],
                                kxb[ph][0:67, 128:192], qx_rhs,
                                start=True, stop=True), waits=wtail,
                                inc=(ql == 1))
                        tk_st_pe = {"pe": P.cnt["pe"]}
                        wexp = [tk_st_pe] + msets
                        P.op("act", lambda e, fullp=fullp: e.activation(
                            eful, fullp, AF.Exp), waits=wexp)
                        P.op("act", lambda e: e.activation(
                            etail[0:64, 0:512], psC[0:64, :], AF.Exp))
                        exp_done[gp] = P.op("act", lambda e: e.activation(
                            etail[0:64, 512:1024], psD[0:64, :], AF.Exp))
                        st["tails_read"] = exp_done[gp]
                        g4g = h * 4 + pr // 2
                        pvb = [psV, psVB][g4g % 2]
                        for ql in range(2):
                            qt = pr * 2 + ql
                            qv = (qt % 4) * 128
                            w0 = [exp_done[gp], st.get(("pvfree", g4g % 2)),
                                  psv_free.get(g4g - 2)]
                            if g4g % 2 == 0:
                                w0.append(st.get(("pass_free", 1)))
                            P.op("pe", lambda e, qt=qt, qv=qv, ql=ql, ph=ph,
                                 pvb=pvb: e.matmul(
                                pvb[0:65, qv:qv + 128],
                                vaug[ph][:, qt * 65:qt * 65 + 65],
                                eful[:, ql * 256:ql * 256 + 128],
                                start=True, stop=False), waits=w0)
                            P.op("pe", lambda e, qv=qv, ql=ql, ph=ph,
                                 pvb=pvb: e.matmul(
                                pvb[0:65, qv:qv + 128],
                                vxg[ph][:, 0:65],
                                eful[:, ql * 256 + 128:ql * 256 + 256],
                                start=False, stop=False))
                            P.op("pe", lambda e, qv=qv, ql=ql, ph=ph,
                                 pvb=pvb: e.matmul(
                                pvb[0:65, qv:qv + 128],
                                vaug[ph][0:64, 0:65],
                                etail[0:64, ql * 512:ql * 512 + 128],
                                start=False, stop=False))
                            P.op("pe", lambda e, qv=qv, ql=ql, ph=ph,
                                 pvb=pvb: e.matmul(
                                pvb[0:65, qv:qv + 128],
                                vsh[ph][0:64, 15 * 65:15 * 65 + 65],
                                etail[0:64, ql * 512 + 128:ql * 512 + 256],
                                start=False, stop=False))
                            if qt != 0:
                                P.op("pe", lambda e, qt=qt, qv=qv, ql=ql,
                                     ph=ph, pvb=pvb: e.matmul(
                                    pvb[0:65, qv:qv + 64],
                                    vsh[ph][0:64, (qt - 1) * 65:
                                            (qt - 1) * 65 + 65],
                                    etail[0:64, ql * 512 + 256:
                                          ql * 512 + 320],
                                    start=False, stop=False))
                            if qt != NT - 1:
                                P.op("pe", lambda e, qt=qt, qv=qv, ql=ql,
                                     ph=ph, pvb=pvb: e.matmul(
                                    pvb[0:65, qv + 64:qv + 128],
                                    vaug[ph][0:64, (qt + 1) * 65:
                                             (qt + 1) * 65 + 65],
                                    etail[0:64, ql * 512 + 320:
                                          ql * 512 + 384],
                                    start=False, stop=False))
                            P.op("pe", lambda e, qv=qv, ql=ql, ph=ph,
                                 pvb=pvb: e.matmul(
                                pvb[0:65, qv:qv + 128],
                                vxg[ph][0:64, 65:130],
                                etail[0:64, ql * 512 + 384:ql * 512 + 512],
                                start=False, stop=True), inc=True)
                        if pr % 2 == 1:
                            g4 = pr // 2
                            tb = (g4g + 1) % 2
                            otherb = [psV, psVB][tb]
                            tk_pv = {"pe": P.cnt["pe"]}
                            P.op("act", lambda e, pvb=pvb: e.activation(
                                lnz, pvb[64:65, 0:512], AF.Ln),
                                waits=[tk_pv, st.get("rz_reader")])

                            def rzfn(e):
                                with nc.allow_low_precision(
                                        reason="1/Z as bf16 matmul operand"):
                                    return e.activation(rz, lnz, AF.Exp,
                                                        scale=-1.0)
                            tk_rz = P.op("act", rzfn)
                            zw = [tk_rz, psv_free.get(g4g - 1),
                                  st.get("rz_reader")]
                            if tb == 0:
                                zw.append(st.get(("pass_free", 1)))
                            tk_zb = P.op("pe", lambda e, otherb=otherb:
                                         e.matmul(
                                otherb[0:64, 0:512], onesr[0:1, 0:64], rz,
                                start=True, stop=True), waits=zw, inc=True)
                            st["rz_reader"] = tk_zb
                            tk_ze = P.op("act", lambda e, otherb=otherb:
                                         e.copy(rzb, otherb[0:64, 0:512]),
                                waits=[tk_zb, st.get("rzb_free")])
                            st[("pvfree", tb)] = tk_ze
                            tk_at = P.op("dve", lambda e, g4=g4, ph=ph,
                                         pvb=pvb, hp=hp:
                                         e.scalar_tensor_tensor(
                                yT[ph * 64:(ph + 1) * 64,
                                   hp * 2048 + g4 * 512:
                                   hp * 2048 + (g4 + 1) * 512],
                                pvb[0:64, 0:512], 1.0, rzb,
                                ALU.mult, ALU.mult), waits=[tk_ze])
                            st["rzb_free"] = tk_at
                            psv_free[g4g] = tk_at
                            st["attnc_ready"] = tk_at
                    core_pe_ph[(hp, ph)] = {"pe": P.cnt["pe"]}

            donep = [0]
            for hp in range(nhp):
                emit_proj(hp)
                bufs = emit_preps(hp)
                if hp - donep[0] >= 2:
                    emit_pass((donep[0], donep[0] + 1), pctr[0])
                    donep[0] += 2
                    pctr[0] += 1
                wotok[hp] = P.dma("sp", lambda e, hp=hp: e.dma_start(
                    wo_sb[hp % 2], wbf_d[3, hp * 128:(hp + 1) * 128, :]),
                    waits=[st.get(("wo_free", hp % 2))])
                emit_cores(hp, bufs)
                if "attnc0" in taps and b == 0 and hp == 0:
                    P.dma("sp", lambda e: e.dma_start(
                        tap_h["attnc0"], yT[:, 0:2048]),
                        waits=[st["attnc_ready"]])
            emit_pass(tuple(range(donep[0], nhp)), pctr[0])
            pctr[0] += 1
            P.fence()

        for b in range(nb):
            xw = st.get("x_stored")
            for t in range(NT):
                P.dma("sp", lambda e, b=b, t=t: e.dma_start(
                    xtile(t), x_h[b, t * 128:(t + 1) * 128, :]),
                    waits=[xw] if t == 0 else (), group=("xl", b))
            P.fence()
            if do_ffn1:
                emit_ffn(b, "ffn1")
            if do_attn:
                emit_attn(b)
            if do_ffn3:
                emit_ffn(b, "ffn3")
            wtok = P.fence()
            for t in range(NT):
                P.dma("sp", lambda e, b=b, t=t: e.dma_start(
                    out_h[b, t * 128:(t + 1) * 128, :], xtile(t)),
                    waits=[wtok] if t == 0 else (), group=("xs", b))
            st["x_stored"] = P.tok()

        P._waits("sp", [P.tok()])
        P.emit(E(nc.Block()), sems)
    return nc


def make_consts():
    idn = np.eye(128, dtype=np.float32).astype(BFDT)
    ones = np.ones((1, 512), dtype=np.float32).astype(BFDT)
    qbvec = (np.arange(S, dtype=np.float32) // 64).reshape(1, S).astype(np.float32)
    kxind = np.zeros((3, 192), dtype=np.float32)
    for t in range(3):
        kxind[t, t * 64:(t + 1) * 64] = 1.0
    oh64 = np.eye(64, dtype=np.float32).astype(BFDT)
    iotak = np.stack([np.arange(64, dtype=np.int64) * S,
                      np.arange(64, dtype=np.int64) * (NT * 65)],
                     axis=1).astype(np.int32)
    return idn, ones, qbvec, kxind.astype(BFDT), oh64, iotak


def make_inputs(core_slice_x, Wq, bq, Wk, bk, Wv, bv, Wo, bo, src, tgt,
                sel, keys, values):
    k8 = np.ascontiguousarray(
        np.asarray(keys, np.float32).transpose(1, 0, 2).reshape(D, -1))
    val = np.ascontiguousarray(np.asarray(values, np.float32))
    selT = np.ascontiguousarray(np.asarray(sel, np.float32).T)
    idn, ones, qbvec, kxind, oh64, iotak = make_consts()
    return {
        "x": np.ascontiguousarray(np.asarray(core_slice_x, np.float32)),
        "k8": k8, "val": val, "selT": selT,
        "wqT": np.ascontiguousarray(np.asarray(Wq, np.float32).T),
        "wkT": np.ascontiguousarray(np.asarray(Wk, np.float32).T),
        "wvT": np.ascontiguousarray(np.asarray(Wv, np.float32).T),
        "woT": np.ascontiguousarray(np.asarray(Wo, np.float32).T),
        "biases": np.concatenate([np.asarray(v, np.float32).ravel()
                                  for v in (bq, bk, bv, bo)]).reshape(1, 4 * D),
        "srctgt": np.concatenate([np.asarray(src, np.int32).ravel(),
                                  np.asarray(tgt, np.int32).ravel()]).reshape(1, 6),
        "idn": idn, "onesrow": ones, "qbvec": qbvec, "kxind": kxind,
        "oh64": oh64, "iotak": iotak,
    }


def kernel(L0_attn_x, L0_attn_Wq, L0_attn_bq, L0_attn_Wk, L0_attn_bk,
           L0_attn_Wv, L0_attn_bv, L0_attn_Wo, L0_attn_bo,
           L0_attn_src_blocks, L0_attn_tgt_blocks,
           L0_ffn_expert_sel, L0_ffn_keys, L0_ffn_values):
    x = np.asarray(L0_attn_x, dtype=np.float32)
    nc = build_program({})
    in_maps = []
    for c in range(NCORES):
        in_maps.append(make_inputs(
            x[2 * c:2 * c + 2], L0_attn_Wq, L0_attn_bq, L0_attn_Wk, L0_attn_bk,
            L0_attn_Wv, L0_attn_bv, L0_attn_Wo, L0_attn_bo,
            L0_attn_src_blocks, L0_attn_tgt_blocks,
            L0_ffn_expert_sel, L0_ffn_keys, L0_ffn_values))
    res = run_bass_kernel_spmd(nc, in_maps, core_ids=list(range(NCORES)))
    out = np.concatenate([r["out"] for r in res.results], axis=0)
    return out.astype(np.float32)


# version 06 (final)
# speedup vs baseline: 1.1427x; 1.0369x over previous
import numpy as np
import ml_dtypes
import concourse.bass as bass
import concourse.mybir as mybir
from concourse.bass_utils import run_bass_kernel_spmd
from contextlib import ExitStack

F32 = mybir.dt.float32
BF16 = mybir.dt.bfloat16
I32 = mybir.dt.int32
AF = mybir.ActivationFunctionType
ALU = mybir.AluOpType
AX = mybir.AxisListType
BFDT = ml_dtypes.bfloat16

D = 1024
S = 2048
NT = 16
NCORES = 8
NEG = -30000.0


class Plan:
    ENGS = ("pe", "act", "dve", "pool", "sp")
    SELF = {"dve", "act", "pool"}

    def __init__(self):
        self.streams = {e: [] for e in self.ENGS}
        self.cnt = {"pe": 0, "act": 0, "dve": 0, "pool": 0, "d0": 0, "d1": 0}
        self.mark = {}
        self.groups = {}

    def _wait(self, eng, sem, val):
        if val is None or val <= 0:
            return
        k = (eng, sem)
        if self.mark.get(k, 0) >= val:
            return
        self.mark[k] = val
        self.streams[eng].append(("w", sem, val))

    def _waits(self, eng, waits):
        for w in waits:
            if not w:
                continue
            if isinstance(w, dict):
                for s_, v in w.items():
                    self._wait(eng, s_, v)
            else:
                self._wait(eng, w[0], w[1])

    def op(self, eng, fn, waits=(), inc=False):
        self._waits(eng, waits)
        incs = []
        tok = {}
        if eng in self.SELF:
            prev = self.cnt[eng]
            self._wait(eng, eng, prev)
            self.cnt[eng] = prev + 1
            incs.append((eng, 1))
            tok[eng] = prev + 1
        if inc and eng == "pe":
            self.cnt["pe"] += 1
            incs.append(("pe", 1))
            tok["pe"] = self.cnt["pe"]
        self.streams[eng].append(("o", fn, incs))
        return tok

    def dma(self, eng, fn, waits=(), sem=None, group=None):
        sem = sem or ("d0" if eng == "sp" else "d1")
        gkey = (eng, sem)
        if group is None or self.groups.get(gkey) != (sem, group):
            self._wait(eng, sem, self.cnt[sem])
            self.groups[gkey] = (sem, group) if group is not None else None
        self._waits(eng, waits)
        self.cnt[sem] += 16
        self.streams[eng].append(("o", fn, [(sem, 16)]))
        return {sem: self.cnt[sem]}

    def tok(self):
        return dict(self.cnt)

    def fence(self):
        t = self.tok()
        for e in self.ENGS:
            self._waits(e, [t])
        return t

    def emit(self, block, sems):
        def runner(name):
            def run(engine):
                for it in self.streams[name]:
                    if it[0] == "w":
                        engine.wait_ge(sems[it[1]], it[2])
                    else:
                        inst = it[1](engine)
                        for s_, n in it[2]:
                            inst.then_inc(sems[s_], n)
            return run
        block.tensor(runner("pe"))
        block.scalar(runner("act"))
        block.vector(runner("dve"))
        block.gpsimd(runner("pool"))
        block.sync(runner("sp"))


def ap_dyn(apx, reg):
    return bass.AP(apx.tensor, reg, [list(d) for d in apx.ap])


def ap_make(apx, rel, dims):
    return bass.AP(apx.tensor, apx.offset + rel,
                   [list(d) if not isinstance(d, int) else d for d in dims])


def build_program(cfg):
    nexp = cfg.get("nexp", 64)
    ebs = cfg.get("ebs", 8)
    neb = nexp // ebs
    nh = cfg.get("nh", 16)
    nhp = nh // 2
    nb = cfg.get("nb", 2)
    do_ffn1 = cfg.get("ffn1", True)
    do_attn = cfg.get("attn", True)
    do_ffn3 = cfg.get("ffn3", True)
    taps = cfg.get("taps", ())

    nc = bass.Bass("TRN2", target_bir_lowering=False)

    x_h = nc.dram_tensor("x", [nb, S, D], F32, kind="ExternalInput").ap()
    k8_h = nc.dram_tensor("k8", [D, nexp * 128], F32, kind="ExternalInput").ap()
    val_h = nc.dram_tensor("val", [nexp, 128, D], F32, kind="ExternalInput").ap()
    selT_h = nc.dram_tensor("selT", [D, 64], F32, kind="ExternalInput").ap()
    wT_h = {m: nc.dram_tensor("w%sT" % m, [D, D], F32, kind="ExternalInput").ap()
            for m in "qkvo"}
    bias_h = nc.dram_tensor("biases", [1, 4 * D], F32, kind="ExternalInput").ap()
    srctgt_h = nc.dram_tensor("srctgt", [1, 6], I32, kind="ExternalInput").ap()
    idn_h = nc.dram_tensor("idn", [128, 128], BF16, kind="ExternalInput").ap()
    ones_h = nc.dram_tensor("onesrow", [1, 512], BF16, kind="ExternalInput").ap()
    qbvec_h = nc.dram_tensor("qbvec", [1, S], F32, kind="ExternalInput").ap()
    kxind_h = nc.dram_tensor("kxind", [3, 192], BF16, kind="ExternalInput").ap()
    oh64_h = nc.dram_tensor("oh64", [64, 64], BF16, kind="ExternalInput").ap()
    iotak_h = nc.dram_tensor("iotak", [64, 2], I32, kind="ExternalInput").ap()
    out_h = nc.dram_tensor("out", [nb, S, D], F32, kind="ExternalOutput").ap()
    mu_d = nc.dram_tensor("mu_d", [3, S], BF16, kind="Internal").ap()
    akd_h = nc.dram_tensor("akd", [64, S], BF16, kind="Internal").ap()
    avd_h = nc.dram_tensor("avd", [128, NT * 65], BF16, kind="Internal").ap()
    wbf_d = nc.dram_tensor("wbf", [4, D, D], BF16, kind="Internal").ap()
    kd_h = nc.dram_tensor("kd", [D, nexp * 128], BF16, kind="Internal").ap()
    vd_h = nc.dram_tensor("vd", [nexp, 128, D], BF16, kind="Internal").ap()
    tap_h = {}
    for tname in taps:
        shape, dt = {"yT1": ([128, NT * 1024], BF16), "gT": ([64, S], BF16),
                     "scT0": ([128, ebs * S], BF16),
                     "attnc0": ([128, S], BF16)}[tname]
        tap_h[tname] = nc.dram_tensor("tap_" + tname, shape, dt,
                                      kind="ExternalOutput").ap()

    P = Plan()
    st = {}

    with ExitStack() as es:
        E = es.enter_context
        xb = E(nc.sbuf_tensor("xb", [128, NT * 1024], F32)).ap()
        yT = E(nc.sbuf_tensor("yT", [128, NT * 1024], BF16)).ap()
        a1 = E(nc.sbuf_tensor("a1", [128, 18688], BF16)).ap()
        a2 = E(nc.sbuf_tensor("a2", [128, 16384], BF16)).ap()
        gT = E(nc.sbuf_tensor("gT", [64, S], BF16)).ap()
        selbuf = E(nc.sbuf_tensor("selbuf", [128, 8 * 64], BF16)).ap()
        selb = [E(nc.sbuf_tensor("sel%d" % i, [128, 64], F32)).ap()
                for i in range(2)]
        curb = [E(nc.sbuf_tensor("cur%d" % i, [128, 64], F32)).ap()
                for i in range(2)]
        geb = [E(nc.sbuf_tensor("ge%d" % i, [128, 64], F32)).ap()
               for i in range(2)]
        gateb = [E(nc.sbuf_tensor("gate%d" % i, [128, 64], BF16)).ap()
                 for i in range(2)]
        xmb = [E(nc.sbuf_tensor("xm%d" % i, [128, 1024], F32)).ap()
               for i in range(2)]
        ybf = [E(nc.sbuf_tensor("ybf%d" % i, [128, 1024], BF16)).ap()
               for i in range(2)]
        gbc = [E(nc.sbuf_tensor("gbc%d" % i, [128, 512], BF16)).ap()
               for i in range(2)]
        col1 = E(nc.sbuf_tensor("col1", [128, 8], F32)).ap()
        idn = E(nc.sbuf_tensor("idn_s", [128, 128], BF16)).ap()
        onesr = E(nc.sbuf_tensor("onesr", [1, 512], BF16)).ap()
        oh64 = E(nc.sbuf_tensor("oh64_s", [64, 64], BF16)).ap()
        biasb = E(nc.sbuf_tensor("biasb", [1, 4 * D], BF16)).ap()
        srctgt = E(nc.sbuf_tensor("srctgt_s", [1, 6], I32)).ap()
        scr = E(nc.sbuf_tensor("scr", [1, 64], F32)).ap()
        col0i = E(nc.sbuf_tensor("col0i", [1, 3], I32)).ap()
        coli2 = E(nc.sbuf_tensor("coli2", [1, 3], I32)).ap()
        scr3i = E(nc.sbuf_tensor("scr3i", [1, 3], I32)).ap()
        tgtbf = E(nc.sbuf_tensor("tgtbf", [1, 9], BF16)).ap()
        tgtf = E(nc.sbuf_tensor("tgtf", [1, 9], F32)).ap()
        iotak = E(nc.sbuf_tensor("iotak_s", [64, 2], I32)).ap()
        kxidx = E(nc.sbuf_tensor("kxidx", [64, 3], I32)).ap()
        avidx = E(nc.sbuf_tensor("avidx", [64, 3], I32)).ap()
        idxf = E(nc.sbuf_tensor("idxf", [64, 4], F32)).ap()
        kxis = E(nc.sbuf_tensor("kxis", [3, 192], BF16)).ap()
        rz = E(nc.sbuf_tensor("rz", [1, 512], BF16)).ap()
        lnz = E(nc.sbuf_tensor("lnz", [1, 512], F32)).ap()

        scT = a1[:, 0:ebs * S]
        qh = [a1[0:67, 0:2048], a1[0:67, 2048:4096]]
        kh = [a1[0:64, 4096:6144], a1[0:64, 6144:8192]]
        avT = a1[0:64, 8192:10240]
        vaug = [a1[:, 10240:11280], a1[:, 11280:12320]]
        vsh = [a1[0:64, 12320:13360], a1[0:64, 13360:14400]]
        eful = a1[:, 16448:16960]
        etail = a1[0:64, 16960:17984]
        rzb = a1[0:64, 17984:18496]
        keysb = a2[:, 0:8 * 1024]
        valsb = a2[:, 8 * 1024:8 * 1024 + ebs * 1024]
        vstage = a2[:, 12288:13312]
        wo_sb = [a2[:, 13312:14336], a1[:, 14400:15424]]
        kxb = [a2[0:67, 14336:14528], a2[0:67, 14528:14720]]
        vxg = [a2[:, 14720:14850], a2[:, 14850:14980]]
        wtmp = [a2[:, 0:1024], a2[:, 1024:2048]]
        qbv = xb[0:1, 0:S]
        mrow = yT[0:1, 0:3 * S]

        psA = E(nc.psum_tensor("psA", [128, 512], F32)).ap()
        psB = E(nc.psum_tensor("psB", [128, 512], F32)).ap()
        psC = E(nc.psum_tensor("psC", [128, 512], F32)).ap()
        psD = E(nc.psum_tensor("psD", [128, 512], F32)).ap()
        psE = E(nc.psum_tensor("psE", [128, 512], F32)).ap()
        psV = E(nc.psum_tensor("psV", [128, 512], F32)).ap()
        psT = E(nc.psum_tensor("psT", [128, 1024], BF16)).ap()
        psVB = E(nc.psum_tensor("psVB", [128, 512], F32)).ap()

        sems = {}
        for s_ in ("d0", "d1", "pe", "act", "dve", "pool"):
            sems[s_] = E(nc.semaphore(s_))

        EPSC = col1[:, 3:4]
        S1C = [col1[:, 0:1], col1[:, 4:5]]
        S2C = [col1[:, 1:2], col1[:, 5:6]]
        RSC = [col1[:, 2:3], col1[:, 6:7]]
        BROW = {m: biasb[0:1, i * D:(i + 1) * D] for i, m in enumerate("qkvo")}

        def xtile(t, lo=0, n=1024):
            return xb[:, t * 1024 + lo: t * 1024 + lo + n]

        def yT_chunk(t, dc):
            return yT[:, t * 1024 + dc * 128: t * 1024 + dc * 128 + 128]

        def yT_rhs(dc, tok0, ntok):
            return ap_make(yT, (tok0 // 128) * 1024 + dc * 128,
                           [yT.ap[0], [1024, ntok // 128], [1, 128]])

        def ohsel(e):
            return ap_make(oh64, e, [oh64.ap[0], [0, 128]])

        tok_idn = P.dma("sp", lambda e: e.dma_start(idn, idn_h))
        tok_ones = P.dma("sp", lambda e: e.dma_start(onesr, ones_h))
        tok_oh = P.dma("sp", lambda e: e.dma_start(oh64, oh64_h))
        tok_st = P.dma("sp", lambda e: e.dma_start(srctgt, srctgt_h))
        tok_qbv = P.dma("sp", lambda e: e.dma_start(qbv, qbvec_h))
        tok_kxi = P.dma("sp", lambda e: e.dma_start(kxis, kxind_h))
        P.dma("pool", lambda e: e.dma_start(biasb, bias_h))
        for dc in range(8):
            P.dma("pool", lambda e, dc=dc: e.dma_start(
                selbuf[:, dc * 64:(dc + 1) * 64],
                selT_h[dc * 128:(dc + 1) * 128, :]), group="setup")
        P.op("dve", lambda e: e.memset(EPSC, 1e-5))

        sf = scr[0:1, 0:3]
        tf = scr[0:1, 3:6]
        P.op("dve", lambda e: e.tensor_copy(sf, srctgt[0:1, 0:3]), waits=[tok_st])
        P.op("dve", lambda e: e.tensor_copy(tf, srctgt[0:1, 3:6]))
        P.op("dve", lambda e: e.tensor_sub(scr[0:1, 6:9], sf, tf))
        P.op("dve", lambda e: e.tensor_mul(
            scr[0:1, 6:9], scr[0:1, 6:9], scr[0:1, 6:9]))
        P.op("dve", lambda e: e.tensor_single_scalar(
            scr[0:1, 6:9], scr[0:1, 6:9], 2.25, ALU.is_gt))
        P.op("dve", lambda e: e.tensor_single_scalar(
            scr[0:1, 9:12], tf, 0.5, ALU.is_gt))
        P.op("dve", lambda e: e.tensor_single_scalar(
            scr[0:1, 12:15], tf, 30.5, ALU.is_lt))
        P.op("dve", lambda e: e.tensor_scalar(
            scr[0:1, 15:18], sf, 32.0, None, ALU.mult))
        P.op("dve", lambda e: e.tensor_add(scr[0:1, 15:18], scr[0:1, 15:18], tf))
        P.op("dve", lambda e: e.memset(scr[0:1, 18:21], 0.0))
        P.op("dve", lambda e: e.tensor_single_scalar(
            scr[0:1, 19:20], scr[0:1, 16:17], scr[0:1, 15:16], ALU.is_equal))
        P.op("dve", lambda e: e.tensor_single_scalar(
            scr[0:1, 20:21], scr[0:1, 17:18], scr[0:1, 15:16], ALU.is_equal))
        P.op("dve", lambda e: e.tensor_single_scalar(
            scr[0:1, 21:22], scr[0:1, 17:18], scr[0:1, 16:17], ALU.is_equal))
        P.op("dve", lambda e: e.tensor_max(
            scr[0:1, 20:21], scr[0:1, 20:21], scr[0:1, 21:22]))
        P.op("dve", lambda e: e.tensor_mul(
            scr[0:1, 22:25], scr[0:1, 6:9], scr[0:1, 9:12]))
        P.op("dve", lambda e: e.tensor_mul(
            scr[0:1, 22:25], scr[0:1, 22:25], scr[0:1, 12:15]))
        P.op("dve", lambda e: e.tensor_scalar(
            scr[0:1, 25:28], scr[0:1, 18:21], -1.0, 1.0, ALU.mult, ALU.add))
        P.op("dve", lambda e: e.tensor_mul(
            scr[0:1, 22:25], scr[0:1, 22:25], scr[0:1, 25:28]))
        P.op("dve", lambda e: e.tensor_scalar(
            scr[0:1, 28:31], scr[0:1, 22:25], -NEG, None, ALU.mult))
        tok_mu = None
        for t in range(3):
            P.op("dve", lambda e, t=t: e.tensor_single_scalar(
                mrow[0:1, t * S:(t + 1) * S], qbv, sf[0:1, t:t + 1],
                ALU.is_equal), waits=[tok_qbv])
            tok_mu = P.op("dve", lambda e, t=t: e.tensor_scalar(
                mrow[0:1, t * S:(t + 1) * S], mrow[0:1, t * S:(t + 1) * S],
                scr[0:1, 28 + t:29 + t], NEG, ALU.mult, ALU.add))
        tok_mud = P.dma("sp", lambda e: e.dma_start(
            ap_make(mu_d, 0, [[S, 3], [1, S]]),
            ap_make(mrow, 0, [mrow.ap[0], [1, 3 * S]])), waits=[tok_mu])
        tok_iok = P.dma("sp", lambda e: e.dma_start(iotak, iotak_h))
        P.op("dve", lambda e: e.tensor_single_scalar(
            col0i, srctgt[0:1, 3:6], 6, ALU.arith_shift_left), waits=[tok_st])
        P.op("dve", lambda e: e.tensor_single_scalar(
            coli2, srctgt[0:1, 3:6], 1, ALU.bitwise_and))
        P.op("dve", lambda e: e.tensor_single_scalar(
            scr3i, srctgt[0:1, 3:6], 1, ALU.arith_shift_right))
        P.op("dve", lambda e: e.tensor_copy(tgtf[0:1, 0:3], col0i))
        P.op("dve", lambda e: e.tensor_copy(tgtf[0:1, 3:6], coli2))
        P.op("dve", lambda e: e.tensor_copy(tgtf[0:1, 6:9], scr3i))
        tok_tbf = P.op("dve", lambda e: e.tensor_copy(tgtbf, tgtf))
        for t in range(3):
            P.op("pe", lambda e, t=t: e.matmul(
                psE[0:64, 0:3], onesr[0:1, 0:64],
                ap_make(tgtbf, t, [tgtbf.ap[0], [3, 3]]),
                start=True, stop=True),
                waits=[tok_tbf, tok_ones, st.get("idx_ps")], inc=True)
            tk = {"pe": P.cnt["pe"]}
            P.op("dve", lambda e: e.tensor_copy(idxf[:, 0:3], psE[0:64, 0:3]),
                 waits=[tk])
            P.op("dve", lambda e, t=t: e.tensor_copy(
                kxidx[:, t:t + 1], idxf[:, 0:1]))
            st["idx_ps"] = P.op("dve", lambda e, t=t: e.tensor_add(
                kxidx[:, t:t + 1], kxidx[:, t:t + 1], iotak[:, 0:1]),
                waits=[tok_iok])
            P.op("dve", lambda e: e.tensor_scalar(
                idxf[:, 1:2], idxf[:, 1:2], 66560.0, None, ALU.mult))
            P.op("dve", lambda e: e.tensor_scalar(
                idxf[:, 2:3], idxf[:, 2:3], 65.0, None, ALU.mult))
            P.op("dve", lambda e: e.tensor_add(
                idxf[:, 1:2], idxf[:, 1:2], idxf[:, 2:3]))
            P.op("dve", lambda e, t=t: e.tensor_copy(
                avidx[:, t:t + 1], idxf[:, 1:2]))
            st["idx_ps"] = P.op("dve", lambda e, t=t: e.tensor_add(
                avidx[:, t:t + 1], avidx[:, t:t + 1], iotak[:, 1:2]))
        jobs = []
        for mi, m in enumerate("qkvo"):
            for dc in range(8):
                jobs.append((wT_h[m][dc * 128:(dc + 1) * 128, :],
                             wbf_d[mi, dc * 128:(dc + 1) * 128, :], 1024))
        for dc in range(8):
            for c4 in range(nexp // 16):
                jobs.append((k8_h[dc * 128:(dc + 1) * 128,
                                  c4 * 2048:(c4 + 1) * 2048],
                             kd_h[dc * 128:(dc + 1) * 128,
                                  c4 * 2048:(c4 + 1) * 2048], 2048))
        for ei in range(nexp // 2):
            jobs.append((ap_make(val_h, 2 * ei * 128 * D,
                                 [[D, 256], [1, D]]),
                         ap_make(vd_h, 2 * ei * 128 * D,
                                 [[D, 256], [1, D]]), 2048))
        ctk = {}
        stk = {}
        nwv = (len(jobs) + 3) // 4
        for w in range(nwv):
            wj = jobs[4 * w:4 * w + 4]
            lt = {}
            for i, (srcap, dstap, ncol) in enumerate(wj):
                sl = (w % 2) * 4 + i
                if srcap.ap[0][1] == 256:
                    srcap = ap_make(srcap, 0, [[2 * D, 128], [1, 2 * D]])
                eng = "sp" if i < 2 else "pool"
                lt[eng] = P.dma(eng, lambda e, srcap=srcap, sl=sl, ncol=ncol:
                                e.dma_start(
                    xb[:, sl * 2048:sl * 2048 + ncol], srcap),
                    waits=[ctk.get(4 * (w - 2) + i)], group=("cl", w, eng))
            for i, (srcap, dstap, ncol) in enumerate(wj):
                ji = 4 * w + i
                sl = (w % 2) * 4 + i
                stg = xb[:, sl * 2048:sl * 2048 + ncol]
                bfb = a2[:, sl * 2048:sl * 2048 + ncol]
                weng = "pool" if i < 2 else "sp"
                cw = [lt["sp" if i < 2 else "pool"], stk.get((w - 2, weng))]
                if i % 2 == 0:
                    ctk[ji] = P.op("dve", lambda e, stg=stg, bfb=bfb:
                                   e.tensor_copy(bfb, stg), waits=cw)
                else:
                    ctk[ji] = P.op("act", lambda e, stg=stg, bfb=bfb:
                                   e.copy(bfb, stg), waits=cw)
            for i, (srcap, dstap, ncol) in enumerate(wj):
                ji = 4 * w + i
                sl = (w % 2) * 4 + i
                bfb = a2[:, sl * 2048:sl * 2048 + ncol]
                if dstap.ap[0][1] == 256:
                    dstap = ap_make(dstap, 0, [[2 * D, 128], [1, 2 * D]])
                weng = "pool" if i < 2 else "sp"
                stk[(w, weng)] = P.dma(weng, lambda e, dstap=dstap, bfb=bfb:
                                       e.dma_start(dstap, bfb),
                                       waits=[ctk[ji]], group=("cs", w, weng))
        P.fence()

        def emit_ln_pass(gates_too):
            pst_free = {}
            pslg_free = {}
            psg_free = {}
            yev = {}
            prev = {}
            for t in range(NT):
                p = t % 2
                P.op("dve", lambda e, t=t, p=p: e.reduce_sum(S1C[p], xtile(t), AX.X))
                P.op("dve", lambda e, p=p: e.tensor_scalar(
                    S1C[p], S1C[p], 1.0 / D, None, ALU.mult))
                tk_xm = P.op("dve", lambda e, t=t, p=p: e.tensor_scalar(
                    xmb[p], xtile(t), S1C[p], None, ALU.subtract),
                    waits=[prev.get(("sq", p))])
                tk_sq = P.op("act", lambda e, p=p: e.activation(
                    ybf[p], xmb[p], AF.Square, accum_out=S2C[p]),
                    waits=[tk_xm, prev.get(("ytr", p))])
                prev[("sq", p)] = tk_sq
                tk_sd = P.op("act", lambda e, p=p: e.activation(
                    RSC[p], S2C[p], AF.Sqrt, bias=EPSC, scale=1.0 / D),
                    waits=[tk_sq, prev.get(("y", p))])
                tk_rc = P.op("dve", lambda e, p=p: e.reciprocal(RSC[p], RSC[p]),
                             waits=[tk_sd])
                tk_y = P.op("dve", lambda e, p=p: e.tensor_scalar(
                    ybf[p], xmb[p], RSC[p], None, ALU.mult),
                    waits=[tk_rc, prev.get(("ytr", p))])
                prev[("y", p)] = tk_y
                tk_pe = None
                for dc in range(8):
                    w = [tk_y, pst_free.get(t - 1), psg_free.get(t - 1),
                         tok_idn] if dc == 0 else ()
                    tk_pe = P.op("pe", lambda e, dc=dc, p=p: e.transpose(
                        psT[:, dc * 128:(dc + 1) * 128],
                        ybf[p][:, dc * 128:(dc + 1) * 128], idn),
                        waits=w, inc=(dc == 7))
                prev[("ytr", p)] = tk_pe
                yev[t] = P.op("dve", lambda e, t=t: e.tensor_copy(
                    yT[:, t * 1024:(t + 1) * 1024], psT), waits=[tk_pe])
                pst_free[t] = yev[t]
                if not gates_too:
                    continue
                lgs = psE[:, (t % 4) * 64:(t % 4) * 64 + 64]
                for dc in range(8):
                    w = [yev[t], pslg_free.get(t - 4)] if dc == 0 else ()
                    P.op("pe", lambda e, t=t, dc=dc, lgs=lgs: e.matmul(
                        lgs, yT_chunk(t, dc), selbuf[:, dc * 64:(dc + 1) * 64],
                        start=(dc == 0), stop=(dc == 7)), waits=w, inc=(dc == 7))
                tk_lg = {"pe": P.cnt["pe"]}
                tk_sig = P.op("act", lambda e, p=p, lgs=lgs: e.activation(
                    selb[p], lgs, AF.Sigmoid),
                    waits=[tk_lg, prev.get(("gmul", p))])
                pslg_free[t] = tk_sig
                P.op("dve", lambda e, p=p: e.tensor_copy(curb[p], selb[p]),
                     waits=[tk_sig])
                for _ in range(3):
                    P.op("dve", lambda e, p=p: e.reduce_max(S1C[p], curb[p], AX.X))
                    P.op("dve", lambda e, p=p: e.tensor_scalar(
                        geb[p], curb[p], S1C[p], None, ALU.is_ge))
                    P.op("dve", lambda e, p=p: e.scalar_tensor_tensor(
                        curb[p], geb[p], NEG, curb[p], ALU.mult, ALU.add))
                P.op("dve", lambda e, p=p: e.reduce_max(S1C[p], curb[p], AX.X))
                P.op("dve", lambda e, p=p: e.tensor_scalar(
                    geb[p], selb[p], S1C[p], None, ALU.is_ge))
                tk_g = P.op("dve", lambda e, p=p: e.tensor_mul(
                    gateb[p], selb[p], geb[p]), waits=[prev.get(("gtr", p))])
                prev[("gmul", p)] = tk_g
                gsl = psT[:, 0:128]
                tk_gp = P.op("pe", lambda e, p=p, gsl=gsl: e.transpose(
                    gsl[0:64, :], gateb[p], idn),
                    waits=[tk_g, psg_free.get(t - 1), yev[t]], inc=True)
                prev[("gtr", p)] = tk_gp
                psg_free[t] = P.op("dve", lambda e, t=t, gsl=gsl: e.tensor_copy(
                    gT[0:64, t * 128:(t + 1) * 128], gsl[0:64, :]),
                    waits=[tk_gp])

        def emit_ffn(b, tag):
            emit_ln_pass(gates_too=True)
            tok_pro = P.fence()
            if "yT1" in taps and tag == "ffn1" and b == 0:
                P.dma("sp", lambda e: e.dma_start(tap_h["yT1"], yT),
                      waits=[tok_pro])
            if "gT" in taps and tag == "ffn1" and b == 0:
                P.dma("sp", lambda e: e.dma_start(tap_h["gT"], gT),
                      waits=[tok_pro])
            h_ps = [psA, psB]
            cd_ps = [psC, psD]
            for eb in range(neb):
                tokw = None
                for dc in range(8):
                    tokw = P.dma("sp", lambda e, dc=dc, eb=eb: e.dma_start(
                        keysb[:, dc * 1024:(dc + 1) * 1024],
                        kd_h[dc * 128:(dc + 1) * 128,
                             eb * 1024:(eb + 1) * 1024]),
                        waits=[st.get("keys_readers")] if dc == 0 else (),
                        group=("w", tag, eb))
                for el in range(ebs):
                    tokw = P.dma("sp", lambda e, el=el, eb=eb: e.dma_start(
                        valsb[:, el * 1024:(el + 1) * 1024],
                        vd_h[eb * ebs + el, :, :]),
                        waits=[st.get("vals_readers")] if el == 0 else (),
                        group=("w", tag, eb))
                tok_w = dict(tokw)
                for tg in range(4):
                    for el in range(ebs):
                        i = tg * ebs + el
                        hp_ = h_ps[i % 2]
                        bp_ = cd_ps[i % 2]
                        gp_ = gbc[i % 2]
                        for dc in range(8):
                            w = [tok_w, st.get(("httok", i % 2))] if dc == 0 else ()
                            P.op("pe", lambda e, dc=dc, el=el, tg=tg, hp_=hp_:
                                 e.matmul(
                                hp_,
                                keysb[:, dc * 1024 + el * 128:dc * 1024 + el * 128 + 128],
                                yT_rhs(dc, tg * 512, 512),
                                start=(dc == 0), stop=(dc == 7)),
                                waits=w, inc=(dc == 7))
                        tk_h = {"pe": P.cnt["pe"]}
                        tk_bc = P.op("pe", lambda e, el=el, eb=eb, tg=tg, bp_=bp_:
                                     e.matmul(
                            bp_, ohsel(eb * ebs + el),
                            gT[0:64, tg * 512:(tg + 1) * 512],
                            start=True, stop=True),
                            waits=[tok_oh, st.get(("cdtok", i % 2))], inc=True)
                        tk_be = P.op("act", lambda e, gp_=gp_, bp_=bp_:
                                     e.copy(gp_, bp_),
                                     waits=[tk_bc, st.get(("httok", i % 2))])
                        st[("cdtok", i % 2)] = tk_be
                        st[("httok", i % 2)] = P.op(
                            "dve", lambda e, el=el, tg=tg, hp_=hp_, gp_=gp_:
                            e.scalar_tensor_tensor(
                                scT[:, el * S + tg * 512:el * S + tg * 512 + 512],
                                hp_, 0.0, gp_, ALU.max, ALU.mult),
                            waits=[tk_h, tk_be, st.get("scT_readers")])
                st["keys_readers"] = {"pe": P.cnt["pe"]}
                tok_p1 = P.tok()
                if "scT0" in taps and tag == "ffn1" and b == 0 and eb == 0:
                    P.dma("sp", lambda e: e.dma_start(tap_h["scT0"], scT),
                          waits=[tok_p1])
                for t in range(NT):
                    for oh in range(2):
                        j = t * 2 + oh
                        op_ = cd_ps[j % 2]
                        for el in range(ebs):
                            w = [tok_p1, st.get(("cdtok", j % 2))] if el == 0 else ()
                            P.op("pe", lambda e, el=el, t=t, oh=oh, op_=op_:
                                 e.matmul(
                                op_,
                                scT[:, el * S + t * 128:el * S + t * 128 + 128],
                                valsb[:, el * 1024 + oh * 512:el * 1024 + oh * 512 + 512],
                                start=(el == 0), stop=(el == ebs - 1)),
                                waits=w, inc=(el == ebs - 1))
                        tk_o = {"pe": P.cnt["pe"]}
                        st[("cdtok", j % 2)] = P.op(
                            "dve", lambda e, t=t, oh=oh, op_=op_:
                            e.scalar_tensor_tensor(
                                xtile(t, oh * 512, 512), op_, 0.5,
                                xtile(t, oh * 512, 512), ALU.mult, ALU.add),
                            waits=[tk_o])
                st["vals_readers"] = {"pe": P.cnt["pe"]}
                st["scT_readers"] = {"pe": P.cnt["pe"]}
            P.fence()

        def emit_attn(b):
            wl = {}
            wotok = {}

            def emit_wloads(mg):
                if mg >= 3 * nhp:
                    return
                mi = mg % 3
                for dc in range(8):
                    s_ = (8 * mg + dc) % 12
                    eng = ["sp", "pool"][(8 * mg + dc) % 2]
                    wl[(mg, dc)] = P.dma(
                        eng, lambda e, mi=mi, dc=dc, s_=s_: e.dma_start(
                            a2[:, s_ * 1024:(s_ + 1) * 1024],
                            wbf_d[mi, dc * 128:(dc + 1) * 128, :]),
                        waits=[st.get(("slot_free", s_))],
                        group=("w", b, mg, eng, dc))

            emit_wloads(0)
            emit_ln_pass(gates_too=False)
            tok_yT = P.fence()
            tok_muq = []
            for i in range(2):
                tok_muq.append(P.dma("sp", lambda e, i=i: e.dma_start(
                    qh[i][64:67, :], mu_d), waits=[tok_yT, tok_mud]))
            for i in range(2):
                P.op("dve", lambda e, i=i: e.tensor_copy(
                    kxb[i][64:67, :], kxis), waits=[tok_kxi])
            gpair = [0]
            exp_done = {}
            exp_tails = {}
            psv_free = {}
            core_pe_ph = {}
            pctr = [0]

            def emit_proj(hp):
                for mi, m in enumerate("qkv"):
                    mg = hp * 3 + mi
                    tke = None
                    for oc in range(8):
                        bank = oc % 2
                        psl = [psE, psD][bank][:, 0:256]
                        w0 = [st.get(("proj_free", bank)), tok_ones]
                        if mi == 0 and oc == 0:
                            w0.append(st.get(("pass_free", 0)))
                        if mi == 0 and oc == 1:
                            w0.append(st.get("tails_read"))
                        P.op("pe", lambda e, m=m, oc=oc, psl=psl: e.matmul(
                            psl, BROW[m][0:1, oc * 128:(oc + 1) * 128],
                            onesr[0:1, 0:256], start=True, stop=False),
                            waits=w0)
                        for dc in range(8):
                            s_ = (8 * mg + dc) % 12
                            tkdc = P.op("pe", lambda e, dc=dc, oc=oc, hp=hp,
                                        psl=psl, s_=s_: e.matmul(
                                psl, a2[:, s_ * 1024 + oc * 128:
                                        s_ * 1024 + oc * 128 + 128],
                                yT_rhs(dc, hp * 256, 256),
                                start=False, stop=(dc == 7)),
                                waits=[wl[(mg, dc)]] if oc == 0 else (),
                                inc=(oc == 7 or dc == 7))
                            if oc == 7:
                                st[("slot_free", s_)] = tkdc
                        tk = {"pe": P.cnt["pe"]}
                        if m == "q":
                            for rh in range(2):
                                for pp in range(2):
                                    tke = P.op("dve", lambda e, psl=psl, rh=rh,
                                               pp=pp, fc=2 * oc + rh:
                                               e.tensor_scalar(
                                        ap_make(qh[pp], fc,
                                                [[qh[pp].ap[0][0], 64],
                                                 [16, 128]]),
                                        psl[rh * 64:(rh + 1) * 64,
                                            pp * 128:(pp + 1) * 128],
                                        0.125, None, ALU.mult),
                                        waits=[tk] if rh + pp == 0 else ())
                        elif m == "k":
                            for rh in range(2):
                                for pp in range(2):
                                    tke = P.op("dve", lambda e, psl=psl, rh=rh,
                                               pp=pp, fc=2 * oc + rh:
                                               e.tensor_copy(
                                        ap_make(kh[pp], fc,
                                                [[kh[pp].ap[0][0], 64],
                                                 [16, 128]]),
                                        psl[rh * 64:(rh + 1) * 64,
                                            pp * 128:(pp + 1) * 128]),
                                        waits=[tk] if rh + pp == 0 else ())
                        else:
                            for rh in range(2):
                                tke = P.op("dve", lambda e, psl=psl, rh=rh,
                                           fc=2 * oc + rh: e.tensor_copy(
                                    ap_make(avT, fc, [[avT.ap[0][0], 64],
                                                      [16, 128]]),
                                    psl[rh * 64:(rh + 1) * 64, 0:128]),
                                    waits=[tk] if rh == 0 else ())
                            for rh in range(2):
                                tke = P.op("dve", lambda e, psl=psl, rh=rh,
                                           oc=oc: e.tensor_copy(
                                    vstage[rh * 64:(rh + 1) * 64,
                                           oc * 128:(oc + 1) * 128],
                                    psl[rh * 64:(rh + 1) * 64, 128:256]))
                        st[("proj_free", bank)] = tke
                    st[("m_done", m)] = tke
                    emit_wloads(mg + 1)

            def emit_preps(hp):
                bufs = [None, None]
                for ph in range(2):
                    h = hp * 2 + ph
                    if ph == 1:
                        w0 = [st["avT_free"], st[("m_done", "v")]]
                        for j in range(16):
                            tk_vsrc = P.op("dve", lambda e, j=j: e.tensor_copy(
                                ap_make(avT, j, [[avT.ap[0][0], 64],
                                                 [16, 128]]),
                                vstage[(j % 2) * 64:(j % 2) * 64 + 64,
                                       (j // 2) * 128:(j // 2) * 128 + 128]),
                                waits=w0 if j == 0 else ())
                    else:
                        tk_vsrc = st[("m_done", "v")]
                    for w in range(16):
                        P.op("pe", lambda e, w=w: e.transpose(
                            psT[0:128, w * 64:(w + 1) * 64],
                            avT[0:64, w * 128:(w + 1) * 128], idn[0:64, 0:64]),
                            waits=[tk_vsrc, st.get("psT_free"), tok_idn]
                            if w == 0 else (), inc=(w == 15))
                    tk_tr = {"pe": P.cnt["pe"]}
                    if ph == 0:
                        st["avT_free"] = tk_tr
                    tk_ms = P.op("dve", lambda e, ph=ph: e.memset(
                        ap_make(vaug[ph], 64, [vaug[ph].ap[0], [65, NT],
                                               [1, 1]]), 1.0),
                        waits=[core_pe_ph.get((hp - 1, ph))])
                    tk_av = P.op("dve", lambda e, ph=ph: e.tensor_copy(
                        ap_make(vaug[ph], 0, [vaug[ph].ap[0], [65, NT],
                                              [1, 64]]),
                        ap_make(psT, 0, [psT.ap[0], [64, NT], [1, 64]])),
                        waits=[tk_tr, tk_ms])
                    st["psT_free"] = tk_av
                    P.op("dve", lambda e, ph=ph: e.tensor_copy(
                        vsh[ph], vaug[ph][64:128, :]), waits=[tk_av])
                    tok_ak = P.dma("sp", lambda e, ph=ph: e.dma_start(
                        akd_h, kh[ph][0:64, :]),
                        waits=[st[("m_done", "k")], st.get("akd_free")])
                    tok_avd = P.dma("sp", lambda e, ph=ph: e.dma_start(
                        avd_h, vaug[ph]), waits=[tk_av, st.get("avd_free")])
                    gx = []
                    for t in range(3):
                        gx.append(P.dma("pool", lambda e, t=t, ph=ph:
                                        e.indirect_dma_start(
                            kxb[ph][0:64, t * 64:(t + 1) * 64], None, akd_h,
                            bass.IndirectOffsetOnAxis(kxidx[:, t:t + 1], 1)),
                            waits=[tok_ak] if t == 0 else (),
                            group=("gk", b, h)))
                    for t in range(3):
                        dst = (vxg[ph][0:64, 0:65] if t == 0 else
                               vxg[ph][64:128, 0:65] if t == 1 else
                               vxg[ph][0:64, 65:130])
                        gx.append(P.dma("pool", lambda e, t=t, dst=dst:
                                        e.indirect_dma_start(
                            dst, None, avd_h,
                            bass.IndirectOffsetOnAxis(avidx[:, t:t + 1], 1)),
                            waits=[tok_avd] if t == 0 else (),
                            group=("gk", b, h)))
                    st["akd_free"] = dict(gx[-1])
                    st["avd_free"] = dict(gx[-1])
                    bufs[ph] = P.tok()
                return bufs

            def emit_pass(hps, pi):
                wo_toks = [wotok[h_] for h_ in hps]
                for t in range(NT):
                    for oh in range(2):
                        j = t * 2 + oh
                        op_ = [psE, psV][j % 2]
                        w0 = [st.get(("pass_free", j % 2))]
                        if j == 0:
                            w0 += ([st["attnc_ready"],
                                    st.get(("proj_free", 0))] + wo_toks)
                        if j == 1:
                            w0 += [st.get(("pvfree", 0))]
                        if pi == 0:
                            P.op("pe", lambda e, oh=oh, op_=op_: e.matmul(
                                op_, onesr[0:1, 0:128],
                                BROW["o"][0:1, oh * 512:(oh + 1) * 512],
                                start=True, stop=False), waits=w0)
                        for ii, h_ in enumerate(hps):
                            first = (ii == 0 and pi != 0)
                            last = (ii == len(hps) - 1)
                            P.op("pe", lambda e, t=t, oh=oh, op_=op_, h_=h_,
                                 first=first, last=last: e.matmul(
                                op_, yT[:, h_ * 2048 + t * 128:
                                        h_ * 2048 + (t + 1) * 128],
                                wo_sb[h_ % 2][:, oh * 512:(oh + 1) * 512],
                                start=first, stop=last),
                                waits=w0 if first else (), inc=last)
                        tk_o = {"pe": P.cnt["pe"]}
                        tk_x = P.op("dve", lambda e, t=t, oh=oh, op_=op_:
                                    e.scalar_tensor_tensor(
                            xtile(t, oh * 512, 512), op_, 1.0,
                            xtile(t, oh * 512, 512), ALU.mult, ALU.add),
                            waits=[tk_o])
                        st[("pass_free", j % 2)] = tk_x
                        if j % 2 == 1:
                            st[("pvfree", 0)] = tk_x
                cp = {"pe": P.cnt["pe"]}
                for h_ in hps:
                    st[("wo_free", h_ % 2)] = cp

            def emit_cores(hp, bufs):
                for ph in range(2):
                    h = hp * 2 + ph
                    tok_bufs = bufs[ph]
                    qb_ = qh[ph]
                    kb_ = kh[ph]
                    for pr in range(8):
                        gp = gpair[0]
                        gpair[0] += 1
                        pp = gp % 2
                        fullp = [psA, psB][pp]
                        msets = []
                        wfirst = [tok_bufs, exp_done.get(gp - 2), tok_muq[ph]]
                        wtail = [exp_tails.get(gp - 1)]
                        if ph == 0 and pr == 0:
                            wtail.append(st.get(("proj_free", 1)))
                        for ql in range(2):
                            qt = pr * 2 + ql
                            q_rhs = qb_[0:64, qt * 128:(qt + 1) * 128]
                            qx_rhs = qb_[0:67, qt * 128:(qt + 1) * 128]
                            tailp = [psC, psD][ql]
                            P.op("pe", lambda e, kb_=kb_, qt=qt, fullp=fullp,
                                 ql=ql, q_rhs=q_rhs: e.matmul(
                                fullp[:, ql * 256:ql * 256 + 128],
                                kb_[0:64, qt * 128:(qt + 1) * 128], q_rhs,
                                start=True, stop=True),
                                waits=wfirst if ql == 0 else ())
                            P.op("pe", lambda e, fullp=fullp, ql=ql, ph=ph,
                                 qx_rhs=qx_rhs: e.matmul(
                                fullp[:, ql * 256 + 128:ql * 256 + 256],
                                kxb[ph][0:67, 0:128], qx_rhs,
                                start=True, stop=True))
                            if qt != 0:
                                P.op("pe", lambda e, kb_=kb_, tailp=tailp,
                                     q_rhs=q_rhs: e.matmul(
                                    tailp[0:64, 0:128], kb_[0:64, 0:64],
                                    q_rhs, start=True, stop=True), waits=wtail)
                            else:
                                msets.append(P.op("dve", lambda e, tailp=tailp:
                                    e.memset(tailp[0:64, 0:128], NEG),
                                    waits=wtail))
                            if qt != NT - 1:
                                P.op("pe", lambda e, kb_=kb_, tailp=tailp,
                                     q_rhs=q_rhs: e.matmul(
                                    tailp[0:64, 128:256],
                                    kb_[0:64, 1984:2048], q_rhs,
                                    start=True, stop=True), waits=wtail)
                            else:
                                msets.append(P.op("dve", lambda e, tailp=tailp:
                                    e.memset(tailp[0:64, 128:256], NEG),
                                    waits=wtail))
                            if qt != 0:
                                P.op("pe", lambda e, kb_=kb_, qt=qt,
                                     tailp=tailp, qb_=qb_: e.matmul(
                                    tailp[0:64, 256:320],
                                    kb_[0:64, (2 * qt - 1) * 64:(2 * qt) * 64],
                                    qb_[0:64, qt * 128:qt * 128 + 64],
                                    start=True, stop=True), waits=wtail)
                            else:
                                msets.append(P.op("dve", lambda e, tailp=tailp:
                                    e.memset(tailp[0:64, 256:320], NEG),
                                    waits=wtail))
                            if qt != NT - 1:
                                P.op("pe", lambda e, kb_=kb_, qt=qt,
                                     tailp=tailp, qb_=qb_: e.matmul(
                                    tailp[0:64, 320:384],
                                    kb_[0:64, (2 * qt + 2) * 64:
                                        (2 * qt + 3) * 64],
                                    qb_[0:64, qt * 128 + 64:qt * 128 + 128],
                                    start=True, stop=True), waits=wtail)
                            else:
                                msets.append(P.op("dve", lambda e, tailp=tailp:
                                    e.memset(tailp[0:64, 320:384], NEG),
                                    waits=wtail))
                            P.op("pe", lambda e, tailp=tailp, ph=ph,
                                 qx_rhs=qx_rhs: e.matmul(
                                tailp[0:64, 384:512],
                                kxb[ph][0:67, 128:192], qx_rhs,
                                start=True, stop=True), waits=wtail,
                                inc=True)
                            if ql == 0:
                                tk_ql0 = {"pe": P.cnt["pe"]}
                                msets0 = list(msets)
                        tk_st_pe = {"pe": P.cnt["pe"]}
                        P.op("act", lambda e: e.activation(
                            etail[0:64, 0:512], psC[0:64, :], AF.Exp),
                            waits=[tk_ql0] + msets0)
                        exp_tails[gp] = P.op("act", lambda e: e.activation(
                            etail[0:64, 512:1024], psD[0:64, :], AF.Exp),
                            waits=[tk_st_pe] + msets)
                        st["tails_read"] = exp_tails[gp]
                        exp_done[gp] = P.op("act", lambda e, fullp=fullp:
                                            e.activation(
                            eful, fullp, AF.Exp))
                        g4g = h * 4 + pr // 2
                        pvb = [psV, psVB][g4g % 2]
                        for ql in range(2):
                            qt = pr * 2 + ql
                            qv = (qt % 4) * 128
                            w0 = [exp_done[gp], st.get(("pvfree", g4g % 2)),
                                  psv_free.get(g4g - 2)]
                            if g4g % 2 == 0:
                                w0.append(st.get(("pass_free", 1)))
                            P.op("pe", lambda e, qt=qt, qv=qv, ql=ql, ph=ph,
                                 pvb=pvb: e.matmul(
                                pvb[0:65, qv:qv + 128],
                                vaug[ph][:, qt * 65:qt * 65 + 65],
                                eful[:, ql * 256:ql * 256 + 128],
                                start=True, stop=False), waits=w0)
                            P.op("pe", lambda e, qv=qv, ql=ql, ph=ph,
                                 pvb=pvb: e.matmul(
                                pvb[0:65, qv:qv + 128],
                                vxg[ph][:, 0:65],
                                eful[:, ql * 256 + 128:ql * 256 + 256],
                                start=False, stop=False))
                            P.op("pe", lambda e, qv=qv, ql=ql, ph=ph,
                                 pvb=pvb: e.matmul(
                                pvb[0:65, qv:qv + 128],
                                vaug[ph][0:64, 0:65],
                                etail[0:64, ql * 512:ql * 512 + 128],
                                start=False, stop=False))
                            P.op("pe", lambda e, qv=qv, ql=ql, ph=ph,
                                 pvb=pvb: e.matmul(
                                pvb[0:65, qv:qv + 128],
                                vsh[ph][0:64, 15 * 65:15 * 65 + 65],
                                etail[0:64, ql * 512 + 128:ql * 512 + 256],
                                start=False, stop=False))
                            if qt != 0:
                                P.op("pe", lambda e, qt=qt, qv=qv, ql=ql,
                                     ph=ph, pvb=pvb: e.matmul(
                                    pvb[0:65, qv:qv + 64],
                                    vsh[ph][0:64, (qt - 1) * 65:
                                            (qt - 1) * 65 + 65],
                                    etail[0:64, ql * 512 + 256:
                                          ql * 512 + 320],
                                    start=False, stop=False))
                            if qt != NT - 1:
                                P.op("pe", lambda e, qt=qt, qv=qv, ql=ql,
                                     ph=ph, pvb=pvb: e.matmul(
                                    pvb[0:65, qv + 64:qv + 128],
                                    vaug[ph][0:64, (qt + 1) * 65:
                                             (qt + 1) * 65 + 65],
                                    etail[0:64, ql * 512 + 320:
                                          ql * 512 + 384],
                                    start=False, stop=False))
                            P.op("pe", lambda e, qv=qv, ql=ql, ph=ph,
                                 pvb=pvb: e.matmul(
                                pvb[0:65, qv:qv + 128],
                                vxg[ph][0:64, 65:130],
                                etail[0:64, ql * 512 + 384:ql * 512 + 512],
                                start=False, stop=True), inc=True)
                        if pr % 2 == 1:
                            g4 = pr // 2
                            tb = (g4g + 1) % 2
                            otherb = [psV, psVB][tb]
                            tk_pv = {"pe": P.cnt["pe"]}
                            P.op("act", lambda e, pvb=pvb: e.activation(
                                lnz, pvb[64:65, 0:512], AF.Ln),
                                waits=[tk_pv, st.get("rz_reader")])

                            def rzfn(e):
                                with nc.allow_low_precision(
                                        reason="1/Z as bf16 matmul operand"):
                                    return e.activation(rz, lnz, AF.Exp,
                                                        scale=-1.0)
                            tk_rz = P.op("act", rzfn)
                            zw = [tk_rz, psv_free.get(g4g - 1),
                                  st.get("rz_reader")]
                            if tb == 0:
                                zw.append(st.get(("pass_free", 1)))
                            tk_zb = P.op("pe", lambda e, otherb=otherb:
                                         e.matmul(
                                otherb[0:64, 0:512], onesr[0:1, 0:64], rz,
                                start=True, stop=True), waits=zw, inc=True)
                            st["rz_reader"] = tk_zb
                            tk_ze = P.op("act", lambda e, otherb=otherb:
                                         e.copy(rzb, otherb[0:64, 0:512]),
                                waits=[tk_zb, st.get("rzb_free")])
                            st[("pvfree", tb)] = tk_ze
                            tk_at = P.op("dve", lambda e, g4=g4, ph=ph,
                                         pvb=pvb, hp=hp:
                                         e.scalar_tensor_tensor(
                                yT[ph * 64:(ph + 1) * 64,
                                   hp * 2048 + g4 * 512:
                                   hp * 2048 + (g4 + 1) * 512],
                                pvb[0:64, 0:512], 1.0, rzb,
                                ALU.mult, ALU.mult), waits=[tk_ze])
                            st["rzb_free"] = tk_at
                            psv_free[g4g] = tk_at
                            st["attnc_ready"] = tk_at
                    core_pe_ph[(hp, ph)] = {"pe": P.cnt["pe"]}

            donep = [0]
            for hp in range(nhp):
                emit_proj(hp)
                bufs = emit_preps(hp)
                if hp - donep[0] >= 2:
                    emit_pass((donep[0], donep[0] + 1), pctr[0])
                    donep[0] += 2
                    pctr[0] += 1
                wotok[hp] = P.dma("sp", lambda e, hp=hp: e.dma_start(
                    wo_sb[hp % 2], wbf_d[3, hp * 128:(hp + 1) * 128, :]),
                    waits=[st.get(("wo_free", hp % 2))])
                emit_cores(hp, bufs)
                if "attnc0" in taps and b == 0 and hp == 0:
                    P.dma("sp", lambda e: e.dma_start(
                        tap_h["attnc0"], yT[:, 0:2048]),
                        waits=[st["attnc_ready"]])
            emit_pass(tuple(range(donep[0], nhp)), pctr[0])
            pctr[0] += 1
            P.fence()

        for b in range(nb):
            xw = st.get("x_stored")
            for t in range(NT):
                P.dma("sp", lambda e, b=b, t=t: e.dma_start(
                    xtile(t), x_h[b, t * 128:(t + 1) * 128, :]),
                    waits=[xw] if t == 0 else (), group=("xl", b))
            P.fence()
            if do_ffn1:
                emit_ffn(b, "ffn1")
            if do_attn:
                emit_attn(b)
            if do_ffn3:
                emit_ffn(b, "ffn3")
            wtok = P.fence()
            for t in range(NT):
                P.dma("sp", lambda e, b=b, t=t: e.dma_start(
                    out_h[b, t * 128:(t + 1) * 128, :], xtile(t)),
                    waits=[wtok] if t == 0 else (), group=("xs", b))
            st["x_stored"] = P.tok()

        P._waits("sp", [P.tok()])
        P.emit(E(nc.Block()), sems)
    return nc


def make_consts():
    idn = np.eye(128, dtype=np.float32).astype(BFDT)
    ones = np.ones((1, 512), dtype=np.float32).astype(BFDT)
    qbvec = (np.arange(S, dtype=np.float32) // 64).reshape(1, S).astype(np.float32)
    kxind = np.zeros((3, 192), dtype=np.float32)
    for t in range(3):
        kxind[t, t * 64:(t + 1) * 64] = 1.0
    oh64 = np.eye(64, dtype=np.float32).astype(BFDT)
    iotak = np.stack([np.arange(64, dtype=np.int64) * S,
                      np.arange(64, dtype=np.int64) * (NT * 65)],
                     axis=1).astype(np.int32)
    return idn, ones, qbvec, kxind.astype(BFDT), oh64, iotak


def make_inputs(core_slice_x, Wq, bq, Wk, bk, Wv, bv, Wo, bo, src, tgt,
                sel, keys, values):
    k8 = np.ascontiguousarray(
        np.asarray(keys, np.float32).transpose(1, 0, 2).reshape(D, -1))
    val = np.ascontiguousarray(np.asarray(values, np.float32))
    selT = np.ascontiguousarray(np.asarray(sel, np.float32).T)
    idn, ones, qbvec, kxind, oh64, iotak = make_consts()
    return {
        "x": np.ascontiguousarray(np.asarray(core_slice_x, np.float32)),
        "k8": k8, "val": val, "selT": selT,
        "wqT": np.ascontiguousarray(np.asarray(Wq, np.float32).T),
        "wkT": np.ascontiguousarray(np.asarray(Wk, np.float32).T),
        "wvT": np.ascontiguousarray(np.asarray(Wv, np.float32).T),
        "woT": np.ascontiguousarray(np.asarray(Wo, np.float32).T),
        "biases": np.concatenate([np.asarray(v, np.float32).ravel()
                                  for v in (bq, bk, bv, bo)]).reshape(1, 4 * D),
        "srctgt": np.concatenate([np.asarray(src, np.int32).ravel(),
                                  np.asarray(tgt, np.int32).ravel()]).reshape(1, 6),
        "idn": idn, "onesrow": ones, "qbvec": qbvec, "kxind": kxind,
        "oh64": oh64, "iotak": iotak,
    }


def kernel(L0_attn_x, L0_attn_Wq, L0_attn_bq, L0_attn_Wk, L0_attn_bk,
           L0_attn_Wv, L0_attn_bv, L0_attn_Wo, L0_attn_bo,
           L0_attn_src_blocks, L0_attn_tgt_blocks,
           L0_ffn_expert_sel, L0_ffn_keys, L0_ffn_values):
    x = np.asarray(L0_attn_x, dtype=np.float32)
    nc = build_program({})
    in_maps = []
    for c in range(NCORES):
        in_maps.append(make_inputs(
            x[2 * c:2 * c + 2], L0_attn_Wq, L0_attn_bq, L0_attn_Wk, L0_attn_bk,
            L0_attn_Wv, L0_attn_bv, L0_attn_Wo, L0_attn_bo,
            L0_attn_src_blocks, L0_attn_tgt_blocks,
            L0_ffn_expert_sel, L0_ffn_keys, L0_ffn_values))
    res = run_bass_kernel_spmd(nc, in_maps, core_ids=list(range(NCORES)))
    out = np.concatenate([r["out"] for r in res.results], axis=0)
    return out.astype(np.float32)


# version 07
# speedup vs baseline: 1.1457x; 1.0027x over previous
import numpy as np
import ml_dtypes
import concourse.bass as bass
import concourse.mybir as mybir
from concourse.bass_utils import run_bass_kernel_spmd
from contextlib import ExitStack

F32 = mybir.dt.float32
BF16 = mybir.dt.bfloat16
I32 = mybir.dt.int32
AF = mybir.ActivationFunctionType
ALU = mybir.AluOpType
AX = mybir.AxisListType
BFDT = ml_dtypes.bfloat16

D = 1024
S = 2048
NT = 16
NCORES = 8
NEG = -30000.0


class Plan:
    ENGS = ("pe", "act", "dve", "pool", "sp")
    SELF = {"dve", "act", "pool"}

    def __init__(self):
        self.streams = {e: [] for e in self.ENGS}
        self.cnt = {"pe": 0, "act": 0, "dve": 0, "pool": 0, "d0": 0, "d1": 0}
        self.mark = {}
        self.groups = {}

    def _wait(self, eng, sem, val):
        if val is None or val <= 0:
            return
        k = (eng, sem)
        if self.mark.get(k, 0) >= val:
            return
        self.mark[k] = val
        self.streams[eng].append(("w", sem, val))

    def _waits(self, eng, waits):
        for w in waits:
            if not w:
                continue
            if isinstance(w, dict):
                for s_, v in w.items():
                    self._wait(eng, s_, v)
            else:
                self._wait(eng, w[0], w[1])

    def op(self, eng, fn, waits=(), inc=False):
        self._waits(eng, waits)
        incs = []
        tok = {}
        if eng in self.SELF:
            prev = self.cnt[eng]
            self._wait(eng, eng, prev)
            self.cnt[eng] = prev + 1
            incs.append((eng, 1))
            tok[eng] = prev + 1
        if inc and eng == "pe":
            self.cnt["pe"] += 1
            incs.append(("pe", 1))
            tok["pe"] = self.cnt["pe"]
        self.streams[eng].append(("o", fn, incs))
        return tok

    def dma(self, eng, fn, waits=(), sem=None, group=None):
        sem = sem or ("d0" if eng == "sp" else "d1")
        gkey = (eng, sem)
        if group is None or self.groups.get(gkey) != (sem, group):
            self._wait(eng, sem, self.cnt[sem])
            self.groups[gkey] = (sem, group) if group is not None else None
        self._waits(eng, waits)
        self.cnt[sem] += 16
        self.streams[eng].append(("o", fn, [(sem, 16)]))
        return {sem: self.cnt[sem]}

    def tok(self):
        return dict(self.cnt)

    def fence(self):
        t = self.tok()
        for e in self.ENGS:
            self._waits(e, [t])
        return t

    def emit(self, block, sems):
        def runner(name):
            def run(engine):
                for it in self.streams[name]:
                    if it[0] == "w":
                        engine.wait_ge(sems[it[1]], it[2])
                    else:
                        inst = it[1](engine)
                        for s_, n in it[2]:
                            inst.then_inc(sems[s_], n)
            return run
        block.tensor(runner("pe"))
        block.scalar(runner("act"))
        block.vector(runner("dve"))
        block.gpsimd(runner("pool"))
        block.sync(runner("sp"))


def ap_dyn(apx, reg):
    return bass.AP(apx.tensor, reg, [list(d) for d in apx.ap])


def ap_make(apx, rel, dims):
    return bass.AP(apx.tensor, apx.offset + rel,
                   [list(d) if not isinstance(d, int) else d for d in dims])


def build_program(cfg):
    nexp = cfg.get("nexp", 64)
    ebs = cfg.get("ebs", 8)
    neb = nexp // ebs
    nh = cfg.get("nh", 16)
    nhp = nh // 2
    nb = cfg.get("nb", 2)
    do_ffn1 = cfg.get("ffn1", True)
    do_attn = cfg.get("attn", True)
    do_ffn3 = cfg.get("ffn3", True)
    taps = cfg.get("taps", ())

    nc = bass.Bass("TRN2", target_bir_lowering=False)

    x_h = nc.dram_tensor("x", [nb, S, D], F32, kind="ExternalInput").ap()
    k8_h = nc.dram_tensor("k8", [D, nexp * 128], F32, kind="ExternalInput").ap()
    val_h = nc.dram_tensor("val", [nexp, 128, D], F32, kind="ExternalInput").ap()
    selT_h = nc.dram_tensor("selT", [D, 64], F32, kind="ExternalInput").ap()
    wT_h = {m: nc.dram_tensor("w%sT" % m, [D, D], F32, kind="ExternalInput").ap()
            for m in "qkvo"}
    bias_h = nc.dram_tensor("biases", [1, 4 * D], F32, kind="ExternalInput").ap()
    srctgt_h = nc.dram_tensor("srctgt", [1, 6], I32, kind="ExternalInput").ap()
    idn_h = nc.dram_tensor("idn", [128, 128], BF16, kind="ExternalInput").ap()
    ones_h = nc.dram_tensor("onesrow", [1, 512], BF16, kind="ExternalInput").ap()
    qbvec_h = nc.dram_tensor("qbvec", [1, S], F32, kind="ExternalInput").ap()
    kxind_h = nc.dram_tensor("kxind", [3, 192], BF16, kind="ExternalInput").ap()
    oh64_h = nc.dram_tensor("oh64", [64, 64], BF16, kind="ExternalInput").ap()
    iotak_h = nc.dram_tensor("iotak", [64, 2], I32, kind="ExternalInput").ap()
    out_h = nc.dram_tensor("out", [nb, S, D], F32, kind="ExternalOutput").ap()
    mu_d = nc.dram_tensor("mu_d", [3, S], BF16, kind="Internal").ap()
    akd_h = nc.dram_tensor("akd", [64, S], BF16, kind="Internal").ap()
    avd_h = nc.dram_tensor("avd", [128, NT * 65], BF16, kind="Internal").ap()
    wbf_d = nc.dram_tensor("wbf", [4, D, D], BF16, kind="Internal").ap()
    kd_h = nc.dram_tensor("kd", [D, nexp * 128], BF16, kind="Internal").ap()
    vd_h = nc.dram_tensor("vd", [nexp, 128, D], BF16, kind="Internal").ap()
    tap_h = {}
    for tname in taps:
        shape, dt = {"yT1": ([128, NT * 1024], BF16), "gT": ([64, S], BF16),
                     "scT0": ([128, ebs * S], BF16),
                     "attnc0": ([128, S], BF16)}[tname]
        tap_h[tname] = nc.dram_tensor("tap_" + tname, shape, dt,
                                      kind="ExternalOutput").ap()

    P = Plan()
    st = {}

    with ExitStack() as es:
        E = es.enter_context
        xb = E(nc.sbuf_tensor("xb", [128, NT * 1024], F32)).ap()
        yT = E(nc.sbuf_tensor("yT", [128, NT * 1024], BF16)).ap()
        a1 = E(nc.sbuf_tensor("a1", [128, 18688], BF16)).ap()
        a2 = E(nc.sbuf_tensor("a2", [128, 16384], BF16)).ap()
        gT = E(nc.sbuf_tensor("gT", [64, S], BF16)).ap()
        selbuf = E(nc.sbuf_tensor("selbuf", [128, 8 * 64], BF16)).ap()
        selb = [E(nc.sbuf_tensor("sel%d" % i, [128, 64], F32)).ap()
                for i in range(2)]
        curb = [E(nc.sbuf_tensor("cur%d" % i, [128, 64], F32)).ap()
                for i in range(2)]
        geb = [E(nc.sbuf_tensor("ge%d" % i, [128, 64], F32)).ap()
               for i in range(2)]
        gateb = [E(nc.sbuf_tensor("gate%d" % i, [128, 64], BF16)).ap()
                 for i in range(2)]
        xmb = [E(nc.sbuf_tensor("xm%d" % i, [128, 1024], F32)).ap()
               for i in range(2)]
        ybf = [E(nc.sbuf_tensor("ybf%d" % i, [128, 1024], BF16)).ap()
               for i in range(2)]
        gbc = [E(nc.sbuf_tensor("gbc%d" % i, [128, 512], BF16)).ap()
               for i in range(2)]
        col1 = E(nc.sbuf_tensor("col1", [128, 8], F32)).ap()
        idn = E(nc.sbuf_tensor("idn_s", [128, 128], BF16)).ap()
        onesr = E(nc.sbuf_tensor("onesr", [1, 512], BF16)).ap()
        oh64 = E(nc.sbuf_tensor("oh64_s", [64, 64], BF16)).ap()
        biasb = E(nc.sbuf_tensor("biasb", [1, 4 * D], BF16)).ap()
        srctgt = E(nc.sbuf_tensor("srctgt_s", [1, 6], I32)).ap()
        scr = E(nc.sbuf_tensor("scr", [1, 64], F32)).ap()
        col0i = E(nc.sbuf_tensor("col0i", [1, 3], I32)).ap()
        coli2 = E(nc.sbuf_tensor("coli2", [1, 3], I32)).ap()
        scr3i = E(nc.sbuf_tensor("scr3i", [1, 3], I32)).ap()
        tgtbf = E(nc.sbuf_tensor("tgtbf", [1, 9], BF16)).ap()
        tgtf = E(nc.sbuf_tensor("tgtf", [1, 9], F32)).ap()
        iotak = E(nc.sbuf_tensor("iotak_s", [64, 2], I32)).ap()
        kxidx = E(nc.sbuf_tensor("kxidx", [64, 3], I32)).ap()
        avidx = E(nc.sbuf_tensor("avidx", [64, 3], I32)).ap()
        idxf = E(nc.sbuf_tensor("idxf", [64, 4], F32)).ap()
        kxis = E(nc.sbuf_tensor("kxis", [3, 192], BF16)).ap()
        rz = E(nc.sbuf_tensor("rz", [1, 512], BF16)).ap()
        lnz = E(nc.sbuf_tensor("lnz", [1, 512], F32)).ap()

        scT = a1[:, 0:ebs * S]
        qh = [a1[0:67, 0:2048], a1[0:67, 2048:4096]]
        kh = [a1[0:64, 4096:6144], a1[0:64, 6144:8192]]
        avT = a1[0:64, 8192:10240]
        vaug = [a1[:, 10240:11280], a1[:, 11280:12320]]
        vsh = [a1[0:64, 12320:13360], a1[0:64, 13360:14400]]
        eful = a1[:, 16448:16960]
        etail = a1[0:64, 16960:17984]
        rzb = a1[0:64, 17984:18496]
        keysb = a2[:, 0:8 * 1024]
        valsb = a2[:, 8 * 1024:8 * 1024 + ebs * 1024]
        vstage = a2[:, 12288:13312]
        wo_sb = [a2[:, 13312:14336], a1[:, 14400:15424]]
        kxb = [a2[0:67, 14336:14528], a2[0:67, 14528:14720]]
        vxg = [a2[:, 14720:14850], a2[:, 14850:14980]]
        wtmp = [a2[:, 0:1024], a2[:, 1024:2048]]
        qbv = xb[0:1, 0:S]
        mrow = yT[0:1, 0:3 * S]

        psA = E(nc.psum_tensor("psA", [128, 512], F32)).ap()
        psB = E(nc.psum_tensor("psB", [128, 512], F32)).ap()
        psC = E(nc.psum_tensor("psC", [128, 512], F32)).ap()
        psD = E(nc.psum_tensor("psD", [128, 512], F32)).ap()
        psE = E(nc.psum_tensor("psE", [128, 512], F32)).ap()
        psV = E(nc.psum_tensor("psV", [128, 512], F32)).ap()
        psT = E(nc.psum_tensor("psT", [128, 1024], BF16)).ap()
        psVB = E(nc.psum_tensor("psVB", [128, 512], F32)).ap()

        sems = {}
        for s_ in ("d0", "d1", "pe", "act", "dve", "pool"):
            sems[s_] = E(nc.semaphore(s_))

        EPSC = col1[:, 3:4]
        S1C = [col1[:, 0:1], col1[:, 4:5]]
        S2C = [col1[:, 1:2], col1[:, 5:6]]
        RSC = [col1[:, 2:3], col1[:, 6:7]]
        BROW = {m: biasb[0:1, i * D:(i + 1) * D] for i, m in enumerate("qkvo")}

        def xtile(t, lo=0, n=1024):
            return xb[:, t * 1024 + lo: t * 1024 + lo + n]

        def yT_chunk(t, dc):
            return yT[:, t * 1024 + dc * 128: t * 1024 + dc * 128 + 128]

        def yT_rhs(dc, tok0, ntok):
            return ap_make(yT, (tok0 // 128) * 1024 + dc * 128,
                           [yT.ap[0], [1024, ntok // 128], [1, 128]])

        def ohsel(e):
            return ap_make(oh64, e, [oh64.ap[0], [0, 128]])

        tok_idn = P.dma("sp", lambda e: e.dma_start(idn, idn_h))
        tok_ones = P.dma("sp", lambda e: e.dma_start(onesr, ones_h))
        tok_oh = P.dma("sp", lambda e: e.dma_start(oh64, oh64_h))
        tok_st = P.dma("sp", lambda e: e.dma_start(srctgt, srctgt_h))
        tok_qbv = P.dma("sp", lambda e: e.dma_start(qbv, qbvec_h))
        tok_kxi = P.dma("sp", lambda e: e.dma_start(kxis, kxind_h))
        P.dma("pool", lambda e: e.dma_start(biasb, bias_h))
        for dc in range(8):
            P.dma("pool", lambda e, dc=dc: e.dma_start(
                selbuf[:, dc * 64:(dc + 1) * 64],
                selT_h[dc * 128:(dc + 1) * 128, :]), group="setup")
        P.op("dve", lambda e: e.memset(EPSC, 1e-5))

        sf = scr[0:1, 0:3]
        tf = scr[0:1, 3:6]
        P.op("dve", lambda e: e.tensor_copy(sf, srctgt[0:1, 0:3]), waits=[tok_st])
        P.op("dve", lambda e: e.tensor_copy(tf, srctgt[0:1, 3:6]))
        P.op("dve", lambda e: e.tensor_sub(scr[0:1, 6:9], sf, tf))
        P.op("dve", lambda e: e.tensor_mul(
            scr[0:1, 6:9], scr[0:1, 6:9], scr[0:1, 6:9]))
        P.op("dve", lambda e: e.tensor_single_scalar(
            scr[0:1, 6:9], scr[0:1, 6:9], 2.25, ALU.is_gt))
        P.op("dve", lambda e: e.tensor_single_scalar(
            scr[0:1, 9:12], tf, 0.5, ALU.is_gt))
        P.op("dve", lambda e: e.tensor_single_scalar(
            scr[0:1, 12:15], tf, 30.5, ALU.is_lt))
        P.op("dve", lambda e: e.tensor_scalar(
            scr[0:1, 15:18], sf, 32.0, None, ALU.mult))
        P.op("dve", lambda e: e.tensor_add(scr[0:1, 15:18], scr[0:1, 15:18], tf))
        P.op("dve", lambda e: e.memset(scr[0:1, 18:21], 0.0))
        P.op("dve", lambda e: e.tensor_single_scalar(
            scr[0:1, 19:20], scr[0:1, 16:17], scr[0:1, 15:16], ALU.is_equal))
        P.op("dve", lambda e: e.tensor_single_scalar(
            scr[0:1, 20:21], scr[0:1, 17:18], scr[0:1, 15:16], ALU.is_equal))
        P.op("dve", lambda e: e.tensor_single_scalar(
            scr[0:1, 21:22], scr[0:1, 17:18], scr[0:1, 16:17], ALU.is_equal))
        P.op("dve", lambda e: e.tensor_max(
            scr[0:1, 20:21], scr[0:1, 20:21], scr[0:1, 21:22]))
        P.op("dve", lambda e: e.tensor_mul(
            scr[0:1, 22:25], scr[0:1, 6:9], scr[0:1, 9:12]))
        P.op("dve", lambda e: e.tensor_mul(
            scr[0:1, 22:25], scr[0:1, 22:25], scr[0:1, 12:15]))
        P.op("dve", lambda e: e.tensor_scalar(
            scr[0:1, 25:28], scr[0:1, 18:21], -1.0, 1.0, ALU.mult, ALU.add))
        P.op("dve", lambda e: e.tensor_mul(
            scr[0:1, 22:25], scr[0:1, 22:25], scr[0:1, 25:28]))
        P.op("dve", lambda e: e.tensor_scalar(
            scr[0:1, 28:31], scr[0:1, 22:25], -NEG, None, ALU.mult))
        tok_mu = None
        for t in range(3):
            P.op("dve", lambda e, t=t: e.tensor_single_scalar(
                mrow[0:1, t * S:(t + 1) * S], qbv, sf[0:1, t:t + 1],
                ALU.is_equal), waits=[tok_qbv])
            tok_mu = P.op("dve", lambda e, t=t: e.tensor_scalar(
                mrow[0:1, t * S:(t + 1) * S], mrow[0:1, t * S:(t + 1) * S],
                scr[0:1, 28 + t:29 + t], NEG, ALU.mult, ALU.add))
        tok_mud = P.dma("sp", lambda e: e.dma_start(
            ap_make(mu_d, 0, [[S, 3], [1, S]]),
            ap_make(mrow, 0, [mrow.ap[0], [1, 3 * S]])), waits=[tok_mu])
        tok_iok = P.dma("sp", lambda e: e.dma_start(iotak, iotak_h))
        P.op("dve", lambda e: e.tensor_single_scalar(
            col0i, srctgt[0:1, 3:6], 6, ALU.arith_shift_left), waits=[tok_st])
        P.op("dve", lambda e: e.tensor_single_scalar(
            coli2, srctgt[0:1, 3:6], 1, ALU.bitwise_and))
        P.op("dve", lambda e: e.tensor_single_scalar(
            scr3i, srctgt[0:1, 3:6], 1, ALU.arith_shift_right))
        P.op("dve", lambda e: e.tensor_copy(tgtf[0:1, 0:3], col0i))
        P.op("dve", lambda e: e.tensor_copy(tgtf[0:1, 3:6], coli2))
        P.op("dve", lambda e: e.tensor_copy(tgtf[0:1, 6:9], scr3i))
        tok_tbf = P.op("dve", lambda e: e.tensor_copy(tgtbf, tgtf))
        for t in range(3):
            P.op("pe", lambda e, t=t: e.matmul(
                psE[0:64, 0:3], onesr[0:1, 0:64],
                ap_make(tgtbf, t, [tgtbf.ap[0], [3, 3]]),
                start=True, stop=True),
                waits=[tok_tbf, tok_ones, st.get("idx_ps")], inc=True)
            tk = {"pe": P.cnt["pe"]}
            P.op("dve", lambda e: e.tensor_copy(idxf[:, 0:3], psE[0:64, 0:3]),
                 waits=[tk])
            P.op("dve", lambda e, t=t: e.tensor_copy(
                kxidx[:, t:t + 1], idxf[:, 0:1]))
            st["idx_ps"] = P.op("dve", lambda e, t=t: e.tensor_add(
                kxidx[:, t:t + 1], kxidx[:, t:t + 1], iotak[:, 0:1]),
                waits=[tok_iok])
            P.op("dve", lambda e: e.tensor_scalar(
                idxf[:, 1:2], idxf[:, 1:2], 66560.0, None, ALU.mult))
            P.op("dve", lambda e: e.tensor_scalar(
                idxf[:, 2:3], idxf[:, 2:3], 65.0, None, ALU.mult))
            P.op("dve", lambda e: e.tensor_add(
                idxf[:, 1:2], idxf[:, 1:2], idxf[:, 2:3]))
            P.op("dve", lambda e, t=t: e.tensor_copy(
                avidx[:, t:t + 1], idxf[:, 1:2]))
            st["idx_ps"] = P.op("dve", lambda e, t=t: e.tensor_add(
                avidx[:, t:t + 1], avidx[:, t:t + 1], iotak[:, 1:2]))
        jobs = []
        for mi, m in enumerate("qkvo"):
            for dc in range(8):
                jobs.append((wT_h[m][dc * 128:(dc + 1) * 128, :],
                             wbf_d[mi, dc * 128:(dc + 1) * 128, :], 1024))
        for dc in range(8):
            for c4 in range(nexp // 16):
                jobs.append((k8_h[dc * 128:(dc + 1) * 128,
                                  c4 * 2048:(c4 + 1) * 2048],
                             kd_h[dc * 128:(dc + 1) * 128,
                                  c4 * 2048:(c4 + 1) * 2048], 2048))
        for ei in range(nexp // 2):
            jobs.append((ap_make(val_h, 2 * ei * 128 * D,
                                 [[D, 256], [1, D]]),
                         ap_make(vd_h, 2 * ei * 128 * D,
                                 [[D, 256], [1, D]]), 2048))
        ctk = {}
        stk = {}
        nwv = (len(jobs) + 3) // 4
        for w in range(nwv):
            wj = jobs[4 * w:4 * w + 4]
            lt = {}
            for i, (srcap, dstap, ncol) in enumerate(wj):
                sl = (w % 2) * 4 + i
                if srcap.ap[0][1] == 256:
                    srcap = ap_make(srcap, 0, [[2 * D, 128], [1, 2 * D]])
                eng = "sp" if i < 2 else "pool"
                lt[eng] = P.dma(eng, lambda e, srcap=srcap, sl=sl, ncol=ncol:
                                e.dma_start(
                    xb[:, sl * 2048:sl * 2048 + ncol], srcap),
                    waits=[ctk.get(4 * (w - 2) + i)], group=("cl", w, eng))
            for i, (srcap, dstap, ncol) in enumerate(wj):
                ji = 4 * w + i
                sl = (w % 2) * 4 + i
                stg = xb[:, sl * 2048:sl * 2048 + ncol]
                bfb = a2[:, sl * 2048:sl * 2048 + ncol]
                weng = "pool" if i < 2 else "sp"
                cw = [lt["sp" if i < 2 else "pool"], stk.get((w - 2, weng))]
                if i % 2 == 0:
                    ctk[ji] = P.op("dve", lambda e, stg=stg, bfb=bfb:
                                   e.tensor_copy(bfb, stg), waits=cw)
                else:
                    ctk[ji] = P.op("act", lambda e, stg=stg, bfb=bfb:
                                   e.copy(bfb, stg), waits=cw)
            for i, (srcap, dstap, ncol) in enumerate(wj):
                ji = 4 * w + i
                sl = (w % 2) * 4 + i
                bfb = a2[:, sl * 2048:sl * 2048 + ncol]
                if dstap.ap[0][1] == 256:
                    dstap = ap_make(dstap, 0, [[2 * D, 128], [1, 2 * D]])
                weng = "pool" if i < 2 else "sp"
                stk[(w, weng)] = P.dma(weng, lambda e, dstap=dstap, bfb=bfb:
                                       e.dma_start(dstap, bfb),
                                       waits=[ctk[ji]], group=("cs", w, weng))
        P.fence()

        def emit_ln_pass(gates_too):
            pst_free = {}
            pslg_free = {}
            psg_free = {}
            yev = {}
            prev = {}
            for t in range(NT):
                p = t % 2
                P.op("dve", lambda e, t=t, p=p: e.reduce_sum(S1C[p], xtile(t), AX.X))
                P.op("dve", lambda e, p=p: e.tensor_scalar(
                    S1C[p], S1C[p], 1.0 / D, None, ALU.mult))
                tk_xm = P.op("dve", lambda e, t=t, p=p: e.tensor_scalar(
                    xmb[p], xtile(t), S1C[p], None, ALU.subtract),
                    waits=[prev.get(("sq", p))])
                tk_sq = P.op("act", lambda e, p=p: e.activation(
                    ybf[p], xmb[p], AF.Square, accum_out=S2C[p]),
                    waits=[tk_xm, prev.get(("ytr", p))])
                prev[("sq", p)] = tk_sq
                tk_sd = P.op("act", lambda e, p=p: e.activation(
                    RSC[p], S2C[p], AF.Sqrt, bias=EPSC, scale=1.0 / D),
                    waits=[tk_sq, prev.get(("y", p))])
                tk_rc = P.op("dve", lambda e, p=p: e.reciprocal(RSC[p], RSC[p]),
                             waits=[tk_sd])
                tk_y = P.op("dve", lambda e, p=p: e.tensor_scalar(
                    ybf[p], xmb[p], RSC[p], None, ALU.mult),
                    waits=[tk_rc, prev.get(("ytr", p))])
                prev[("y", p)] = tk_y
                tk_pe = None
                for dc in range(8):
                    w = [tk_y, pst_free.get(t - 1), psg_free.get(t - 1),
                         tok_idn] if dc == 0 else ()
                    tk_pe = P.op("pe", lambda e, dc=dc, p=p: e.transpose(
                        psT[:, dc * 128:(dc + 1) * 128],
                        ybf[p][:, dc * 128:(dc + 1) * 128], idn),
                        waits=w, inc=(dc == 7))
                prev[("ytr", p)] = tk_pe
                yev[t] = P.op("dve", lambda e, t=t: e.tensor_copy(
                    yT[:, t * 1024:(t + 1) * 1024], psT), waits=[tk_pe])
                pst_free[t] = yev[t]
                if not gates_too:
                    continue
                lgs = psE[:, (t % 4) * 64:(t % 4) * 64 + 64]
                for dc in range(8):
                    w = [yev[t], pslg_free.get(t - 4)] if dc == 0 else ()
                    P.op("pe", lambda e, t=t, dc=dc, lgs=lgs: e.matmul(
                        lgs, yT_chunk(t, dc), selbuf[:, dc * 64:(dc + 1) * 64],
                        start=(dc == 0), stop=(dc == 7)), waits=w, inc=(dc == 7))
                tk_lg = {"pe": P.cnt["pe"]}
                tk_sig = P.op("act", lambda e, p=p, lgs=lgs: e.activation(
                    selb[p], lgs, AF.Sigmoid),
                    waits=[tk_lg, prev.get(("gmul", p))])
                pslg_free[t] = tk_sig
                P.op("dve", lambda e, p=p: e.tensor_copy(curb[p], selb[p]),
                     waits=[tk_sig])
                for _ in range(3):
                    P.op("dve", lambda e, p=p: e.reduce_max(S1C[p], curb[p], AX.X))
                    P.op("dve", lambda e, p=p: e.tensor_scalar(
                        geb[p], curb[p], S1C[p], None, ALU.is_ge))
                    P.op("dve", lambda e, p=p: e.scalar_tensor_tensor(
                        curb[p], geb[p], NEG, curb[p], ALU.mult, ALU.add))
                P.op("dve", lambda e, p=p: e.reduce_max(S1C[p], curb[p], AX.X))
                P.op("dve", lambda e, p=p: e.tensor_scalar(
                    geb[p], selb[p], S1C[p], None, ALU.is_ge))
                tk_g = P.op("dve", lambda e, p=p: e.tensor_mul(
                    gateb[p], selb[p], geb[p]), waits=[prev.get(("gtr", p))])
                prev[("gmul", p)] = tk_g
                gsl = psT[:, 0:128]
                tk_gp = P.op("pe", lambda e, p=p, gsl=gsl: e.transpose(
                    gsl[0:64, :], gateb[p], idn),
                    waits=[tk_g, psg_free.get(t - 1), yev[t]], inc=True)
                prev[("gtr", p)] = tk_gp
                psg_free[t] = P.op("dve", lambda e, t=t, gsl=gsl: e.tensor_copy(
                    gT[0:64, t * 128:(t + 1) * 128], gsl[0:64, :]),
                    waits=[tk_gp])

        def emit_ffn(b, tag):
            emit_ln_pass(gates_too=True)
            tok_pro = P.fence()
            if "yT1" in taps and tag == "ffn1" and b == 0:
                P.dma("sp", lambda e: e.dma_start(tap_h["yT1"], yT),
                      waits=[tok_pro])
            if "gT" in taps and tag == "ffn1" and b == 0:
                P.dma("sp", lambda e: e.dma_start(tap_h["gT"], gT),
                      waits=[tok_pro])
            h_ps = [psA, psB]
            cd_ps = [psC, psD]
            for eb in range(neb):
                tokw = None
                for dc in range(8):
                    tokw = P.dma("sp", lambda e, dc=dc, eb=eb: e.dma_start(
                        keysb[:, dc * 1024:(dc + 1) * 1024],
                        kd_h[dc * 128:(dc + 1) * 128,
                             eb * 1024:(eb + 1) * 1024]),
                        waits=[st.get("keys_readers")] if dc == 0 else (),
                        group=("w", tag, eb))
                for el in range(ebs):
                    tokw = P.dma("sp", lambda e, el=el, eb=eb: e.dma_start(
                        valsb[:, el * 1024:(el + 1) * 1024],
                        vd_h[eb * ebs + el, :, :]),
                        waits=[st.get("vals_readers")] if el == 0 else (),
                        group=("w", tag, eb))
                tok_w = dict(tokw)
                for tg in range(4):
                    for el in range(ebs):
                        i = tg * ebs + el
                        hp_ = h_ps[i % 2]
                        bp_ = cd_ps[i % 2]
                        gp_ = gbc[i % 2]
                        for dc in range(8):
                            w = [tok_w, st.get(("httok", i % 2))] if dc == 0 else ()
                            P.op("pe", lambda e, dc=dc, el=el, tg=tg, hp_=hp_:
                                 e.matmul(
                                hp_,
                                keysb[:, dc * 1024 + el * 128:dc * 1024 + el * 128 + 128],
                                yT_rhs(dc, tg * 512, 512),
                                start=(dc == 0), stop=(dc == 7)),
                                waits=w, inc=(dc == 7))
                        tk_h = {"pe": P.cnt["pe"]}
                        tk_bc = P.op("pe", lambda e, el=el, eb=eb, tg=tg, bp_=bp_:
                                     e.matmul(
                            bp_, ohsel(eb * ebs + el),
                            gT[0:64, tg * 512:(tg + 1) * 512],
                            start=True, stop=True),
                            waits=[tok_oh, st.get(("cdtok", i % 2))], inc=True)
                        tk_be = P.op("act", lambda e, gp_=gp_, bp_=bp_:
                                     e.copy(gp_, bp_),
                                     waits=[tk_bc, st.get(("httok", i % 2))])
                        st[("cdtok", i % 2)] = tk_be
                        st[("httok", i % 2)] = P.op(
                            "dve", lambda e, el=el, tg=tg, hp_=hp_, gp_=gp_:
                            e.scalar_tensor_tensor(
                                scT[:, el * S + tg * 512:el * S + tg * 512 + 512],
                                hp_, 0.0, gp_, ALU.max, ALU.mult),
                            waits=[tk_h, tk_be, st.get("scT_readers")])
                st["keys_readers"] = {"pe": P.cnt["pe"]}
                tok_p1 = P.tok()
                if "scT0" in taps and tag == "ffn1" and b == 0 and eb == 0:
                    P.dma("sp", lambda e: e.dma_start(tap_h["scT0"], scT),
                          waits=[tok_p1])
                for t in range(NT):
                    for oh in range(2):
                        j = t * 2 + oh
                        op_ = cd_ps[j % 2]
                        for el in range(ebs):
                            w = [tok_p1, st.get(("cdtok", j % 2))] if el == 0 else ()
                            P.op("pe", lambda e, el=el, t=t, oh=oh, op_=op_:
                                 e.matmul(
                                op_,
                                scT[:, el * S + t * 128:el * S + t * 128 + 128],
                                valsb[:, el * 1024 + oh * 512:el * 1024 + oh * 512 + 512],
                                start=(el == 0), stop=(el == ebs - 1)),
                                waits=w, inc=(el == ebs - 1))
                        tk_o = {"pe": P.cnt["pe"]}
                        st[("cdtok", j % 2)] = P.op(
                            "dve", lambda e, t=t, oh=oh, op_=op_:
                            e.scalar_tensor_tensor(
                                xtile(t, oh * 512, 512), op_, 0.5,
                                xtile(t, oh * 512, 512), ALU.mult, ALU.add),
                            waits=[tk_o])
                st["vals_readers"] = {"pe": P.cnt["pe"]}
                st["scT_readers"] = {"pe": P.cnt["pe"]}
            P.fence()

        def emit_attn(b):
            wl = {}
            wotok = {}

            def emit_wloads(mg):
                if mg >= 3 * nhp:
                    return
                mi = mg % 3
                for dc in range(8):
                    s_ = (8 * mg + dc) % 12
                    eng = ["sp", "pool"][(8 * mg + dc) % 2]
                    wl[(mg, dc)] = P.dma(
                        eng, lambda e, mi=mi, dc=dc, s_=s_: e.dma_start(
                            a2[:, s_ * 1024:(s_ + 1) * 1024],
                            wbf_d[mi, dc * 128:(dc + 1) * 128, :]),
                        waits=[st.get(("slot_free", s_))],
                        group=("w", b, mg, eng, dc))

            emit_wloads(0)
            emit_ln_pass(gates_too=False)
            tok_yT = P.fence()
            tok_muq = []
            for i in range(2):
                tok_muq.append(P.dma("sp", lambda e, i=i: e.dma_start(
                    qh[i][64:67, :], mu_d), waits=[tok_yT, tok_mud]))
            for i in range(2):
                P.op("dve", lambda e, i=i: e.tensor_copy(
                    kxb[i][64:67, :], kxis), waits=[tok_kxi])
            gpair = [0]
            pms = {}
            exp_done = {}
            exp_tails = {}
            psv_free = {}
            core_pe_ph = {}
            pctr = [0]

            def emit_proj(hp):
                for mi, m in enumerate("qkv"):
                    mg = hp * 3 + mi
                    tke = None
                    for oc in range(8):
                        bank = oc % 2
                        psl = [psE, psD][bank][:, 0:256]
                        w0 = [st.get(("proj_free", bank)), tok_ones]
                        if mi == 0 and oc == 0:
                            w0.append(st.get(("pass_free", 0)))
                        if mi == 0 and oc == 1:
                            w0.append(st.get("tails_read"))
                        P.op("pe", lambda e, m=m, oc=oc, psl=psl: e.matmul(
                            psl, BROW[m][0:1, oc * 128:(oc + 1) * 128],
                            onesr[0:1, 0:256], start=True, stop=False),
                            waits=w0)
                        for dc in range(8):
                            s_ = (8 * mg + dc) % 12
                            tkdc = P.op("pe", lambda e, dc=dc, oc=oc, hp=hp,
                                        psl=psl, s_=s_: e.matmul(
                                psl, a2[:, s_ * 1024 + oc * 128:
                                        s_ * 1024 + oc * 128 + 128],
                                yT_rhs(dc, hp * 256, 256),
                                start=False, stop=(dc == 7)),
                                waits=[wl[(mg, dc)]] if oc == 0 else (),
                                inc=(oc == 7 or dc == 7))
                            if oc == 7:
                                st[("slot_free", s_)] = tkdc
                        tk = {"pe": P.cnt["pe"]}
                        if m == "q":
                            for rh in range(2):
                                for pp in range(2):
                                    tke = P.op("dve", lambda e, psl=psl, rh=rh,
                                               pp=pp, fc=2 * oc + rh:
                                               e.tensor_scalar(
                                        ap_make(qh[pp], fc,
                                                [[qh[pp].ap[0][0], 64],
                                                 [16, 128]]),
                                        psl[rh * 64:(rh + 1) * 64,
                                            pp * 128:(pp + 1) * 128],
                                        0.125, None, ALU.mult),
                                        waits=[tk] if rh + pp == 0 else ())
                        elif m == "k":
                            for rh in range(2):
                                for pp in range(2):
                                    tke = P.op("dve", lambda e, psl=psl, rh=rh,
                                               pp=pp, fc=2 * oc + rh:
                                               e.tensor_copy(
                                        ap_make(kh[pp], fc,
                                                [[kh[pp].ap[0][0], 64],
                                                 [16, 128]]),
                                        psl[rh * 64:(rh + 1) * 64,
                                            pp * 128:(pp + 1) * 128]),
                                        waits=[tk] if rh + pp == 0 else ())
                        else:
                            for rh in range(2):
                                tke = P.op("dve", lambda e, psl=psl, rh=rh,
                                           fc=2 * oc + rh: e.tensor_copy(
                                    ap_make(avT, fc, [[avT.ap[0][0], 64],
                                                      [16, 128]]),
                                    psl[rh * 64:(rh + 1) * 64, 0:128]),
                                    waits=[tk] if rh == 0 else ())
                            for rh in range(2):
                                tke = P.op("dve", lambda e, psl=psl, rh=rh,
                                           oc=oc: e.tensor_copy(
                                    vstage[rh * 64:(rh + 1) * 64,
                                           oc * 128:(oc + 1) * 128],
                                    psl[rh * 64:(rh + 1) * 64, 128:256]))
                        st[("proj_free", bank)] = tke
                    st[("m_done", m)] = tke
                    emit_wloads(mg + 1)

            def emit_preps(hp):
                bufs = [None, None]
                for ph in range(2):
                    h = hp * 2 + ph
                    if ph == 1:
                        w0 = [st["avT_free"], st[("m_done", "v")]]
                        for j in range(16):
                            tk_vsrc = P.op("dve", lambda e, j=j: e.tensor_copy(
                                ap_make(avT, j, [[avT.ap[0][0], 64],
                                                 [16, 128]]),
                                vstage[(j % 2) * 64:(j % 2) * 64 + 64,
                                       (j // 2) * 128:(j // 2) * 128 + 128]),
                                waits=w0 if j == 0 else ())
                    else:
                        tk_vsrc = st[("m_done", "v")]
                    for w in range(16):
                        P.op("pe", lambda e, w=w: e.transpose(
                            psT[0:128, w * 64:(w + 1) * 64],
                            avT[0:64, w * 128:(w + 1) * 128], idn[0:64, 0:64]),
                            waits=[tk_vsrc, st.get("psT_free"), tok_idn]
                            if w == 0 else (), inc=(w == 15))
                    tk_tr = {"pe": P.cnt["pe"]}
                    if ph == 0:
                        st["avT_free"] = tk_tr
                    tk_ms = P.op("dve", lambda e, ph=ph: e.memset(
                        ap_make(vaug[ph], 64, [vaug[ph].ap[0], [65, NT],
                                               [1, 1]]), 1.0),
                        waits=[core_pe_ph.get((hp - 1, ph))])
                    tk_av = P.op("dve", lambda e, ph=ph: e.tensor_copy(
                        ap_make(vaug[ph], 0, [vaug[ph].ap[0], [65, NT],
                                              [1, 64]]),
                        ap_make(psT, 0, [psT.ap[0], [64, NT], [1, 64]])),
                        waits=[tk_tr, tk_ms])
                    st["psT_free"] = tk_av
                    P.op("dve", lambda e, ph=ph: e.tensor_copy(
                        vsh[ph], vaug[ph][64:128, :]), waits=[tk_av])
                    tok_ak = P.dma("sp", lambda e, ph=ph: e.dma_start(
                        akd_h, kh[ph][0:64, :]),
                        waits=[st[("m_done", "k")], st.get("akd_free")])
                    tok_avd = P.dma("sp", lambda e, ph=ph: e.dma_start(
                        avd_h, vaug[ph]), waits=[tk_av, st.get("avd_free")])
                    gx = []
                    for t in range(3):
                        gx.append(P.dma("pool", lambda e, t=t, ph=ph:
                                        e.indirect_dma_start(
                            kxb[ph][0:64, t * 64:(t + 1) * 64], None, akd_h,
                            bass.IndirectOffsetOnAxis(kxidx[:, t:t + 1], 1)),
                            waits=[tok_ak] if t == 0 else (),
                            group=("gk", b, h)))
                    for t in range(3):
                        dst = (vxg[ph][0:64, 0:65] if t == 0 else
                               vxg[ph][64:128, 0:65] if t == 1 else
                               vxg[ph][0:64, 65:130])
                        gx.append(P.dma("pool", lambda e, t=t, dst=dst:
                                        e.indirect_dma_start(
                            dst, None, avd_h,
                            bass.IndirectOffsetOnAxis(avidx[:, t:t + 1], 1)),
                            waits=[tok_avd] if t == 0 else (),
                            group=("gk", b, h)))
                    st["akd_free"] = dict(gx[-1])
                    st["avd_free"] = dict(gx[-1])
                    if ph == 0:
                        pms[(hp, 0)] = [
                            P.op("dve", lambda e, c0=c0, n_=n_: e.memset(
                                psC[0:64, c0:c0 + n_], NEG),
                                waits=[exp_tails.get(gpair[0] - 1)])
                            for c0, n_ in ((0, 128), (256, 64))]
                    bufs[ph] = P.tok()
                return bufs

            def emit_pass(hps, pi):
                wo_toks = [wotok[h_] for h_ in hps]
                for t in range(NT):
                    for oh in range(2):
                        j = t * 2 + oh
                        op_ = [psE, psV][j % 2]
                        w0 = [st.get(("pass_free", j % 2))]
                        if j == 0:
                            w0 += ([st["attnc_ready"],
                                    st.get(("proj_free", 0))] + wo_toks)
                        if j == 1:
                            w0 += [st.get(("pvfree", 0))]
                        if pi == 0:
                            P.op("pe", lambda e, oh=oh, op_=op_: e.matmul(
                                op_, onesr[0:1, 0:128],
                                BROW["o"][0:1, oh * 512:(oh + 1) * 512],
                                start=True, stop=False), waits=w0)
                        for ii, h_ in enumerate(hps):
                            first = (ii == 0 and pi != 0)
                            last = (ii == len(hps) - 1)
                            P.op("pe", lambda e, t=t, oh=oh, op_=op_, h_=h_,
                                 first=first, last=last: e.matmul(
                                op_, yT[:, h_ * 2048 + t * 128:
                                        h_ * 2048 + (t + 1) * 128],
                                wo_sb[h_ % 2][:, oh * 512:(oh + 1) * 512],
                                start=first, stop=last),
                                waits=w0 if first else (), inc=last)
                        tk_o = {"pe": P.cnt["pe"]}
                        tk_x = P.op("dve", lambda e, t=t, oh=oh, op_=op_:
                                    e.scalar_tensor_tensor(
                            xtile(t, oh * 512, 512), op_, 1.0,
                            xtile(t, oh * 512, 512), ALU.mult, ALU.add),
                            waits=[tk_o])
                        st[("pass_free", j % 2)] = tk_x
                        if j % 2 == 1:
                            st[("pvfree", 0)] = tk_x
                cp = {"pe": P.cnt["pe"]}
                for h_ in hps:
                    st[("wo_free", h_ % 2)] = cp

            def emit_cores(hp, bufs):
                for ph in range(2):
                    h = hp * 2 + ph
                    if ph == 1:
                        pms[(hp, 1)] = [
                            P.op("dve", lambda e, c0=c0, n_=n_: e.memset(
                                psC[0:64, c0:c0 + n_], NEG),
                                waits=[exp_tails.get(gpair[0] - 1)])
                            for c0, n_ in ((0, 128), (256, 64))]
                    tok_bufs = bufs[ph]
                    qb_ = qh[ph]
                    kb_ = kh[ph]
                    for pr in range(8):
                        gp = gpair[0]
                        gpair[0] += 1
                        pp = gp % 2
                        fullp = [psA, psB][pp]
                        msets = []
                        wfirst = [tok_bufs, exp_done.get(gp - 2), tok_muq[ph]]
                        wtail = [exp_tails.get(gp - 1)]
                        if ph == 0 and pr == 0:
                            wtail.append(st.get(("proj_free", 1)))
                        for ql in range(2):
                            qt = pr * 2 + ql
                            q_rhs = qb_[0:64, qt * 128:(qt + 1) * 128]
                            qx_rhs = qb_[0:67, qt * 128:(qt + 1) * 128]
                            tailp = [psC, psD][ql]
                            P.op("pe", lambda e, kb_=kb_, qt=qt, fullp=fullp,
                                 ql=ql, q_rhs=q_rhs: e.matmul(
                                fullp[:, ql * 256:ql * 256 + 128],
                                kb_[0:64, qt * 128:(qt + 1) * 128], q_rhs,
                                start=True, stop=True),
                                waits=wfirst if ql == 0 else ())
                            P.op("pe", lambda e, fullp=fullp, ql=ql, ph=ph,
                                 qx_rhs=qx_rhs: e.matmul(
                                fullp[:, ql * 256 + 128:ql * 256 + 256],
                                kxb[ph][0:67, 0:128], qx_rhs,
                                start=True, stop=True))
                            if qt != 0:
                                P.op("pe", lambda e, kb_=kb_, tailp=tailp,
                                     q_rhs=q_rhs: e.matmul(
                                    tailp[0:64, 0:128], kb_[0:64, 0:64],
                                    q_rhs, start=True, stop=True), waits=wtail)
                            else:
                                msets += pms[(hp, ph)]
                            if qt != NT - 1:
                                P.op("pe", lambda e, kb_=kb_, tailp=tailp,
                                     q_rhs=q_rhs: e.matmul(
                                    tailp[0:64, 128:256],
                                    kb_[0:64, 1984:2048], q_rhs,
                                    start=True, stop=True), waits=wtail)
                            else:
                                msets.append(P.op("dve", lambda e, tailp=tailp:
                                    e.memset(tailp[0:64, 128:256], NEG),
                                    waits=wtail))
                            if qt != 0:
                                P.op("pe", lambda e, kb_=kb_, qt=qt,
                                     tailp=tailp, qb_=qb_: e.matmul(
                                    tailp[0:64, 256:320],
                                    kb_[0:64, (2 * qt - 1) * 64:(2 * qt) * 64],
                                    qb_[0:64, qt * 128:qt * 128 + 64],
                                    start=True, stop=True), waits=wtail)
                            if qt != NT - 1:
                                P.op("pe", lambda e, kb_=kb_, qt=qt,
                                     tailp=tailp, qb_=qb_: e.matmul(
                                    tailp[0:64, 320:384],
                                    kb_[0:64, (2 * qt + 2) * 64:
                                        (2 * qt + 3) * 64],
                                    qb_[0:64, qt * 128 + 64:qt * 128 + 128],
                                    start=True, stop=True), waits=wtail)
                            else:
                                msets.append(P.op("dve", lambda e, tailp=tailp:
                                    e.memset(tailp[0:64, 320:384], NEG),
                                    waits=wtail))
                            P.op("pe", lambda e, tailp=tailp, ph=ph,
                                 qx_rhs=qx_rhs: e.matmul(
                                tailp[0:64, 384:512],
                                kxb[ph][0:67, 128:192], qx_rhs,
                                start=True, stop=True), waits=wtail,
                                inc=True)
                            if ql == 0:
                                tk_ql0 = {"pe": P.cnt["pe"]}
                                msets0 = list(msets)
                        tk_st_pe = {"pe": P.cnt["pe"]}
                        P.op("act", lambda e: e.activation(
                            etail[0:64, 0:512], psC[0:64, :], AF.Exp),
                            waits=[tk_ql0] + msets0)
                        exp_tails[gp] = P.op("act", lambda e: e.activation(
                            etail[0:64, 512:1024], psD[0:64, :], AF.Exp),
                            waits=[tk_st_pe] + msets)
                        st["tails_read"] = exp_tails[gp]
                        exp_done[gp] = P.op("act", lambda e, fullp=fullp:
                                            e.activation(
                            eful, fullp, AF.Exp))
                        g4g = h * 4 + pr // 2
                        pvb = [psV, psVB][g4g % 2]
                        for ql in range(2):
                            qt = pr * 2 + ql
                            qv = (qt % 4) * 128
                            w0 = [exp_done[gp], st.get(("pvfree", g4g % 2)),
                                  psv_free.get(g4g - 2)]
                            if g4g % 2 == 0:
                                w0.append(st.get(("pass_free", 1)))
                            P.op("pe", lambda e, qt=qt, qv=qv, ql=ql, ph=ph,
                                 pvb=pvb: e.matmul(
                                pvb[0:65, qv:qv + 128],
                                vaug[ph][:, qt * 65:qt * 65 + 65],
                                eful[:, ql * 256:ql * 256 + 128],
                                start=True, stop=False), waits=w0)
                            P.op("pe", lambda e, qv=qv, ql=ql, ph=ph,
                                 pvb=pvb: e.matmul(
                                pvb[0:65, qv:qv + 128],
                                vxg[ph][:, 0:65],
                                eful[:, ql * 256 + 128:ql * 256 + 256],
                                start=False, stop=False))
                            P.op("pe", lambda e, qv=qv, ql=ql, ph=ph,
                                 pvb=pvb: e.matmul(
                                pvb[0:65, qv:qv + 128],
                                vaug[ph][0:64, 0:65],
                                etail[0:64, ql * 512:ql * 512 + 128],
                                start=False, stop=False))
                            P.op("pe", lambda e, qv=qv, ql=ql, ph=ph,
                                 pvb=pvb: e.matmul(
                                pvb[0:65, qv:qv + 128],
                                vsh[ph][0:64, 15 * 65:15 * 65 + 65],
                                etail[0:64, ql * 512 + 128:ql * 512 + 256],
                                start=False, stop=False))
                            if qt != 0:
                                P.op("pe", lambda e, qt=qt, qv=qv, ql=ql,
                                     ph=ph, pvb=pvb: e.matmul(
                                    pvb[0:65, qv:qv + 64],
                                    vsh[ph][0:64, (qt - 1) * 65:
                                            (qt - 1) * 65 + 65],
                                    etail[0:64, ql * 512 + 256:
                                          ql * 512 + 320],
                                    start=False, stop=False))
                            if qt != NT - 1:
                                P.op("pe", lambda e, qt=qt, qv=qv, ql=ql,
                                     ph=ph, pvb=pvb: e.matmul(
                                    pvb[0:65, qv + 64:qv + 128],
                                    vaug[ph][0:64, (qt + 1) * 65:
                                             (qt + 1) * 65 + 65],
                                    etail[0:64, ql * 512 + 320:
                                          ql * 512 + 384],
                                    start=False, stop=False))
                            P.op("pe", lambda e, qv=qv, ql=ql, ph=ph,
                                 pvb=pvb: e.matmul(
                                pvb[0:65, qv:qv + 128],
                                vxg[ph][0:64, 65:130],
                                etail[0:64, ql * 512 + 384:ql * 512 + 512],
                                start=False, stop=True), inc=True)
                        if pr % 2 == 1:
                            g4 = pr // 2
                            tb = (g4g + 1) % 2
                            otherb = [psV, psVB][tb]
                            tk_pv = {"pe": P.cnt["pe"]}
                            P.op("act", lambda e, pvb=pvb: e.activation(
                                lnz, pvb[64:65, 0:512], AF.Ln),
                                waits=[tk_pv, st.get("rz_reader")])

                            def rzfn(e):
                                with nc.allow_low_precision(
                                        reason="1/Z as bf16 matmul operand"):
                                    return e.activation(rz, lnz, AF.Exp,
                                                        scale=-1.0)
                            tk_rz = P.op("act", rzfn)
                            zw = [tk_rz, psv_free.get(g4g - 1),
                                  st.get("rz_reader")]
                            if tb == 0:
                                zw.append(st.get(("pass_free", 1)))
                            tk_zb = P.op("pe", lambda e, otherb=otherb:
                                         e.matmul(
                                otherb[0:64, 0:512], onesr[0:1, 0:64], rz,
                                start=True, stop=True), waits=zw, inc=True)
                            st["rz_reader"] = tk_zb
                            tk_ze = P.op("act", lambda e, otherb=otherb:
                                         e.copy(rzb, otherb[0:64, 0:512]),
                                waits=[tk_zb, st.get("rzb_free")])
                            st[("pvfree", tb)] = tk_ze
                            tk_at = P.op("dve", lambda e, g4=g4, ph=ph,
                                         pvb=pvb, hp=hp:
                                         e.scalar_tensor_tensor(
                                yT[ph * 64:(ph + 1) * 64,
                                   hp * 2048 + g4 * 512:
                                   hp * 2048 + (g4 + 1) * 512],
                                pvb[0:64, 0:512], 1.0, rzb,
                                ALU.mult, ALU.mult), waits=[tk_ze])
                            st["rzb_free"] = tk_at
                            psv_free[g4g] = tk_at
                            st["attnc_ready"] = tk_at
                    core_pe_ph[(hp, ph)] = {"pe": P.cnt["pe"]}

            donep = [0]
            for hp in range(nhp):
                emit_proj(hp)
                bufs = emit_preps(hp)
                if hp - donep[0] >= 2:
                    emit_pass((donep[0], donep[0] + 1), pctr[0])
                    donep[0] += 2
                    pctr[0] += 1
                wotok[hp] = P.dma("sp", lambda e, hp=hp: e.dma_start(
                    wo_sb[hp % 2], wbf_d[3, hp * 128:(hp + 1) * 128, :]),
                    waits=[st.get(("wo_free", hp % 2))])
                emit_cores(hp, bufs)
                if "attnc0" in taps and b == 0 and hp == 0:
                    P.dma("sp", lambda e: e.dma_start(
                        tap_h["attnc0"], yT[:, 0:2048]),
                        waits=[st["attnc_ready"]])
            emit_pass(tuple(range(donep[0], nhp)), pctr[0])
            pctr[0] += 1
            P.fence()

        for b in range(nb):
            xw = st.get("x_stored")
            for t in range(NT):
                P.dma("sp", lambda e, b=b, t=t: e.dma_start(
                    xtile(t), x_h[b, t * 128:(t + 1) * 128, :]),
                    waits=[xw] if t == 0 else (), group=("xl", b))
            P.fence()
            if do_ffn1:
                emit_ffn(b, "ffn1")
            if do_attn:
                emit_attn(b)
            if do_ffn3:
                emit_ffn(b, "ffn3")
            wtok = P.fence()
            for t in range(NT):
                P.dma("sp", lambda e, b=b, t=t: e.dma_start(
                    out_h[b, t * 128:(t + 1) * 128, :], xtile(t)),
                    waits=[wtok] if t == 0 else (), group=("xs", b))
            st["x_stored"] = P.tok()

        P._waits("sp", [P.tok()])
        P.emit(E(nc.Block()), sems)
    return nc


def make_consts():
    idn = np.eye(128, dtype=np.float32).astype(BFDT)
    ones = np.ones((1, 512), dtype=np.float32).astype(BFDT)
    qbvec = (np.arange(S, dtype=np.float32) // 64).reshape(1, S).astype(np.float32)
    kxind = np.zeros((3, 192), dtype=np.float32)
    for t in range(3):
        kxind[t, t * 64:(t + 1) * 64] = 1.0
    oh64 = np.eye(64, dtype=np.float32).astype(BFDT)
    iotak = np.stack([np.arange(64, dtype=np.int64) * S,
                      np.arange(64, dtype=np.int64) * (NT * 65)],
                     axis=1).astype(np.int32)
    return idn, ones, qbvec, kxind.astype(BFDT), oh64, iotak


def make_inputs(core_slice_x, Wq, bq, Wk, bk, Wv, bv, Wo, bo, src, tgt,
                sel, keys, values):
    k8 = np.ascontiguousarray(
        np.asarray(keys, np.float32).transpose(1, 0, 2).reshape(D, -1))
    val = np.ascontiguousarray(np.asarray(values, np.float32))
    selT = np.ascontiguousarray(np.asarray(sel, np.float32).T)
    idn, ones, qbvec, kxind, oh64, iotak = make_consts()
    return {
        "x": np.ascontiguousarray(np.asarray(core_slice_x, np.float32)),
        "k8": k8, "val": val, "selT": selT,
        "wqT": np.ascontiguousarray(np.asarray(Wq, np.float32).T),
        "wkT": np.ascontiguousarray(np.asarray(Wk, np.float32).T),
        "wvT": np.ascontiguousarray(np.asarray(Wv, np.float32).T),
        "woT": np.ascontiguousarray(np.asarray(Wo, np.float32).T),
        "biases": np.concatenate([np.asarray(v, np.float32).ravel()
                                  for v in (bq, bk, bv, bo)]).reshape(1, 4 * D),
        "srctgt": np.concatenate([np.asarray(src, np.int32).ravel(),
                                  np.asarray(tgt, np.int32).ravel()]).reshape(1, 6),
        "idn": idn, "onesrow": ones, "qbvec": qbvec, "kxind": kxind,
        "oh64": oh64, "iotak": iotak,
    }


def kernel(L0_attn_x, L0_attn_Wq, L0_attn_bq, L0_attn_Wk, L0_attn_bk,
           L0_attn_Wv, L0_attn_bv, L0_attn_Wo, L0_attn_bo,
           L0_attn_src_blocks, L0_attn_tgt_blocks,
           L0_ffn_expert_sel, L0_ffn_keys, L0_ffn_values):
    x = np.asarray(L0_attn_x, dtype=np.float32)
    nc = build_program({})
    in_maps = []
    for c in range(NCORES):
        in_maps.append(make_inputs(
            x[2 * c:2 * c + 2], L0_attn_Wq, L0_attn_bq, L0_attn_Wk, L0_attn_bk,
            L0_attn_Wv, L0_attn_bv, L0_attn_Wo, L0_attn_bo,
            L0_attn_src_blocks, L0_attn_tgt_blocks,
            L0_ffn_expert_sel, L0_ffn_keys, L0_ffn_values))
    res = run_bass_kernel_spmd(nc, in_maps, core_ids=list(range(NCORES)))
    out = np.concatenate([r["out"] for r in res.results], axis=0)
    return out.astype(np.float32)
